# version 01
# baseline (speedup 1.0000x reference)
import contextlib
import numpy as np
import ml_dtypes
import concourse.bass as bass
import concourse.mybir as mybir
from concourse.bass_utils import run_bass_kernel_spmd

F32 = mybir.dt.float32
BF16 = mybir.dt.bfloat16
AF = mybir.ActivationFunctionType
ALU = mybir.AluOpType
AX = mybir.AxisListType

N = 2048
D = 1024
NG = 8
NT = 16
NTC = 4
TCW = 512
TWO_PI = 6.283185307179586
PI = 3.141592653589793
HPI = 1.5707963267948966
ENG = ("tensor", "vector", "scalar", "gpsimd", "sync")


class Ctx:
    def __init__(self, nc, block, es):
        self.nc, self.block, self.es = nc, block, es
        self.sem_h = {}
        self.cnt = {}
        self.t_ = {}
        self.es_phase = None

    def sem(self, name):
        if name not in self.sem_h:
            self.sem_h[name] = self.es.enter_context(self.nc.semaphore(name))
            self.cnt[name] = 0
        return self.sem_h[name]

    def inc(self, ins, name, k=1):
        h = self.sem(name)
        self.cnt[name] += k
        ins.then_inc(h, k)
        return (h, self.cnt[name])

    def base(self, name):
        self.sem(name)
        return self.cnt[name]

    def tok(self, name):
        return (self.sem(name), self.cnt[name])


def barrier(ctx, engines=ENG):
    b = ctx.block
    toks = []
    j = ctx.t_["junk"]
    if "vector" in engines:
        @b.vector
        def _(ve):
            toks.append(ctx.inc(ve.tensor_copy(j[0:1, 0:1], j[0:1, 1:2]), "bar_v"))
    if "scalar" in engines:
        @b.scalar
        def _(se):
            toks.append(ctx.inc(se.copy(j[0:1, 2:3], j[0:1, 3:4]), "bar_s"))
    if "gpsimd" in engines:
        @b.gpsimd
        def _(po):
            toks.append(ctx.inc(po.tensor_copy(j[0:1, 4:5], j[0:1, 5:6]), "bar_p"))
    if "tensor" in engines:
        @b.tensor
        def _(te):
            toks.append(ctx.inc(te.nop(), "bar_t"))
    for eng in engines:
        def mk(eng_name):
            def f(e):
                for t in toks:
                    e.wait_ge(*t)
            return f
        getattr(b, eng)(mk(eng))


def stage(ctx, name, items, nbanks, banks, mm_fn, steps, pe_waits=None):
    b = ctx.block
    sp = name + "p"
    bp = ctx.base(sp)
    nst = len(steps)
    bs = [ctx.base(name + f"s{j}") for j in range(nst)]
    free_j = [j for j, st in enumerate(steps) if st[2]][0]

    @b.tensor
    def _(te):
        for i, item in enumerate(items):
            if pe_waits:
                for t in pe_waits(i, item):
                    te.wait_ge(*t)
            if i >= nbanks:
                te.wait_ge(ctx.sem(name + f"s{free_j}"), bs[free_j] + i - nbanks + 1)
            last = mm_fn(te, item, banks[i % nbanks], i)
            ctx.inc(last, sp)

    for j, (eng_name, fn, frees) in enumerate(steps):
        def mk(j, eng_name, fn):
            prev = sp if j == 0 else name + f"s{j-1}"
            prev_base = bp if j == 0 else bs[j - 1]

            def body(eng):
                for i, item in enumerate(items):
                    eng.wait_ge(ctx.sem(prev), prev_base + i + 1)
                    if j < nst - 1 and i >= 2:
                        eng.wait_ge(ctx.sem(name + f"s{nst-1}"), bs[nst - 1] + i - 1)
                    last = fn(eng, item, banks[i % nbanks], i)
                    ctx.inc(last, name + f"s{j}")
            return body
        getattr(b, eng_name)(mk(j, eng_name, fn))


class Pack:
    def __init__(self):
        self.cols = []
        self.idx = {}

    def add(self, name, arr):
        arr = np.asarray(arr, np.float32)
        assert arr.shape[0] == 128, (name, arr.shape)
        w = arr.shape[1]
        pad = (-w) % 16
        if pad:
            arr = np.concatenate([arr, np.zeros((128, pad), np.float32)], axis=1)
        self.idx[name] = sum(c.shape[1] for c in self.cols)
        self.cols.append(arr)
        return self.idx[name]

    def array(self, bf16=False):
        a = (np.concatenate(self.cols, axis=1) if self.cols
             else np.zeros((128, 1), np.float32))
        return a.astype(ml_dtypes.bfloat16) if bf16 else a.astype(np.float32)

    def width(self):
        return max(1, sum(c.shape[1] for c in self.cols))


def col_pp(vec):
    return np.asarray(vec, np.float32).reshape(NG, 128).T


def base_packs():
    cpf, cb = Pack(), Pack()
    cpf.add("ones128", np.ones((128, 128), np.float32))
    cb.add("ones", np.ones((128, 1), np.float32))
    cb.add("idn", np.eye(128, dtype=np.float32))
    cb.add("onesrow", np.ones((128, 128), np.float32))
    return cpf, cb


def bcast8(ap_2d, width):
    return ap_2d.unsqueeze(2).broadcast_to((ap_2d.shape[0], ap_2d.shape[1], width))


def heads_view(ap, blocks, inner0, inner1):
    r = ap.rearrange("p (b c) -> p b c", c=128)
    return r[:, :, inner0:inner1]



LNUP = 9


def emit_ln(ctx, es, tag, banks, gamma_beta=None, eps=1e-5):
    nc, b = ctx.nc, ctx.block
    XF, XB = ctx.t_["XF"], ctx.t_["XB"]
    CB = ctx.t_["CB"]
    bi = ctx.t_["cb_idx"]
    onesb = CB[:, bi["ones"]:bi["ones"] + 1]
    ones_row_b = CB[0:1, bi["onesrow"]:bi["onesrow"] + 128]

    SB = es.enter_context(nc.sbuf_tensor(f"{tag}_sb", [128, 2 * TCW], BF16))
    AB = es.enter_context(nc.sbuf_tensor(f"{tag}_abr", [1, 4 * TCW], F32))
    ABL = es.enter_context(nc.sbuf_tensor(f"{tag}_abl", [1, 8 * TCW], BF16))
    REP = es.enter_context(nc.sbuf_tensor(f"{tag}_rep", [128, 2 * TCW], F32))
    T5 = es.enter_context(nc.sbuf_tensor(f"{tag}_t5", [128, TCW], F32))
    psS = banks[0][0:1, 0:TCW]
    psQ = banks[1][0:1, 0:TCW]
    psR = banks[2]
    psR2 = banks[3]

    b_sq = ctx.base(tag + "sq")
    b_mm = ctx.base(tag + "mm")
    b_rm = ctx.base(tag + "rm")
    b_rs = ctx.base(tag + "rs")
    b_rd = ctx.base(tag + "rd")
    b_bc = ctx.base(tag + "bc")
    b_ap = ctx.base(tag + "ap")
    b_pb = ctx.base(tag + "pb")

    for tc in range(NTC):
        par2 = tc % 2
        A = AB[0:1, par2 * 2 * TCW: par2 * 2 * TCW + TCW]
        B = AB[0:1, par2 * 2 * TCW + TCW:(par2 + 1) * 2 * TCW]
        base_l = par2 * 4 * TCW
        AH = ABL[0:1, base_l + 0 * TCW: base_l + 1 * TCW]
        AL = ABL[0:1, base_l + 1 * TCW: base_l + 2 * TCW]
        BH = ABL[0:1, base_l + 2 * TCW: base_l + 3 * TCW]
        BL = ABL[0:1, base_l + 3 * TCW: base_l + 4 * TCW]

        def mk_se(tc, A):
            def f(se):
                for p8 in range(2):
                    for g in range(NG):
                        j = tc * 16 + p8 * 8 + g
                        par = j % 2
                        if j >= 2:
                            se.wait_ge(ctx.sem(tag + "mm"), b_mm + j - 1)
                        src = XF[:, g * N + tc * TCW: g * N + (tc + 1) * TCW]
                        if p8 == 0:
                            ins = se.copy(SB[:, par * TCW:(par + 1) * TCW], src)
                        else:
                            ins = se.activation(SB[:, par * TCW:(par + 1) * TCW], src, AF.Square)
                        ctx.inc(ins, tag + "sq")
                se.wait_ge(ctx.sem(tag + "rm"), b_rm + tc + 1)
                ctx.inc(se.activation(A, A, AF.Sqrt), tag + "rs")
                se.wait_ge(ctx.sem(tag + "pb"), b_pb + tc + 1)
                if tc >= 1:
                    se.wait_ge(ctx.sem(tag + "ap"), b_ap + tc * NG)
                se.copy(REP[:, 0:TCW], psR[:, 0:TCW])
                ctx.inc(se.copy(REP[:, TCW:2 * TCW], psR2[:, 0:TCW]), tag + "bc")
                for g in range(NG):
                    se.wait_ge(ctx.sem(tag + "ap"), b_ap + tc * NG + g + 1)
                    sl = np.s_[:, g * N + tc * TCW: g * N + (tc + 1) * TCW]
                    ctx.inc(se.copy(XB[sl], XF[sl]), tag + "xb")
            return f
        b.scalar(mk_se(tc, A))

        def mk_te(tc):
            def f(te):
                for p8 in range(2):
                    dst = psQ if p8 else psS
                    for g in range(NG):
                        j = tc * 16 + p8 * 8 + g
                        par = j % 2
                        te.wait_ge(ctx.sem(tag + "sq"), b_sq + j + 1)
                        if p8 == 0 and g == 0 and tc >= 1:
                            te.wait_ge(ctx.sem(tag + "rd"), b_rd + tc)
                        ctx.inc(te.matmul(dst, onesb, SB[:, par * TCW:(par + 1) * TCW],
                                          start=(g == 0), stop=(g == NG - 1)), tag + "mm")
                base_l = (tc % 2) * 4 * TCW
                te.wait_ge(ctx.sem(tag + "rd"), b_rd + tc + 1)
                if tc >= 1:
                    te.wait_ge(ctx.sem(tag + "bc"), b_bc + tc)
                te.matmul(psR[:, 0:TCW], ones_row_b,
                          ABL[0:1, base_l:base_l + TCW], start=True, stop=False)
                te.matmul(psR[:, 0:TCW], ones_row_b,
                          ABL[0:1, base_l + TCW: base_l + 2 * TCW], start=False, stop=True)
                te.matmul(psR2[:, 0:TCW], ones_row_b,
                          ABL[0:1, base_l + 2 * TCW: base_l + 3 * TCW], start=True, stop=False)
                ctx.inc(te.matmul(psR2[:, 0:TCW], ones_row_b,
                                  ABL[0:1, base_l + 3 * TCW: base_l + 4 * TCW],
                                  start=False, stop=True), tag + "pb")
            return f
        b.tensor(mk_te(tc))

        def mk_ve(tc, A, B, AH, AL, BH, BL):
            def f(ve):
                ve.wait_ge(ctx.sem(tag + "mm"), b_mm + tc * 16 + 16)
                if tc >= 2:
                    ve.wait_ge(ctx.sem(tag + "pb"), b_pb + tc - 1)
                ve.tensor_scalar_mul(B, psS, 1.0 / D)
                ve.tensor_scalar_mul(A, psQ, 1.0 / D)
                ve.tensor_mul(psS, B, B)
                ve.tensor_sub(A, A, psS)
                ctx.inc(ve.tensor_scalar_add(A, A, eps), tag + "rm")
                ve.wait_ge(ctx.sem(tag + "rs"), b_rs + tc + 1)
                ctx.inc(ve.reciprocal(A, A), tag + "rr")
                ve.wait_ge(*ctx.tok(tag + "rr"))
                ve.scalar_tensor_tensor(B, B, -1.0, A, ALU.mult, ALU.mult)
                ve.tensor_copy(AH, A)
                ve.tensor_sub(AL, A, AH)
                ve.tensor_copy(BH, B)
                ctx.inc(ve.tensor_sub(BL, B, BH), tag + "rd")
                ve.wait_ge(ctx.sem(tag + "bc"), b_bc + tc + 1)
                for g in range(NG):
                    sl = np.s_[:, g * N + tc * TCW: g * N + (tc + 1) * TCW]
                    ve.tensor_mul(T5[:, 0:TCW], XF[sl], REP[:, 0:TCW])
                    ins = ve.tensor_add(XF[sl], T5[:, 0:TCW], REP[:, TCW:2 * TCW])
                    if gamma_beta is not None:
                        gc, bc2 = gamma_beta
                        ins = ve.tensor_scalar(XF[sl], XF[sl], gc(g), bc2(g), ALU.mult, ALU.add)
                    ctx.inc(ins, tag + "ap")
            return f
        b.vector(mk_ve(tc, A, B, AH, AL, BH, BL))



def host_prep_c(inp):
    return {
        "c_wq": np.asarray(inp["L1_attn_Wq"], np.float32),
        "c_wd": np.asarray(inp["L1_attn_Wd"], np.float32),
        "c_wu": np.asarray(inp["L1_attn_Wu"], np.float32),
        "c_wo": np.asarray(inp["L1_attn_Wo"], np.float32),
    }


def decl_c(nc):
    return {
        "c_wq": nc.dram_tensor("c_wq", [D, D], F32, kind="ExternalInput").ap(),
        "c_wd": nc.dram_tensor("c_wd", [D, 256], F32, kind="ExternalInput").ap(),
        "c_wu": nc.dram_tensor("c_wu", [256, N], F32, kind="ExternalInput").ap(),
        "c_wo": nc.dram_tensor("c_wo", [D, D], F32, kind="ExternalInput").ap(),
    }


def pack_c(cpf, cb, inp):
    cpf.add("c_bq", col_pp(inp["L1_attn_bq"]))
    cpf.add("c_bd", np.asarray(inp["L1_attn_bd"], np.float32).reshape(2, 128).T)
    cpf.add("c_bo", col_pp(inp["L1_attn_bo"]))
    bu = np.zeros((128, N), np.float32)
    bu[0, :] = np.asarray(inp["L1_attn_bu"], np.float32)
    cb.add("c_bu", bu)


def emit_tap(ctx, src_ap, cols):
    b = ctx.block
    dbg = ctx.t_["dbg"]
    barrier(ctx)

    @b.gpsimd
    def _(po):
        CH = 2048
        for c0 in range(0, cols, CH):
            w = min(CH, cols - c0)
            ctx.inc(po.dma_start(out=dbg[:, c0:c0 + w], in_=src_ap[:, c0:c0 + w]), "dmt", 16)

    @b.sync
    def _(sy):
        sy.wait_ge(*ctx.tok("dmt"))
    barrier(ctx)


def emit_phase_c(ctx, aps, upto=99):
    nc, b = ctx.nc, ctx.block
    es = ctx.es_phase
    XF, XB = ctx.t_["XF"], ctx.t_["XB"]
    CPF, CB = ctx.t_["CPF"], ctx.t_["CB"]
    ci, bi = ctx.t_["cpf_idx"], ctx.t_["cb_idx"]

    QT = es.enter_context(nc.sbuf_tensor("c_QT", [128, NG * N], BF16))
    T1T = es.enter_context(nc.sbuf_tensor("c_T1T", [128, 2 * N], BF16))
    KVS = es.enter_context(nc.sbuf_tensor("c_KVS", [128, 3 * N], BF16))
    CTXB = es.enter_context(nc.sbuf_tensor("c_CTXB", [128, D], BF16))
    KSB = es.enter_context(nc.sbuf_tensor("c_KSB", [128, 16], BF16))
    ZT = es.enter_context(nc.sbuf_tensor("c_ZT", [128, NT * 16], F32))
    CS = es.enter_context(nc.sbuf_tensor("c_CS", [128, 6 * TCW], BF16))
    WO = es.enter_context(nc.sbuf_tensor("c_WO", [128, NG * D], BF16))
    es_w = contextlib.ExitStack()
    es.callback(es_w.close)
    WQ = es_w.enter_context(nc.sbuf_tensor("c_WQ", [128, NG * D], BF16))
    WD = es_w.enter_context(nc.sbuf_tensor("c_WD", [128, NG * 256], BF16))
    WU = es_w.enter_context(nc.sbuf_tensor("c_WU", [128, 2 * N], BF16))
    ps4 = [es.enter_context(nc.psum_tensor(f"c_ps{i}", [128, TCW], F32)) for i in range(4)]
    ps = ps4[:3]
    psEO = es.enter_context(nc.psum_tensor("c_psEO", [128, 2 * TCW], F32))
    psE = psEO[:, 0:TCW]
    psO = psEO[:, TCW:2 * TCW]
    psKb = ps4[3]
    psK = psKb[:, 0:16]
    psB = [es.enter_context(nc.psum_tensor(f"c_psb{i}", [128, 1024], BF16)) for i in range(2)]

    @b.gpsimd
    def _(po):
        for g in range(NG):
            ctx.inc(po.dma_start(out=WQ[:, g * D:(g + 1) * D],
                                 in_=aps["c_wq"][g * 128:(g + 1) * 128, :]), "dmw", 16)
        for g in range(NG):
            ctx.inc(po.dma_start(out=WD[:, g * 256:(g + 1) * 256],
                                 in_=aps["c_wd"][g * 128:(g + 1) * 128, :]), "dmw", 16)
        for g in range(2):
            ctx.inc(po.dma_start(out=WU[:, g * N:(g + 1) * N],
                                 in_=aps["c_wu"][g * 128:(g + 1) * 128, :]), "dmw", 16)
        for g in range(NG):
            ctx.inc(po.dma_start(out=WO[:, g * D:(g + 1) * D],
                                 in_=aps["c_wo"][g * 128:(g + 1) * 128, :]), "dmw", 16)
        po.memset(KSB[:, :], 0.0)
    wtok = ctx.tok("dmw")

    @b.vector
    def _(ve):
        ve.memset(psEO[:, 0:2 * TCW], 0.0)
        ve.memset(psKb[:, 0:16], 0.0)
    barrier(ctx)

    @b.tensor
    def _(te):
        te.wait_ge(*wtok)

    if upto < 1:
        return
    bqc = ci["c_bq"]

    def mm_q(te, item, p, i):
        dog, tc = item
        for kg in range(NG):
            ins = te.matmul(p[:, 0:TCW], WQ[:, kg * D + dog * 128: kg * D + (dog + 1) * 128],
                            XB[:, kg * N + tc * TCW: kg * N + (tc + 1) * TCW],
                            start=(kg == 0), stop=(kg == NG - 1))
        return ins

    def ev_q1(ve, item, p, i):
        dog, tc = item
        par = i % 2
        bq_ap = CPF[:, bqc + dog:bqc + dog + 1]
        ve.tensor_scalar(CS[:, par * TCW:(par + 1) * TCW], p[:, 0:TCW], bq_ap, 0.0,
                         ALU.add, ALU.min)
        return ve.tensor_scalar(CS[:, (2 + par) * TCW:(3 + par) * TCW], p[:, 0:TCW], bq_ap, 0.0,
                                ALU.add, ALU.max)

    def ev_q2(se, item, p, i):
        par = i % 2
        return se.activation(CS[:, (4 + par) * TCW:(5 + par) * TCW],
                             CS[:, par * TCW:(par + 1) * TCW], AF.Exp)

    def ev_q3(po, item, p, i):
        dog, tc = item
        par = i % 2
        return po.tensor_add(QT[:, dog * N + tc * TCW: dog * N + (tc + 1) * TCW],
                             CS[:, (2 + par) * TCW:(3 + par) * TCW],
                             CS[:, (4 + par) * TCW:(5 + par) * TCW])

    stage(ctx, "cq", [(dog, tc) for dog in range(NG) for tc in range(NTC)], 3, ps,
          mm_q, [("vector", ev_q1, True), ("scalar", ev_q2, False), ("gpsimd", ev_q3, False)])
    barrier(ctx)

    if upto < 2:
        return
    bdc = ci["c_bd"]

    def mm_t1(te, item, p, i):
        dog, tc = item
        for kg in range(NG):
            ins = te.matmul(p[:, 0:TCW], WD[:, kg * 256 + dog * 128: kg * 256 + (dog + 1) * 128],
                            XB[:, kg * N + tc * TCW: kg * N + (tc + 1) * TCW],
                            start=(kg == 0), stop=(kg == NG - 1))
        return ins

    def ev_t1(ve, item, p, i):
        dog, tc = item
        return ve.tensor_scalar_add(T1T[:, dog * N + tc * TCW: dog * N + (tc + 1) * TCW],
                                    p[:, 0:TCW], CPF[:, bdc + dog:bdc + dog + 1])

    stage(ctx, "ct1", [(dog, tc) for dog in range(2) for tc in range(NTC)], 3, ps,
          mm_t1, [("vector", ev_t1, True)])
    barrier(ctx)
    if TAP == "qt":
        emit_tap(ctx, QT[:, :], NG * N)
    if TAP == "t1t":
        emit_tap(ctx, T1T[:, :], 2 * N)

    if upto < 3:
        return
    ones_row_b = CB[0:1, bi["ones"]:bi["ones"] + 1].broadcast_to((1, 128))
    bu_row = CB[0:1, bi["c_bu"]: bi["c_bu"] + N]
    onesb = CB[:, bi["ones"]:bi["ones"] + 1]
    b_kvp = ctx.base("kvp")
    b_kv1 = ctx.base("kv1")
    b_kv2 = ctx.base("kv2")
    b_kv3 = ctx.base("kv3")
    b_cx = ctx.base("cxm")

    @b.tensor
    def _(te):
        for t in range(NT):
            slot = t % 3
            for cc in range(4):
                i = t * 4 + cc
                if i >= 3:
                    te.wait_ge(ctx.sem("kv1"), b_kv1 + i - 2)
                p = ps[i % 3]
                for kg in range(2):
                    te.matmul(p[:, 0:TCW], T1T[:, kg * N + t * 128: kg * N + (t + 1) * 128],
                              WU[:, kg * N + cc * TCW: kg * N + (cc + 1) * TCW],
                              start=(kg == 0), stop=False)
                ctx.inc(te.matmul(p[:, 0:TCW], ones_row_b,
                                  bu_row[0:1, cc * TCW:(cc + 1) * TCW],
                                  start=False, stop=True), "kvp")
            te.wait_ge(ctx.sem("kv3"), b_kv3 + (t + 1) * 4)
            te.wait_ge(ctx.sem("kv1"), b_kv1 + (t + 1) * 4)
            kv = KVS[:, slot * N:(slot + 1) * N]
            ins = None
            for h in range(16):
                kh = kv[:, h * 128: h * 128 + 64]
                vh = kv[:, h * 128 + 64:(h + 1) * 128]
                dst = psE if h % 2 == 0 else psO
                te.matmul(dst[0:64, (h // 2) * 64:(h // 2) * 64 + 64], kh, vh,
                          start=False, stop=True, skip_group_check=True)
                ins = te.matmul(psK[0:64, h:h + 1], kh, onesb,
                                start=False, stop=True, skip_group_check=True)
            ctx.inc(ins, "cxm")

    @b.vector
    def _(ve):
        for t in range(NT):
            slot = t % 3
            kv = KVS[:, slot * N:(slot + 1) * N]
            for cc in range(4):
                i = t * 4 + cc
                ve.wait_ge(ctx.sem("kvp"), b_kvp + i + 1)
                if t >= 3 and cc == 0:
                    ve.wait_ge(ctx.sem("cxm"), b_cx + t - 2)
                if i >= 2:
                    ve.wait_ge(ctx.sem("kv3"), b_kv3 + i - 1)
                p = ps[i % 3]
                par = i % 2
                pv = heads_view(p[:, 0:TCW], 4, 64, 128)
                kvv = heads_view(kv[:, cc * TCW:(cc + 1) * TCW], 4, 64, 128)
                pk = heads_view(p[:, 0:TCW], 4, 0, 64)
                ve.tensor_copy(kvv, pv)
                ve.tensor_single_scalar(CS[:, par * 256:par * 256 + 256], pk, 0.0, ALU.min)
                ins = ve.tensor_single_scalar(CS[:, 512 + par * 256:512 + par * 256 + 256],
                                              pk, 0.0, ALU.max)
                ctx.inc(ins, "kv1")

    @b.scalar
    def _(se):
        for i in range(NT * 4):
            se.wait_ge(ctx.sem("kv1"), b_kv1 + i + 1)
            if i >= 2:
                se.wait_ge(ctx.sem("kv3"), b_kv3 + i - 1)
            par = i % 2
            ctx.inc(se.activation(CS[:, 1024 + par * 256:1024 + par * 256 + 256],
                                  CS[:, par * 256:par * 256 + 256], AF.Exp), "kv2")

    @b.gpsimd
    def _(po):
        for t in range(NT):
            slot = t % 3
            kv = KVS[:, slot * N:(slot + 1) * N]
            for cc in range(4):
                i = t * 4 + cc
                po.wait_ge(ctx.sem("kv2"), b_kv2 + i + 1)
                par = i % 2
                kvk = heads_view(kv[:, cc * TCW:(cc + 1) * TCW], 4, 0, 64)
                ctx.inc(po.tensor_add(kvk, CS[:, 512 + par * 256:512 + par * 256 + 256],
                                      CS[:, 1024 + par * 256:1024 + par * 256 + 256]), "kv3")

    cx_done = ctx.tok("cxm")

    @b.scalar
    def _(se):
        se.wait_ge(*cx_done)
        for h in range(16):
            src = psE if h % 2 == 0 else psO
            r0 = (h % 2) * 64
            se.copy(CTXB[r0:r0 + 64, h * 64:(h + 1) * 64],
                    src[0:64, (h // 2) * 64:(h // 2) * 64 + 64])
            ctx.inc(se.copy(KSB[r0:r0 + 64, h:h + 1], psK[0:64, h:h + 1]), "cxe")
    barrier(ctx)

    if upto < 5:
        return
    if TAP == "ctx":
        emit_tap(ctx, CTXB[:, :], D)
    if TAP == "ksb":
        emit_tap(ctx, KSB[:, :], 16)
    es_w.close()

    b_zp = ctx.base("zp")
    b_zf = ctx.base("zf")

    @b.tensor
    def _(te):
        for t in range(NT):
            if t >= 1:
                te.wait_ge(ctx.sem("zf"), b_zf + t)
            ins = None
            for g in range(NG):
                ins = te.matmul(ps[0][:, 2 * g:2 * g + 2], QT[:, g * N + t * 128: g * N + (t + 1) * 128],
                                KSB[:, 2 * g:2 * g + 2], start=(g == 0), stop=True,
                                skip_group_check=True)
            ctx.inc(ins, "zp")

    @b.vector
    def _(ve):
        for t in range(NT):
            ve.wait_ge(ctx.sem("zp"), b_zp + t + 1)
            ins = ve.tensor_scalar_add(ZT[:, t * 16:(t + 1) * 16], ps[0][:, 0:16], 1e-6)
            if TAP != "zraw":
                ctx.inc(ins, "zadd")
                ve.wait_ge(*ctx.tok("zadd"))
                ins = ve.reciprocal(ZT[:, t * 16:(t + 1) * 16], ZT[:, t * 16:(t + 1) * 16])
            ctx.inc(ins, "zf")
    barrier(ctx)
    if TAP in ("zt", "zraw"):
        emit_tap(ctx, ZT[:, :], NT * 16)

    if upto < 6:
        return
    def mm_out(te, item, p, i):
        t, par = item
        r0 = par * 64
        ins = None
        for j in range(8):
            h = 2 * j + par
            ins = te.matmul(p[:, j * 64:(j + 1) * 64],
                            QT[r0:r0 + 64, (h // 2) * N + t * 128:(h // 2) * N + (t + 1) * 128],
                            CTXB[r0:r0 + 64, h * 64:(h + 1) * 64], start=(j == 0), stop=True,
                            skip_group_check=True)
        return ins

    def ev_out(ve, item, p, i):
        t, par = item
        zv = bcast8(ZT[:, t * 16 + par: t * 16 + 16: 2], 64)
        pv = p[:, 0:TCW].rearrange("p (h c) -> p h c", c=64)
        dst = XB[:, t * 1024:(t + 1) * 1024].rearrange("p (h c) -> p h c", c=64)[:, par::2, :]
        return ve.tensor_tensor(dst, pv, zv, ALU.mult)

    stage(ctx, "co", [(t, par) for t in range(NT) for par in range(2)], 3, ps,
          mm_out, [("vector", ev_out, True)])
    barrier(ctx)
    if TAP == "otok":
        emit_tap(ctx, XB[:, :], NG * N)

    if upto < 7:
        return
    idn = CB[:, bi["idn"]:bi["idn"] + 128]
    b_tp = ctx.base("tp")
    b_tf = ctx.base("tf")

    @b.tensor
    def _(te):
        for i in range(128):
            t, og = divmod(i, NG)
            if i >= 2:
                te.wait_ge(ctx.sem("tf"), b_tf + i - 1)
            ctx.inc(te.transpose(psB[i % 2][:, 0:128],
                                 XB[:, t * 1024 + og * 128: t * 1024 + (og + 1) * 128], idn), "tp")

    @b.scalar
    def _(se):
        for i in range(128):
            t, og = divmod(i, NG)
            se.wait_ge(ctx.sem("tp"), b_tp + i + 1)
            ins = se.copy(QT[:, og * N + t * 128: og * N + (t + 1) * 128], psB[i % 2][:, 0:128])
            ctx.inc(ins, "tf")
    barrier(ctx)
    if TAP == "outt":
        emit_tap(ctx, QT[:, :], NG * N)

    if upto < 8:
        return
    boc = ci["c_bo"]

    def mm_wo(te, item, p, i):
        dog, tc = item
        for kg in range(NG):
            ins = te.matmul(p[:, 0:TCW], WO[:, kg * D + dog * 128: kg * D + (dog + 1) * 128],
                            QT[:, kg * N + tc * TCW: kg * N + (tc + 1) * TCW],
                            start=(kg == 0), stop=(kg == NG - 1))
        return ins

    def ev_wo(ve, item, p, i):
        dog, tc = item
        sl = np.s_[:, dog * N + tc * TCW: dog * N + (tc + 1) * TCW]
        return ve.scalar_tensor_tensor(XF[sl], p[:, 0:TCW], CPF[:, boc + dog:boc + dog + 1],
                                       XF[sl], ALU.add, ALU.add)

    stage(ctx, "cw", [(dog, tc) for dog in range(NG) for tc in range(NTC)], 3, ps,
          mm_wo, [("vector", ev_wo, True)])
    barrier(ctx)

    if upto < 9:
        return
    emit_ln(ctx, es, "lnc", ps4)
    barrier(ctx)



def host_prep_a(inp):
    return {
        "a_wq": np.asarray(inp["L0_attn_Wq"], np.float32),
        "a_wk": np.asarray(inp["L0_attn_Wk"], np.float32),
        "a_wv": np.asarray(inp["L0_attn_Wv"], np.float32),
        "a_wo": np.asarray(inp["L0_attn_Wo"], np.float32),
        "a_g1": np.asarray(inp["L0_attn_G1"], np.float32),
        "a_g2": np.asarray(inp["L0_attn_G2"], np.float32),
    }


def decl_a(nc):
    return {
        "a_wq": nc.dram_tensor("a_wq", [D, D], F32, kind="ExternalInput").ap(),
        "a_wk": nc.dram_tensor("a_wk", [D, D], F32, kind="ExternalInput").ap(),
        "a_wv": nc.dram_tensor("a_wv", [D, D], F32, kind="ExternalInput").ap(),
        "a_wo": nc.dram_tensor("a_wo", [D, D], F32, kind="ExternalInput").ap(),
        "a_g1": nc.dram_tensor("a_g1", [D, 64], F32, kind="ExternalInput").ap(),
        "a_g2": nc.dram_tensor("a_g2", [64, D], F32, kind="ExternalInput").ap(),
    }


def pack_a(cpf, cb, inp):
    th = np.asarray(inp["L0_attn_theta"], np.float32).reshape(D)
    cpf.add("a_th", th.reshape(NG, 128).T)
    cpf.add("a_gam", col_pp(inp["L0_attn_gamma"]))
    gr = np.zeros((128, D), np.float32)
    gr[0, :] = np.asarray(inp["L0_attn_gamma"], np.float32)
    cb.add("a_gamr", gr)
    br = np.zeros((128, D), np.float32)
    br[0, :] = np.asarray(inp["L0_attn_beta"], np.float32)
    cb.add("a_betr", br)


def emit_phase_a(ctx, aps, upto=99):
    nc, b = ctx.nc, ctx.block
    es = ctx.es_phase
    XF, XB = ctx.t_["XF"], ctx.t_["XB"]
    CPF, CB = ctx.t_["CPF"], ctx.t_["CB"]
    ci, bi = ctx.t_["cpf_idx"], ctx.t_["cb_idx"]
    idn = CB[:, bi["idn"]:bi["idn"] + 128]
    TH2 = 6.283185307179586

    OUT = es.enter_context(nc.sbuf_tensor("a_OUT", [128, NG * N], BF16))
    ctx.t_["a_OUT"] = OUT
    es_a = contextlib.ExitStack()
    es.callback(es_a.close)
    WQB = [es_a.enter_context(nc.sbuf_tensor(f"a_WQB{i}", [128, 3 * 8 * 128], BF16))
           for i in range(2)]
    QG = es_a.enter_context(nc.sbuf_tensor("a_QG", [128, N], BF16))
    KG = es_a.enter_context(nc.sbuf_tensor("a_KG", [128, N], BF16))
    VG = es_a.enter_context(nc.sbuf_tensor("a_VG", [128, N], BF16))
    ANG = es_a.enter_context(nc.sbuf_tensor("a_ANG", [128, 1024], F32))
    ANGU = es_a.enter_context(nc.sbuf_tensor("a_ANGU", [128, 1024], F32))
    ANGI = es_a.enter_context(nc.sbuf_tensor("a_ANGI", [128, 1024], mybir.dt.int32))
    SINT = es_a.enter_context(nc.sbuf_tensor("a_SINT", [128, N], BF16))
    COST = es_a.enter_context(nc.sbuf_tensor("a_COST", [128, N], BF16))
    QCOS = es_a.enter_context(nc.sbuf_tensor("a_QCOS", [128, N], BF16))
    QSIN = es_a.enter_context(nc.sbuf_tensor("a_QSIN", [128, N], BF16))
    KCOS = es_a.enter_context(nc.sbuf_tensor("a_KCOS", [128, N], BF16))
    KSIN = es_a.enter_context(nc.sbuf_tensor("a_KSIN", [128, N], BF16))
    KVT = es_a.enter_context(nc.sbuf_tensor("a_KVT", [128, 2 * 384], BF16))
    ES2 = es_a.enter_context(nc.sbuf_tensor("a_ES2", [128, 2 * 256], BF16))
    ESR = es_a.enter_context(nc.sbuf_tensor("a_ESR", [128, 2 * 256], BF16))
    SSB = es_a.enter_context(nc.sbuf_tensor("a_SSB", [128, 128], BF16))
    TBH = es_a.enter_context(nc.sbuf_tensor("a_TBH", [64, 2 * 128], F32))
    TB2 = es_a.enter_context(nc.sbuf_tensor("a_TB2", [64, 2 * 128], BF16))
    psE0 = es_a.enter_context(nc.psum_tensor("a_psE0", [128, TCW], F32))
    psE1 = es_a.enter_context(nc.psum_tensor("a_psE1", [128, TCW], F32))
    psT0 = es_a.enter_context(nc.psum_tensor("a_psT0", [128, 1024], BF16))
    psT1 = es_a.enter_context(nc.psum_tensor("a_psT1", [128, 1024], BF16))
    psST = es_a.enter_context(nc.psum_tensor("a_psST", [128, TCW], F32))
    psOA = es_a.enter_context(nc.psum_tensor("a_psOA", [128, TCW], F32))
    psOB = es_a.enter_context(nc.psum_tensor("a_psOB", [128, TCW], F32))
    psOC = es_a.enter_context(nc.psum_tensor("a_psOC", [128, TCW], F32))

    thc = ci["a_th"]
    wtoks = {}

    def load_ablk(po, gh, slot):
        for m, ap in enumerate([aps["a_wq"], aps["a_wk"], aps["a_wv"]]):
            for kg in range(NG):
                ctx.inc(po.dma_start(
                    out=WQB[slot][:, m * 1024 + kg * 128: m * 1024 + (kg + 1) * 128],
                    in_=ap[kg * 128:(kg + 1) * 128, gh * 128:(gh + 1) * 128]), "dmw", 16)
        return ctx.tok("dmw")

    @b.gpsimd
    def _(po):
        wtoks[0] = load_ablk(po, 0, 0)
        wtoks[1] = load_ablk(po, 1, 1)

    A = {k: ctx.base(k) for k in
         ["aio", "apj", "apv", "atg", "asn", "atr", "atp", "akv",
          "ae", "aes", "aem", "aop", "ass", "ath", "aoe", "ao2"]}

    for gh in range(NG):
        def mk_po1(gh):
            def f(po):
                if gh >= 2:
                    po.wait_ge(ctx.sem("apv"), A["apv"] + (gh - 1) * 12)
                    wtoks[gh] = load_ablk(po, gh, gh % 2)
            return f
        b.gpsimd(mk_po1(gh))

        def mk_pe1(gh):
            def f(te):
                te.wait_ge(*wtoks[gh])
                if gh > 0:
                    te.wait_ge(ctx.sem("aes"), A["aes"] + gh * 32)
                    te.wait_ge(ctx.sem("ass"), A["ass"] + gh * 16)
                rot = [psE0, psE1, psST]
                for r in range(12):
                    m, tc = divmod(r, 4)
                    j = gh * 12 + r
                    if r >= 3:
                        te.wait_ge(ctx.sem("apv"), A["apv"] + j - 2)
                    for kg in range(NG):
                        ins = te.matmul(rot[r % 3][:, 0:TCW],
                                        WQB[gh % 2][:, m * 1024 + kg * 128: m * 1024 + (kg + 1) * 128],
                                        XB[:, kg * N + tc * TCW: kg * N + (tc + 1) * TCW],
                                        start=(kg == 0), stop=(kg == NG - 1))
                    ctx.inc(ins, "apj")
            return f
        b.tensor(mk_pe1(gh))

        def mk_se1(gh):
            def f(se):
                rot = [psE0, psE1, psST]
                for r in range(12):
                    m, tc = divmod(r, 4)
                    j = gh * 12 + r
                    se.wait_ge(ctx.sem("apj"), A["apj"] + j + 1)
                    dstt = [QG, KG, VG][m]
                    if m < 2:
                        ins = se.activation(dstt[:, tc * TCW:(tc + 1) * TCW],
                                            rot[r % 3][:, 0:TCW], AF.Silu)
                    else:
                        ins = se.copy(dstt[:, tc * TCW:(tc + 1) * TCW], rot[r % 3][:, 0:TCW])
                    ctx.inc(ins, "apv")
            return f
        b.scalar(mk_se1(gh))

        def mk_ve1(gh, hf2):
            def f(ve):
                ve.wait_ge(ctx.sem("aio"), A["aio"] + gh * 2 + hf2 + 1)
                if gh >= 1 or hf2 >= 1:
                    ve.wait_ge(ctx.sem("asn"), A["asn"] + (gh * 2 + hf2 - 1) * 2 + 2)
                th_col = CPF[:, thc + gh: thc + gh + 1]
                R = ANG[:, 0:1024]
                U = ANGU[:, 0:1024]
                ve.tensor_scalar_mul(R, R, th_col)
                ve.tensor_scalar_mul(U, R, 1.0 / TH2)
                ve.tensor_copy(ANGI[:, 0:1024], U)
                ve.tensor_copy(U, ANGI[:, 0:1024])
                ve.scalar_tensor_tensor(R, U, -TH2, R, ALU.mult, ALU.add)
                ve.tensor_single_scalar(U, R, PI, ALU.is_gt)
                ctx.inc(ve.scalar_tensor_tensor(R, U, -TH2, R, ALU.mult, ALU.add), "atg")
            return f

        def mk_se2(gh, hf2):
            def f(se):
                se.wait_ge(ctx.sem("atg"), A["atg"] + (gh * 2 + hf2) * 2 + 1)
                ctx.inc(se.activation(SINT[:, hf2 * 1024:(hf2 + 1) * 1024],
                                      ANG[:, 0:1024], AF.Sin), "asn")
            return f

        def mk_ve2(gh, hf2):
            def f(ve):
                ve.wait_ge(ctx.sem("asn"), A["asn"] + (gh * 2 + hf2) * 2 + 1)
                R = ANG[:, 0:1024]
                U = ANGU[:, 0:1024]
                ve.tensor_scalar_add(R, R, HPI)
                ve.tensor_single_scalar(U, R, PI, ALU.is_gt)
                ctx.inc(ve.scalar_tensor_tensor(R, U, -TH2, R, ALU.mult, ALU.add), "atg")
            return f

        def mk_se3(gh, hf2):
            def f(se):
                se.wait_ge(ctx.sem("atg"), A["atg"] + (gh * 2 + hf2) * 2 + 2)
                ctx.inc(se.activation(COST[:, hf2 * 1024:(hf2 + 1) * 1024],
                                      ANG[:, 0:1024], AF.Sin), "asn")
            return f

        def mk_po1b(gh, hf2):
            def f(po):
                g2h = gh * 2 + hf2
                if g2h >= 1:
                    po.wait_ge(ctx.sem("asn"), A["asn"] + g2h * 2)
                ctx.inc(po.iota(ANG[:, 0:1024], [[1, 1024]], base=hf2 * 1024,
                                channel_multiplier=0,
                                allow_small_or_imprecise_dtypes=True), "aio")
            return f

        for hf2 in range(2):
            b.gpsimd(mk_po1b(gh, hf2))
            b.vector(mk_ve1(gh, hf2))
            b.scalar(mk_se2(gh, hf2))
            b.vector(mk_ve2(gh, hf2))
            b.scalar(mk_se3(gh, hf2))

        def mk_ve3(gh):
            def f(ve):
                ve.wait_ge(ctx.sem("asn"), A["asn"] + gh * 4 + 4)
                ve.wait_ge(ctx.sem("apv"), A["apv"] + gh * 12 + 12)
                if gh >= 1:
                    ve.wait_ge(ctx.sem("aop"), A["aop"] + gh * 16)
                    ve.wait_ge(ctx.sem("ae"), A["ae"] + gh * 32)
                ve.tensor_mul(QCOS[:, :], QG[:, :], COST[:, :])
                ve.tensor_mul(QSIN[:, :], QG[:, :], SINT[:, :])
                ve.tensor_mul(KCOS[:, :], KG[:, :], COST[:, :])
                ve.tensor_mul(KSIN[:, :], KG[:, :], SINT[:, :])
                ctx.inc(ve.memset(psST[:, 0:TCW], 0.0), "atr")
            return f
        b.vector(mk_ve3(gh))

        def mk_pe2(gh):
            def f(te):
                te.wait_ge(ctx.sem("atr"), A["atr"] + gh + 1)
                for c in range(NT):
                    cs = np.s_[c * 128:(c + 1) * 128]
                    par = c % 2
                    if c >= 1:
                        te.wait_ge(ctx.sem("akv"), A["akv"] + gh * 16 + c)
                    te.transpose(psT0[:, 0:64], KCOS[0:64, cs], idn[0:64, 0:64])
                    te.transpose(psT0[:, 64:128], KSIN[0:64, cs], idn[0:64, 0:64])
                    te.transpose(psT0[:, 128:192], VG[0:64, cs], idn[0:64, 0:64])
                    te.transpose(psT1[:, 0:64], KCOS[64:128, cs], idn[64:128, 64:128])
                    te.transpose(psT1[:, 64:128], KSIN[64:128, cs], idn[64:128, 64:128])
                    ins = te.transpose(psT1[:, 128:192], VG[64:128, cs], idn[64:128, 64:128])
                    ctx.inc(ins, "atp")
                    if c >= 2:
                        te.wait_ge(ctx.sem("aes"), A["aes"] + gh * 32 + (c - 1) * 2)
                    for hh, (r0, pse) in enumerate([(0, psE0), (64, psE1)]):
                        te.matmul(pse[:, 0:128], KCOS[r0:r0 + 64, cs], QCOS[r0:r0 + 64, cs],
                                  start=True, stop=False)
                        ins = te.matmul(pse[:, 0:128], KSIN[r0:r0 + 64, cs], QSIN[r0:r0 + 64, cs],
                                        start=False, stop=True)
                        ctx.inc(ins, "ae")
                    te.wait_ge(ctx.sem("aem"), A["aem"] + gh * 32 + c * 2 + 2)
                    te.wait_ge(ctx.sem("akv"), A["akv"] + gh * 16 + c + 1)
                    if c >= 1:
                        te.wait_ge(ctx.sem("ass"), A["ass"] + gh * 16 + c)
                        te.wait_ge(ctx.sem("aoe"), A["aoe"] + gh * 32 + 2 * c)
                        te.wait_ge(ctx.sem("ath"), A["ath"] + gh * 16 + c)
                    kv = KVT[:, par * 384:(par + 1) * 384]
                    if c == 0:
                        te.matmul(psOA[0:64, 0:128], kv[:, 128:192],
                                  ES2[:, par * 256: par * 256 + 128],
                                  start=True, stop=True)
                    else:
                        te.matmul(psOA[0:64, 0:128], kv[:, 128:192],
                                  ES2[:, par * 256: par * 256 + 128],
                                  start=True, stop=False)
                        te.matmul(psOA[0:64, 0:128], SSB[0:64, 0:64], QCOS[0:64, cs],
                                  start=False, stop=False, skip_group_check=True)
                        te.matmul(psOA[0:64, 0:128], SSB[0:64, 64:128], QSIN[0:64, cs],
                                  start=False, stop=True, skip_group_check=True)
                    te.matmul(psOB[0:64, 0:128], kv[:, 320:384],
                              ES2[:, par * 256 + 128: par * 256 + 256],
                              start=True, stop=True)
                    if c >= 1:
                        te.matmul(psOC[0:64, 0:128], SSB[64:128, 0:64], QCOS[64:128, cs],
                                  start=True, stop=False)
                        te.matmul(psOC[0:64, 0:128], SSB[64:128, 64:128], QSIN[64:128, cs],
                                  start=False, stop=True)
                    te.matmul(psST[0:64, 0:64], kv[:, 0:64], kv[:, 128:192],
                              start=False, stop=True, skip_group_check=True)
                    te.matmul(psST[0:64, 64:128], kv[:, 64:128], kv[:, 128:192],
                              start=False, stop=True, skip_group_check=True)
                    te.matmul(psST[0:64, 128:192], kv[:, 192:256], kv[:, 320:384],
                              start=False, stop=True, skip_group_check=True)
                    ins = te.matmul(psST[0:64, 192:256], kv[:, 256:320], kv[:, 320:384],
                                    start=False, stop=True, skip_group_check=True)
                    ctx.inc(ins, "aop")
            return f
        b.tensor(mk_pe2(gh))

        def mk_se4(gh):
            def f(se):
                for c in range(NT):
                    par = c % 2
                    se.wait_ge(ctx.sem("atp"), A["atp"] + gh * 16 + c + 1)
                    if c >= 2:
                        se.wait_ge(ctx.sem("aop"), A["aop"] + gh * 16 + c - 1)
                    kv = KVT[:, par * 384:(par + 1) * 384]
                    se.copy(kv[:, 0:192], psT0[:, 0:192])
                    ctx.inc(se.copy(kv[:, 192:384], psT1[:, 0:192]), "akv")
                    se.wait_ge(ctx.sem("ae"), A["ae"] + gh * 32 + c * 2 + 2)
                    ctx.inc(se.copy(ESR[:, par * 256: par * 256 + 128], psE0[:, 0:128]), "aes")
                    ctx.inc(se.copy(ESR[:, par * 256 + 128: par * 256 + 256], psE1[:, 0:128]),
                            "aes")
                    se.wait_ge(ctx.sem("aop"), A["aop"] + gh * 16 + c + 1)
                    se.copy(SSB[0:64, 0:128], psST[0:64, 0:128])
                    ctx.inc(se.copy(SSB[64:128, 0:128], psST[0:64, 128:256]), "ass")
                    if c >= 1:
                        ins = se.copy(TBH[0:64, par * 128:(par + 1) * 128], psOC[0:64, 0:128])
                    else:
                        ins = se.copy(TBH[0:64, 0:1], psOC[0:64, 0:1])
                    ctx.inc(ins, "ath")
                    se.wait_ge(ctx.sem("aoe"), A["aoe"] + gh * 32 + 2 * c + 2)
                    ocs = np.s_[gh * N + c * 128: gh * N + (c + 1) * 128]
                    ctx.inc(se.copy(OUT[64:128, ocs], TB2[0:64, par * 128:(par + 1) * 128]),
                            "ao2")
            return f
        b.scalar(mk_se4(gh))

        def mk_po2(gh):
            def f(po):
                for c in range(NT):
                    par = c % 2
                    po.wait_ge(ctx.sem("aes"), A["aes"] + gh * 32 + c * 2 + 2)
                    if c >= 2:
                        po.wait_ge(ctx.sem("aop"), A["aop"] + gh * 16 + c - 1)
                    po.affine_select(ES2[:, par * 256: par * 256 + 128],
                                     ESR[:, par * 256: par * 256 + 128],
                                     [[1, 128]], ALU.is_ge, 0.0, base=0, channel_multiplier=-1)
                    ctx.inc(po.affine_select(ES2[:, par * 256 + 128: par * 256 + 256],
                                             ESR[:, par * 256 + 128: par * 256 + 256],
                                             [[1, 128]], ALU.is_ge, 0.0,
                                             base=0, channel_multiplier=-1), "aem")
                    ctx.inc(po.tensor_copy(ES2[0:1, par * 256: par * 256 + 1],
                                           ES2[0:1, par * 256: par * 256 + 1]), "aem")
            return f
        b.gpsimd(mk_po2(gh))

        def mk_ve4(gh):
            def f(ve):
                for c in range(NT):
                    par = c % 2
                    ocs = np.s_[gh * N + c * 128: gh * N + (c + 1) * 128]
                    ve.wait_ge(ctx.sem("aop"), A["aop"] + gh * 16 + c + 1)
                    if c >= 2:
                        ve.wait_ge(ctx.sem("ao2"), A["ao2"] + gh * 16 + c - 1)
                    ctx.inc(ve.tensor_copy(OUT[0:64, ocs], psOA[0:64, 0:128]), "aoe")
                    ve.wait_ge(ctx.sem("ath"), A["ath"] + gh * 16 + c + 1)
                    if c >= 1:
                        ins = ve.tensor_add(TB2[0:64, par * 128:(par + 1) * 128],
                                            psOB[0:64, 0:128],
                                            TBH[0:64, par * 128:(par + 1) * 128])
                    else:
                        ins = ve.tensor_copy(TB2[0:64, par * 128:(par + 1) * 128],
                                             psOB[0:64, 0:128])
                    ctx.inc(ins, "aoe")
            return f
        b.vector(mk_ve4(gh))

    barrier(ctx)
    if TAP == "aatt":
        emit_tap(ctx, OUT[:, :], NG * N)
    if TAP == "aqc":
        emit_tap(ctx, QCOS[:, :], N)
    if TAP == "akc":
        emit_tap(ctx, KCOS[:, :], N)
    if TAP == "aqg":
        emit_tap(ctx, QG[:, :], N)
    es_a.close()
    if upto < 2:
        return
    es_f = contextlib.ExitStack()
    es.callback(es_f.close)
    emit_phase_a_finale(ctx, aps, es_f)
    if upto < 3:
        return
    barrier(ctx)
    es_f.close()
    psL = [es.enter_context(nc.psum_tensor(f"a_psL{i}", [128, TCW], F32)) for i in range(4)]
    emit_ln(ctx, es, "lna", psL)
    barrier(ctx)


def emit_phase_a_finale(ctx, aps, es):
    nc, b = ctx.nc, ctx.block
    XF, XB = ctx.t_["XF"], ctx.t_["XB"]
    CPF, CB = ctx.t_["CPF"], ctx.t_["CB"]
    ci, bi = ctx.t_["cpf_idx"], ctx.t_["cb_idx"]
    OUT = ctx.t_["a_OUT"]
    onesb = CB[:, bi["ones"]:bi["ones"] + 1]
    ones_row_b = CB[0:1, bi["onesrow"]:bi["onesrow"] + 128]

    G1B = es.enter_context(nc.sbuf_tensor("af_G1B", [128, NG * 64], BF16))
    G2B = es.enter_context(nc.sbuf_tensor("af_G2B", [64, D], BF16))
    WOB = es.enter_context(nc.sbuf_tensor("af_WOB", [128, NG * D], BF16))
    G1T = es.enter_context(nc.sbuf_tensor("af_G1T", [64, N], BF16))
    GTC = es.enter_context(nc.sbuf_tensor("af_GTC", [128, 2 * TCW], BF16))
    GOC = es.enter_context(nc.sbuf_tensor("af_GOC", [128, 2 * NG * TCW], BF16))
    SQ5 = es.enter_context(nc.sbuf_tensor("af_SQ5", [128, NG * TCW], BF16))
    AB5 = es.enter_context(nc.sbuf_tensor("af_AB5", [1, 4 * TCW], F32))
    AL5 = es.enter_context(nc.sbuf_tensor("af_AL5", [1, 8 * TCW], BF16))
    RP5 = es.enter_context(nc.sbuf_tensor("af_RP5", [128, 2 * TCW], F32))
    T55 = es.enter_context(nc.sbuf_tensor("af_T55", [128, TCW], F32))
    ps = [es.enter_context(nc.psum_tensor(f"af_ps{i}", [128, TCW], F32)) for i in range(4)]

    @b.gpsimd
    def _(po):
        for g in range(NG):
            ctx.inc(po.dma_start(out=G1B[:, g * 64:(g + 1) * 64],
                                 in_=aps["a_g1"][g * 128:(g + 1) * 128, :]), "dmw", 16)
        ctx.inc(po.dma_start(out=G2B[0:64, :], in_=aps["a_g2"][:, :]), "dmw", 16)
        for g in range(NG):
            ctx.inc(po.dma_start(out=WOB[:, g * D:(g + 1) * D],
                                 in_=aps["a_wo"][g * 128:(g + 1) * 128, :]), "dmw", 16)
    wtok = ctx.tok("dmw")
    barrier(ctx)

    F = {k: ctx.base(k) for k in ["fg1", "fge", "fgp", "fgt", "fwo", "fgo", "fsq",
                                  "fmm", "frm", "frs", "frd", "fpb", "fbc", "fap",
                                  "fgb", "fge2", "frr"]}
    gamc = ci["a_gam"]
    gamr = bi["a_gamr"]
    betr = bi["a_betr"]

    @b.tensor
    def _(te):
        te.wait_ge(*wtok)
        for tc in range(NTC):
            if tc >= 2:
                te.wait_ge(ctx.sem("fge"), F["fge"] + tc - 1)
            for kg in range(NG):
                ins = te.matmul(ps[tc % 2][0:64, 0:TCW], G1B[:, kg * 64:(kg + 1) * 64],
                                XB[:, kg * N + tc * TCW: kg * N + (tc + 1) * TCW],
                                start=(kg == 0), stop=(kg == NG - 1))
            ctx.inc(ins, "fg1")

    @b.vector
    def _(ve):
        for tc in range(NTC):
            ve.wait_ge(ctx.sem("fg1"), F["fg1"] + tc + 1)
            ctx.inc(ve.tensor_copy(G1T[0:64, tc * TCW:(tc + 1) * TCW],
                                   ps[tc % 2][0:64, 0:TCW]), "fge")

    for tc in range(NTC):
        par = tc % 2

        def mk_pe(tc, par):
            def f(te):
                te.wait_ge(ctx.sem("fge"), F["fge"] + tc + 1)
                for dog in range(NG):
                    jg = tc * NG + dog
                    if jg >= 1:
                        te.wait_ge(ctx.sem("fgt"), F["fgt"] + jg)
                    ctx.inc(te.matmul(ps[2][:, 0:TCW], G2B[0:64, dog * 128:(dog + 1) * 128],
                                      G1T[0:64, tc * TCW:(tc + 1) * TCW],
                                      start=True, stop=True), "fgp")
                    if jg >= 1:
                        te.wait_ge(ctx.sem("fgo"), F["fgo"] + jg)
                    for kg in range(NG):
                        ins = te.matmul(ps[3][:, 0:TCW],
                                        WOB[:, kg * D + dog * 128: kg * D + (dog + 1) * 128],
                                        OUT[:, kg * N + tc * TCW: kg * N + (tc + 1) * TCW],
                                        start=(kg == 0), stop=(kg == NG - 1))
                    ctx.inc(ins, "fwo")
            return f
        b.tensor(mk_pe(tc, par))

        def mk_se(tc, par):
            def f(se):
                for dog in range(NG):
                    jg = tc * NG + dog
                    se.wait_ge(ctx.sem("fgp"), F["fgp"] + jg + 1)
                    if jg >= 2:
                        se.wait_ge(ctx.sem("fgo"), F["fgo"] + jg - 1)
                    ctx.inc(se.activation(GTC[:, (jg % 2) * TCW:((jg % 2) + 1) * TCW],
                                          ps[2][:, 0:TCW], AF.Sigmoid), "fgt")
                    se.wait_ge(ctx.sem("fgo"), F["fgo"] + jg + 1)
                    if dog == 0 and tc >= 1:
                        se.wait_ge(ctx.sem("fmm"), F["fmm"] + tc * 16)
                    ctx.inc(se.activation(
                        SQ5[:, dog * TCW:(dog + 1) * TCW],
                        GOC[:, par * NG * TCW + dog * TCW: par * NG * TCW + (dog + 1) * TCW],
                        AF.Square), "fsq")
            return f
        b.scalar(mk_se(tc, par))

        def mk_ve(tc, par):
            def f(ve):
                for dog in range(NG):
                    jg = tc * NG + dog
                    ve.wait_ge(ctx.sem("fgt"), F["fgt"] + jg + 1)
                    ve.wait_ge(ctx.sem("fwo"), F["fwo"] + jg + 1)
                    if dog == 0 and tc >= 2:
                        ve.wait_ge(ctx.sem("fap"), F["fap"] + (tc - 1) * NG)
                    ctx.inc(ve.tensor_mul(
                        GOC[:, par * NG * TCW + dog * TCW: par * NG * TCW + (dog + 1) * TCW],
                        ps[3][:, 0:TCW], GTC[:, (jg % 2) * TCW:((jg % 2) + 1) * TCW]), "fgo")
            return f
        b.vector(mk_ve(tc, par))

        def mk_pe2(tc, par):
            def f(te):
                for p8 in range(2):
                    for dog in range(NG):
                        jg = tc * NG + dog
                        te.wait_ge(ctx.sem("fsq"), F["fsq"] + jg + 1)
                        if p8 == 0 and dog == 0 and tc == 0:
                            te.wait_ge(ctx.sem("fge"), F["fge"] + NTC)
                        if p8 == 0 and dog == 0 and tc >= 1:
                            te.wait_ge(ctx.sem("frd"), F["frd"] + tc)
                            te.wait_ge(ctx.sem("fbc"), F["fbc"] + tc)
                            te.wait_ge(ctx.sem("fge2"), F["fge2"] + tc * NG)
                        if p8 == 0:
                            ins = te.matmul(
                                ps[0][0:1, 0:TCW], onesb,
                                GOC[:, par * NG * TCW + dog * TCW: par * NG * TCW + (dog + 1) * TCW],
                                start=(dog == 0), stop=(dog == NG - 1))
                        else:
                            ins = te.matmul(ps[1][0:1, 0:TCW], onesb,
                                            SQ5[:, dog * TCW:(dog + 1) * TCW],
                                            start=(dog == 0), stop=(dog == NG - 1))
                        ctx.inc(ins, "fmm")
            return f
        b.tensor(mk_pe2(tc, par))

        A5 = AB5[0:1, par * 2 * TCW: par * 2 * TCW + TCW]
        B5 = AB5[0:1, par * 2 * TCW + TCW:(par + 1) * 2 * TCW]
        base_l = par * 4 * TCW
        AH5 = AL5[0:1, base_l: base_l + TCW]
        ALo5 = AL5[0:1, base_l + TCW: base_l + 2 * TCW]
        BH5 = AL5[0:1, base_l + 2 * TCW: base_l + 3 * TCW]
        BL5 = AL5[0:1, base_l + 3 * TCW: base_l + 4 * TCW]

        def mk_rm(tc, A5, B5):
            def f(ve):
                ve.wait_ge(ctx.sem("fmm"), F["fmm"] + (tc + 1) * 16)
                if tc >= 2:
                    ve.wait_ge(ctx.sem("fgb"), F["fgb"] + (tc - 1) * NG)
                ctx.inc(ve.tensor_scalar_mul(B5, ps[0][0:1, 0:TCW], 1.0 / D), "frr")
                ve.wait_ge(*ctx.tok("frr"))
                ctx.inc(ve.tensor_scalar_mul(A5, ps[1][0:1, 0:TCW], 1.0 / D), "frr")
                ve.wait_ge(*ctx.tok("frr"))
                ctx.inc(ve.tensor_mul(ps[0][0:1, 0:TCW], B5, B5), "frr")
                ve.wait_ge(*ctx.tok("frr"))
                ctx.inc(ve.tensor_sub(A5, A5, ps[0][0:1, 0:TCW]), "frr")
                ve.wait_ge(*ctx.tok("frr"))
                ctx.inc(ve.tensor_scalar_add(A5, A5, 1e-5), "frm")
            return f
        b.vector(mk_rm(tc, A5, B5))

        def mk_rs(tc, A5):
            def f(se):
                se.wait_ge(ctx.sem("frm"), F["frm"] + tc + 1)
                ctx.inc(se.activation(A5, A5, AF.Sqrt), "frs")
            return f
        b.scalar(mk_rs(tc, A5))

        def mk_rd(tc, A5, B5, AH5, ALo5, BH5, BL5):
            def f(ve):
                ve.wait_ge(ctx.sem("frs"), F["frs"] + tc + 1)
                ctx.inc(ve.reciprocal(A5, A5), "frr")
                ve.wait_ge(*ctx.tok("frr"))
                ctx.inc(ve.scalar_tensor_tensor(B5, B5, -1.0, A5, ALU.mult, ALU.mult), "frr")
                ve.wait_ge(*ctx.tok("frr"))
                ve.tensor_copy(AH5, A5)
                ctx.inc(ve.tensor_sub(ALo5, A5, AH5), "frr")
                ve.wait_ge(*ctx.tok("frr"))
                ve.tensor_copy(BH5, B5)
                ctx.inc(ve.tensor_sub(BL5, B5, BH5), "frd")
            return f
        b.vector(mk_rd(tc, A5, B5, AH5, ALo5, BH5, BL5))

        def mk_pb(tc, AH5, ALo5):
            def f(te):
                te.wait_ge(ctx.sem("frd"), F["frd"] + tc + 1)
                if tc >= 1:
                    te.wait_ge(ctx.sem("fbc"), F["fbc"] + tc)
                te.matmul(ps[0][:, 0:TCW], ones_row_b, AH5, start=True, stop=False)
                ctx.inc(te.matmul(ps[0][:, 0:TCW], ones_row_b, ALo5, start=False, stop=True),
                        "fpb")
            return f
        b.tensor(mk_pb(tc, AH5, ALo5))

        def mk_bc(tc):
            def f(se):
                se.wait_ge(ctx.sem("fpb"), F["fpb"] + tc + 1)
                if tc >= 1:
                    se.wait_ge(ctx.sem("fap"), F["fap"] + tc * NG)
                ctx.inc(se.copy(RP5[:, 0:TCW], ps[0][:, 0:TCW]), "fbc")
            return f
        b.scalar(mk_bc(tc))

        def mk_pgb(tc, BH5, BL5):
            def f(te):
                for dog in range(NG):
                    jg = tc * NG + dog
                    te.wait_ge(ctx.sem("fbc"), F["fbc"] + tc + 1)
                    if jg >= 1:
                        te.wait_ge(ctx.sem("fge2"), F["fge2"] + jg)
                    gr = CB[0:1, gamr + dog * 128: gamr + (dog + 1) * 128]
                    br = CB[0:1, betr + dog * 128: betr + (dog + 1) * 128]
                    te.matmul(ps[1][:, 0:TCW], gr, BH5, start=True, stop=False)
                    te.matmul(ps[1][:, 0:TCW], gr, BL5, start=False, stop=False)
                    ctx.inc(te.matmul(ps[1][:, 0:TCW], br,
                                      ones_row_b[0:1, 0:1].broadcast_to((1, TCW)),
                                      start=False, stop=True), "fgb")
            return f
        b.tensor(mk_pgb(tc, BH5, BL5))

        def mk_ge2(tc):
            def f(se):
                for dog in range(NG):
                    jg = tc * NG + dog
                    se.wait_ge(ctx.sem("fgb"), F["fgb"] + jg + 1)
                    se.wait_ge(ctx.sem("fap"), F["fap"] + jg)
                    ctx.inc(se.copy(RP5[:, TCW:2 * TCW], ps[1][:, 0:TCW]), "fge2")
            return f
        b.scalar(mk_ge2(tc))

        def mk_ap(tc, par):
            def f(ve):
                for dog in range(NG):
                    jg = tc * NG + dog
                    ve.wait_ge(ctx.sem("fge2"), F["fge2"] + jg + 1)
                    ve.wait_ge(ctx.sem("fbc"), F["fbc"] + tc + 1)
                    gcol = CPF[:, gamc + dog: gamc + dog + 1]
                    src = GOC[:, par * NG * TCW + dog * TCW: par * NG * TCW + (dog + 1) * TCW]
                    sl = np.s_[:, dog * N + tc * TCW: dog * N + (tc + 1) * TCW]
                    ctx.inc(ve.scalar_tensor_tensor(T55[:, 0:TCW], src, gcol,
                                                    RP5[:, 0:TCW], ALU.mult, ALU.mult), "frr")
                    ve.wait_ge(*ctx.tok("frr"))
                    ctx.inc(ve.tensor_add(T55[:, 0:TCW], T55[:, 0:TCW], RP5[:, TCW:2 * TCW]),
                            "frr")
                    ve.wait_ge(*ctx.tok("frr"))
                    ctx.inc(ve.tensor_add(XF[sl], XF[sl], T55[:, 0:TCW]), "fap")
            return f
        b.vector(mk_ap(tc, par))



def host_prep_b(inp):
    d = {
        "b_gwt": np.ascontiguousarray(
            np.asarray(inp["L0_ffn_gate_w"], np.float32).T.reshape(8, 128, 16)
            .transpose(1, 0, 2).reshape(128, 128)),
        "b_be2": np.asarray(inp["L0_ffn_be2"], np.float32),
        "b_we1": np.asarray(inp["L0_ffn_We1"], np.float32),
        "b_we3": np.asarray(inp["L0_ffn_We3"], np.float32),
        "b_we2": np.asarray(inp["L0_ffn_We2"], np.float32),
        "b_ws1": np.asarray(inp["L0_ffn_ws1"], np.float32),
        "b_ws3": np.asarray(inp["L0_ffn_ws3"], np.float32),
        "b_ws2": np.asarray(inp["L0_ffn_ws2"], np.float32),
    }
    return d


def decl_b(nc):
    return {
        "b_gwt": nc.dram_tensor("b_gwt", [128, 128], F32, kind="ExternalInput").ap(),
        "b_be2": nc.dram_tensor("b_be2", [16, D], F32, kind="ExternalInput").ap(),
        "b_we1": nc.dram_tensor("b_we1", [16, D, 512], F32, kind="ExternalInput").ap(),
        "b_we3": nc.dram_tensor("b_we3", [16, D, 512], F32, kind="ExternalInput").ap(),
        "b_we2": nc.dram_tensor("b_we2", [16, 512, D], F32, kind="ExternalInput").ap(),
        "b_ws1": nc.dram_tensor("b_ws1", [D, D], F32, kind="ExternalInput").ap(),
        "b_ws3": nc.dram_tensor("b_ws3", [D, D], F32, kind="ExternalInput").ap(),
        "b_ws2": nc.dram_tensor("b_ws2", [D, D], F32, kind="ExternalInput").ap(),
    }


def pack_b(cpf, cb, inp):
    cpf.add("b_gb", np.broadcast_to(np.asarray(inp["L0_ffn_gate_b"], np.float32), (128, 16)))
    cpf.add("b_be1", np.asarray(inp["L0_ffn_be1"], np.float32).reshape(16 * 4, 128).T)
    cpf.add("b_be3", np.asarray(inp["L0_ffn_be3"], np.float32).reshape(16 * 4, 128).T)
    cpf.add("b_bs1", np.asarray(inp["L0_ffn_bs1"], np.float32).reshape(8, 128).T)
    cpf.add("b_bs3", np.asarray(inp["L0_ffn_bs3"], np.float32).reshape(8, 128).T)
    cpf.add("b_bs2", col_pp(inp["L0_ffn_bs2"]))


def emit_phase_b(ctx, aps, upto=99):
    nc, b = ctx.nc, ctx.block
    es = ctx.es_phase
    XF, XB = ctx.t_["XF"], ctx.t_["XB"]
    CPF, CB = ctx.t_["CPF"], ctx.t_["CB"]
    ci, bi = ctx.t_["cpf_idx"], ctx.t_["cb_idx"]
    BIG = 1e30

    GWT = es.enter_context(nc.sbuf_tensor("b_GWT", [128, 128], F32))
    GWB = es.enter_context(nc.sbuf_tensor("b_GWB", [128, 128], BF16))
    GWL = es.enter_context(nc.sbuf_tensor("b_GWL", [128, 128], BF16))
    XLS = es.enter_context(nc.sbuf_tensor("b_XLS", [128, 2 * 1024], BF16))
    BE2 = es.enter_context(nc.sbuf_tensor("b_BE2", [16, D], BF16))
    SCR = es.enter_context(nc.sbuf_tensor("b_SCR", [128, 256], F32))
    CMB = es.enter_context(nc.sbuf_tensor("b_CMB", [128, NT * 16], BF16))
    CTT = es.enter_context(nc.sbuf_tensor("b_CTT", [16, N], BF16))
    CRP = es.enter_context(nc.sbuf_tensor("b_CRP", [128, 2 * 1024], BF16))
    HS = [es.enter_context(nc.sbuf_tensor(f"b_HS{i}", [128, 4 * 1024], BF16)) for i in range(3)]
    es_w = contextlib.ExitStack()
    es.callback(es_w.close)
    WE1 = [es_w.enter_context(nc.sbuf_tensor(f"b_WE1{i}", [128, NG * 512], BF16)) for i in range(2)]
    WE3 = [es_w.enter_context(nc.sbuf_tensor(f"b_WE3{i}", [128, NG * 512], BF16)) for i in range(2)]
    WE2 = [es_w.enter_context(nc.sbuf_tensor(f"b_WE2{i}", [128, 4 * 1024], BF16)) for i in range(2)]
    ps4 = [es.enter_context(nc.psum_tensor(f"b_ps{i}", [128, TCW], F32)) for i in range(4)]
    psY = [es.enter_context(nc.psum_tensor(f"b_psy{i}", [128, TCW], F32)) for i in range(2)]
    psC = es.enter_context(nc.psum_tensor("b_psc", [128, TCW], F32))
    psT = es.enter_context(nc.psum_tensor("b_pst", [128, 1024], BF16))

    onesrow = CB[0:1, bi["onesrow"]:bi["onesrow"] + 128]
    idn = CB[:, bi["idn"]:bi["idn"] + 128]

    @b.gpsimd
    def _(po):
        ctx.inc(po.dma_start(out=GWT[:, :], in_=aps["b_gwt"][:, :]), "dmw", 16)
        ctx.inc(po.dma_start(out=BE2[0:16, :], in_=aps["b_be2"][:, :]), "dmw", 16)
    b0tok = ctx.tok("dmw")

    BIGc = 1e30
    b_xl = ctx.base("bxl")
    b_rp = ctx.base("brp")
    b_r0 = ctx.base("br0")
    b_r1 = ctx.base("br1")
    b_r2 = ctx.base("br2")

    def rt_pre(t):
        def f(ve):
            if t == 0:
                ve.wait_ge(*b0tok)
                ve.tensor_copy(GWB[:, :], GWT[:, :])
                ve.tensor_sub(GWL[:, :], GWT[:, :], GWB[:, :])
            if t >= 2:
                ve.wait_ge(ctx.sem("brp"), b_rp + t - 1)
            par = t % 2
            for g in range(NG):
                ins = ve.tensor_sub(XLS[:, par * 1024 + g * 128: par * 1024 + (g + 1) * 128],
                                    XF[:, g * N + t * 128: g * N + (t + 1) * 128],
                                    XB[:, g * N + t * 128: g * N + (t + 1) * 128])
            ctx.inc(ins, "bxl")
        return f

    def rt_mm(t):
        def f(te):
            te.wait_ge(ctx.sem("bxl"), b_xl + t + 1)
            if t >= 2:
                te.wait_ge(ctx.sem("br1"), b_r1 + t - 1)
            p = ps4[t % 3]
            for g in range(NG):
                te.matmul(p[:, 0:16], XB[:, g * N + t * 128: g * N + (t + 1) * 128],
                          GWB[:, g * 16:(g + 1) * 16], start=(g == 0), stop=False)
                te.matmul(p[:, 0:16], XB[:, g * N + t * 128: g * N + (t + 1) * 128],
                          GWL[:, g * 16:(g + 1) * 16], start=False, stop=False)
                ins = te.matmul(p[:, 0:16],
                                XLS[:, (t % 2) * 1024 + g * 128: (t % 2) * 1024 + (g + 1) * 128],
                                GWB[:, g * 16:(g + 1) * 16],
                                start=False, stop=(g == NG - 1))
            ctx.inc(ins, "brp")
        return f

    def rt_s0(t):
        def f(ve):
            ve.wait_ge(ctx.sem("brp"), b_rp + t + 1)
            if t >= 2:
                ve.wait_ge(ctx.sem("br2"), b_r2 + t - 1)
            p = ps4[t % 3]
            if TAP == "praw":
                ve.tensor_copy(CMB[:, t * 16:(t + 1) * 16], p[:, 0:16])
            ctx.inc(ve.reduce_max(SCR[:, (t % 2) * 128: (t % 2) * 128 + 1], p[:, 0:16],
                                  AX.X, negate=True), "br0")
        return f

    def rt_s1(t):
        def f(se):
            se.wait_ge(ctx.sem("br0"), b_r0 + t + 1)
            p = ps4[t % 3]
            pb = (t % 2) * 128
            ctx.inc(se.activation(SCR[:, pb + 16: pb + 32], p[:, 0:16], AF.Exp,
                                  bias=SCR[:, pb:pb + 1]), "br1")
        return f

    def rt_s2(t):
        def f(ve):
            ve.wait_ge(ctx.sem("br1"), b_r1 + t + 1)

            def ev(ins):
                ctx.inc(ins, "rtr")
                ve.wait_ge(*ctx.tok("rtr"))

            pb = (t % 2) * 128
            E = SCR[:, pb + 16: pb + 32]
            SM = SCR[:, pb + 1: pb + 2]
            OR = SCR[:, pb + 32: pb + 48]
            S2 = SCR[:, pb + 48: pb + 64]
            PR = SCR[:, pb + 64: pb + 88]
            GS = SCR[:, pb + 88: pb + 92]
            T1 = SCR[:, pb + 92: pb + 96]
            M = SCR[:, pb + 96: pb + 112]
            ev(ve.reduce_sum(SM, E, AX.X))
            ev(ve.reciprocal(SCR[:, pb + 2: pb + 3], SM))
            ev(ve.tensor_scalar_mul(OR, E, SCR[:, pb + 2: pb + 3]))
            ev(ve.tensor_add(S2, OR, CPF[:, ci["b_gb"]:ci["b_gb"] + 16]))
            k = 0
            for a in range(4):
                for c in range(a + 1, 4):
                    ev(ve.tensor_add(PR[:, k * 4:(k + 1) * 4], S2[:, a:16:4], S2[:, c:16:4]))
                    k += 1
            ev(ve.tensor_max(GS, PR[:, 0:4], PR[:, 4:8]))
            ev(ve.tensor_max(GS, GS, PR[:, 8:12]))
            ev(ve.tensor_max(GS, GS, PR[:, 12:16]))
            ev(ve.tensor_max(GS, GS, PR[:, 16:20]))
            ev(ve.tensor_max(GS, GS, PR[:, 20:24]))
            ev(ve.reduce_max(SCR[:, pb + 3: pb + 4], GS, AX.X))
            ev(ve.tensor_scalar(T1, GS, SCR[:, pb + 3: pb + 4], None, ALU.is_equal))
            ev(ve.scalar_tensor_tensor(T1, T1, -BIGc, GS, ALU.mult, ALU.add))
            ev(ve.reduce_max(SCR[:, pb + 4: pb + 5], T1, AX.X))
            ev(ve.tensor_scalar(T1, GS, SCR[:, pb + 4: pb + 5], None, ALU.is_ge))
            ev(ve.tensor_scalar(T1, T1, 1.0, BIGc, ALU.subtract, ALU.mult))
            mv = M.rearrange("p (g j) -> p g j", j=4)
            s2v = S2.rearrange("p (g j) -> p g j", j=4)
            ev(ve.tensor_tensor(mv, s2v, bcast8(T1, 4), ALU.add))
            ev(ve.reduce_max(SCR[:, pb + 5: pb + 6], M, AX.X))
            ev(ve.tensor_scalar(E, M, SCR[:, pb + 5: pb + 6], None, ALU.is_equal))
            ev(ve.scalar_tensor_tensor(E, E, -BIGc, M, ALU.mult, ALU.add))
            ev(ve.reduce_max(SCR[:, pb + 6: pb + 7], E, AX.X))
            ev(ve.tensor_scalar(E, M, SCR[:, pb + 6: pb + 7], None, ALU.is_ge))
            if TAP == "orig":
                ctx.inc(ve.tensor_copy(CMB[:, t * 16:(t + 1) * 16], OR), "br2")
            elif TAP == "selm":
                ctx.inc(ve.tensor_copy(CMB[:, t * 16:(t + 1) * 16], E), "br2")
            elif TAP == "s2m":
                ctx.inc(ve.tensor_copy(CMB[:, t * 16:(t + 1) * 16], S2), "br2")
            else:
                ev(ve.tensor_mul(E, OR, E))
                ctx.inc(ve.tensor_copy(CMB[:, t * 16:(t + 1) * 16], E), "br2")
        return f

    for t in range(NT):
        b.vector(rt_pre(t))
        b.tensor(rt_mm(t))
        b.vector(rt_s0(t))
        b.scalar(rt_s1(t))
        b.vector(rt_s2(t))
    barrier(ctx)
    if TAP in ("cmb", "orig", "selm", "s2m", "praw"):
        emit_tap(ctx, CMB[:, :], NT * 16)
    if TAP == "gwt":
        emit_tap(ctx, GWT[:, :], 128)
    if TAP == "xbt":
        emit_tap(ctx, XB[:, :], NG * N)
    if TAP == "scr":
        emit_tap(ctx, SCR[:, :], 256)
    if TAP == "gwb":
        emit_tap(ctx, GWB[:, :], 128)

    if upto < 2:
        return
    b_tp = ctx.base("btp")
    b_tf = ctx.base("btf")

    @b.tensor
    def _(te):
        for t in range(NT):
            if t >= 2:
                te.wait_ge(ctx.sem("btf"), b_tf + t - 1)
            ctx.inc(te.transpose(psT[0:16, (t % 2) * 128:(t % 2) * 128 + 128],
                                 CMB[:, t * 16:(t + 1) * 16], idn), "btp")

    @b.scalar
    def _(se):
        for t in range(NT):
            se.wait_ge(ctx.sem("btp"), b_tp + t + 1)
            ctx.inc(se.copy(CTT[0:16, t * 128:(t + 1) * 128],
                            psT[0:16, (t % 2) * 128:(t % 2) * 128 + 128]), "btf")
    barrier(ctx)

    if upto < 3:
        return
    items = [(e, hf) for e in range(18) for hf in range(2)]
    if TAP in ("ht0", "y0"):
        items = items[:1]
    nit = len(items)
    HR, YR = 8, 16

    def load_expert(po, e, slot):
        if e < 16:
            for g in range(NG):
                ctx.inc(po.dma_start(out=WE1[slot][:, g * 512:(g + 1) * 512],
                                     in_=aps["b_we1"][e, g * 128:(g + 1) * 128, :]), "dmw", 16)
            for g in range(NG):
                ctx.inc(po.dma_start(out=WE3[slot][:, g * 512:(g + 1) * 512],
                                     in_=aps["b_we3"][e, g * 128:(g + 1) * 128, :]), "dmw", 16)
            for g in range(4):
                ctx.inc(po.dma_start(out=WE2[slot][:, g * 1024:(g + 1) * 1024],
                                     in_=aps["b_we2"][e, g * 128:(g + 1) * 128, :]), "dmw", 16)
        else:
            hh = e - 16
            for g in range(NG):
                ctx.inc(po.dma_start(out=WE1[slot][:, g * 512:(g + 1) * 512],
                                     in_=aps["b_ws1"][g * 128:(g + 1) * 128, hh * 512:(hh + 1) * 512]), "dmw", 16)
            for g in range(NG):
                ctx.inc(po.dma_start(out=WE3[slot][:, g * 512:(g + 1) * 512],
                                     in_=aps["b_ws3"][g * 128:(g + 1) * 128, hh * 512:(hh + 1) * 512]), "dmw", 16)
            for g in range(4):
                ctx.inc(po.dma_start(out=WE2[slot][:, g * 1024:(g + 1) * 1024],
                                     in_=aps["b_ws2"][hh * 512 + g * 128: hh * 512 + (g + 1) * 128, :]), "dmw", 16)
        return ctx.tok("dmw")

    wtoks = {}

    @b.gpsimd
    def _(po):
        wtoks[0] = load_expert(po, 0, 0)
        wtoks[1] = load_expert(po, 1, 1)

    B3 = {k: ctx.base(k) for k in ["bcp", "bcf", "bh1", "bh3", "bsa", "bsd",
                                   "bht", "byp", "byf"]}
    crp_ix = {}
    nci = 0
    for i, (e, hf) in enumerate(items):
        if e < 16:
            crp_ix[i] = nci
            nci += 1

    def emit_y_rounds(te, i):
        e, hf = items[i]
        t0 = hf * 1024
        te.wait_ge(ctx.sem("bht"), B3["bht"] + (i + 1) * HR)
        for r in range(YR):
            jy = i * YR + r
            dog, tcc = divmod(r, 2)
            if jy >= 2:
                te.wait_ge(ctx.sem("byf"), B3["byf"] + jy - 1)
            pb = psY[jy % 2]
            for kg in range(4):
                ins = te.matmul(pb[:, 0:TCW],
                                WE2[e % 2][:, kg * 1024 + dog * 128: kg * 1024 + (dog + 1) * 128],
                                HS[i % 3][:, kg * 1024 + tcc * TCW: kg * 1024 + (tcc + 1) * TCW],
                                start=(kg == 0), stop=(kg == 3 and e != 0))
            if e == 0:
                ins = te.matmul(pb[:, 0:TCW], BE2[0:16, dog * 128:(dog + 1) * 128],
                                CTT[0:16, t0 + tcc * TCW: t0 + (tcc + 1) * TCW],
                                start=False, stop=True)
            ctx.inc(ins, "byp")

    def emit_y_evacs(ve, i):
        ep, hp = items[i]
        t0p = hp * 1024
        for r in range(YR):
            jy = i * YR + r
            dog, tcc = divmod(r, 2)
            ve.wait_ge(ctx.sem("byp"), B3["byp"] + jy + 1)
            slx = np.s_[:, dog * N + t0p + tcc * TCW: dog * N + t0p + (tcc + 1) * TCW]
            if ep == 16:
                ins = ve.scalar_tensor_tensor(
                    XF[slx], psY[jy % 2][:, 0:TCW],
                    CPF[:, ci["b_bs2"] + dog: ci["b_bs2"] + dog + 1],
                    XF[slx], ALU.add, ALU.add)
            else:
                ins = ve.tensor_add(XF[slx], XF[slx], psY[jy % 2][:, 0:TCW])
            ctx.inc(ins, "byf")

    for i, (e, hf) in enumerate(items):
        t0 = hf * 1024

        def mk_pe(i, e, hf, t0):
            def f(te):
                if e < 16:
                    for cc in range(2):
                        jc = 2 * crp_ix[i] + cc
                        if jc >= 1:
                            te.wait_ge(ctx.sem("bcf"), B3["bcf"] + jc)
                        sel = idn[0:16, e:e + 1].broadcast_to((16, 128))
                        ctx.inc(te.matmul(psC[:, 0:TCW], sel,
                                          CTT[0:16, t0 + cc * TCW: t0 + (cc + 1) * TCW],
                                          start=True, stop=True), "bcp")
                if hf == 0:
                    te.wait_ge(*wtoks[e])
                for r in range(HR):
                    j = i * HR + r
                    dog4, tcc = divmod(r, 2)
                    if j >= 2:
                        te.wait_ge(ctx.sem("bsa"), B3["bsa"] + j - 1)
                    for kg in range(NG):
                        ins = te.matmul(ps4[r % 2][:, 0:TCW],
                                        WE1[e % 2][:, kg * 512 + dog4 * 128: kg * 512 + (dog4 + 1) * 128],
                                        XB[:, kg * N + t0 + tcc * TCW: kg * N + t0 + (tcc + 1) * TCW],
                                        start=(kg == 0), stop=(kg == NG - 1))
                    ctx.inc(ins, "bh1")
                    if j >= 2:
                        te.wait_ge(ctx.sem("bsd"), B3["bsd"] + j - 1)
                    for kg in range(NG):
                        ins = te.matmul(ps4[2 + r % 2][:, 0:TCW],
                                        WE3[e % 2][:, kg * 512 + dog4 * 128: kg * 512 + (dog4 + 1) * 128],
                                        XB[:, kg * N + t0 + tcc * TCW: kg * N + t0 + (tcc + 1) * TCW],
                                        start=(kg == 0), stop=(kg == NG - 1))
                    ctx.inc(ins, "bh3")
                if i >= 1:
                    emit_y_rounds(te, i - 1)
                if i == nit - 1:
                    emit_y_rounds(te, i)
            return f
        b.tensor(mk_pe(i, e, hf, t0))

        def mk_se(i, e, hf, t0):
            def f(se):
                if e < 16:
                    for cc in range(2):
                        jc = 2 * crp_ix[i] + cc
                        se.wait_ge(ctx.sem("bcp"), B3["bcp"] + jc + 1)
                        if i >= 2:
                            se.wait_ge(ctx.sem("bsd"), B3["bsd"] + (i - 1) * HR)
                        ctx.inc(se.copy(CRP[:, (i % 2) * 1024 + cc * TCW:
                                             (i % 2) * 1024 + (cc + 1) * TCW],
                                        psC[:, 0:TCW]), "bcf")
                for r in range(HR):
                    j = i * HR + r
                    dog4, tcc = divmod(r, 2)
                    se.wait_ge(ctx.sem("bh1"), B3["bh1"] + j + 1)
                    if r == 0 and i >= 1:
                        se.wait_ge(ctx.sem("bht"), B3["bht"] + i * HR)
                    if r == 0 and i >= 3:
                        se.wait_ge(ctx.sem("byp"), B3["byp"] + (i - 2) * YR)
                    if e < 16:
                        bc_col = CPF[:, ci["b_be1"] + e * 4 + dog4: ci["b_be1"] + e * 4 + dog4 + 1]
                    else:
                        bc_col = CPF[:, ci["b_bs1"] + (e - 16) * 4 + dog4:
                                     ci["b_bs1"] + (e - 16) * 4 + dog4 + 1]
                    sl = np.s_[:, dog4 * 1024 + tcc * TCW: dog4 * 1024 + (tcc + 1) * TCW]
                    ctx.inc(se.activation(HS[i % 3][sl], ps4[r % 2][:, 0:TCW], AF.Silu,
                                          bias=bc_col), "bsa")
            return f
        b.scalar(mk_se(i, e, hf, t0))

        def mk_ve(i, e, hf, t0):
            def f(ve):
                for r in range(HR):
                    j = i * HR + r
                    dog4, tcc = divmod(r, 2)
                    ve.wait_ge(ctx.sem("bh3"), B3["bh3"] + j + 1)
                    if r == 0 and i >= 2:
                        ve.wait_ge(ctx.sem("byp"), B3["byp"] + (i - 1) * YR)
                    sl = np.s_[:, dog4 * 1024 + tcc * TCW: dog4 * 1024 + (tcc + 1) * TCW]
                    if e < 16:
                        if r == 0:
                            ve.wait_ge(ctx.sem("bcf"), B3["bcf"] + 2 * (crp_ix[i] + 1))
                        b3c = CPF[:, ci["b_be3"] + e * 4 + dog4: ci["b_be3"] + e * 4 + dog4 + 1]
                        ins = ve.scalar_tensor_tensor(
                            HS[(i + 1) % 3][sl], ps4[2 + r % 2][:, 0:TCW], b3c,
                            CRP[:, (i % 2) * 1024 + tcc * TCW: (i % 2) * 1024 + (tcc + 1) * TCW],
                            ALU.add, ALU.mult)
                    else:
                        b3c = CPF[:, ci["b_bs3"] + (e - 16) * 4 + dog4:
                                  ci["b_bs3"] + (e - 16) * 4 + dog4 + 1]
                        ins = ve.tensor_scalar_add(HS[(i + 1) % 3][sl],
                                                   ps4[2 + r % 2][:, 0:TCW], b3c)
                    ctx.inc(ins, "bsd")
                if i >= 1:
                    emit_y_evacs(ve, i - 1)
                if i == nit - 1:
                    emit_y_evacs(ve, i)
            return f
        b.vector(mk_ve(i, e, hf, t0))

        def mk_po(i, e, hf):
            def f(po):
                for r in range(HR):
                    j = i * HR + r
                    dog4, tcc = divmod(r, 2)
                    po.wait_ge(ctx.sem("bsa"), B3["bsa"] + j + 1)
                    po.wait_ge(ctx.sem("bsd"), B3["bsd"] + j + 1)
                    sl = np.s_[:, dog4 * 1024 + tcc * TCW: dog4 * 1024 + (tcc + 1) * TCW]
                    sA = HS[i % 3]
                    sB = HS[(i + 1) % 3]
                    ctx.inc(po.tensor_mul(sA[sl], sA[sl], sB[sl]), "bht")
                if hf == 1 and e + 2 < 18:
                    po.wait_ge(ctx.sem("byp"), B3["byp"] + (i + 1) * YR)
                    wtoks[e + 2] = load_expert(po, e + 2, e % 2)
            return f
        b.gpsimd(mk_po(i, e, hf))

    barrier(ctx)
    if TAP == "ht0":
        emit_tap(ctx, HS[0][:, :], 4 * 1024)
    es_w.close()

    if upto < 4:
        return
    emit_ln(ctx, es, "lnb", ps4)
    barrier(ctx)




def host_prep_d(inp):
    w1 = np.asarray(inp["L1_ffn_w1"], np.float32)
    w2 = np.asarray(inp["L1_ffn_w2"], np.float32)
    w1a = np.zeros((2, D, 2816), np.float32)
    w1g = np.zeros((2, D, 2816), np.float32)
    w2p = np.zeros((2, 2816, D), np.float32)
    w1a[:, :, :2730] = w1[:, :, :2730]
    w1g[:, :, :2730] = w1[:, :, 2730:]
    w2p[:, :2730, :] = w2
    DH, WIN = 64, 512
    half = np.arange(0, DH, 2, dtype=np.float32)
    inv_freq = 1.0 / 10000.0 ** (half / DH)
    t_idx = np.arange(N, dtype=np.float32)
    freqs = np.concatenate([t_idx[:, None] * inv_freq[None, :]] * 2, -1)
    base = (half + 0.4 * DH) / (1.4 * DH)
    power = (t_idx - N // 2) / (WIN // 2)
    scale = np.concatenate([base[None, :] ** power[:, None]] * 2, -1)
    cosf, sinf = np.cos(freqs), np.sin(freqs)
    sgn = np.concatenate([-np.ones(32, np.float32), np.ones(32, np.float32)])
    def tm(a):
        return np.ascontiguousarray(
            a.reshape(NT, 128, 64).transpose(1, 0, 2).reshape(128, NT * 64), dtype=np.float32)
    return {
        "d_wqkv": np.asarray(inp["L1_ffn_wqkv"], np.float32),
        "d_wo": np.asarray(inp["L1_ffn_wo"], np.float32),
        "d_w1a": w1a, "d_w1g": w1g, "d_w2": w2p,
        "d_rqa": tm(cosf * scale), "d_rqb": tm(sinf * scale * sgn),
        "d_rka": tm(cosf / scale), "d_rkb": tm(sinf / scale * sgn),
    }


def decl_d(nc):
    return {
        "d_wqkv": nc.dram_tensor("d_wqkv", [2, D, 1536], F32, kind="ExternalInput").ap(),
        "d_wo": nc.dram_tensor("d_wo", [2, 512, D], F32, kind="ExternalInput").ap(),
        "d_w1a": nc.dram_tensor("d_w1a", [2, D, 2816], F32, kind="ExternalInput").ap(),
        "d_w1g": nc.dram_tensor("d_w1g", [2, D, 2816], F32, kind="ExternalInput").ap(),
        "d_w2": nc.dram_tensor("d_w2", [2, 2816, D], F32, kind="ExternalInput").ap(),
        "d_rqa": nc.dram_tensor("d_rqa", [128, NT * 64], F32, kind="ExternalInput").ap(),
        "d_rqb": nc.dram_tensor("d_rqb", [128, NT * 64], F32, kind="ExternalInput").ap(),
        "d_rka": nc.dram_tensor("d_rka", [128, NT * 64], F32, kind="ExternalInput").ap(),
        "d_rkb": nc.dram_tensor("d_rkb", [128, NT * 64], F32, kind="ExternalInput").ap(),
    }


def pack_d(cpf, cb, inp):
    qs = np.asarray(inp["L1_ffn_q_scale"], np.float32)
    ks = np.asarray(inp["L1_ffn_k_scale"], np.float32)
    def rowpack(v):
        r = np.zeros((128, v.size), np.float32)
        r[0, :] = v.reshape(-1)
        return r
    sw = np.concatenate([np.arange(32, 64), np.arange(0, 32)])
    cb.add("d_qsr", rowpack(np.concatenate([qs[0], qs[1]])))
    cb.add("d_qsw", rowpack(np.concatenate([qs[0][sw], qs[1][sw]])))
    cb.add("d_ksr", rowpack(np.concatenate([ks[0], ks[1]])))
    cb.add("d_ksw", rowpack(np.concatenate([ks[0][sw], ks[1][sw]])))
    lg = np.asarray(inp["L1_ffn_ln_g"], np.float32)
    lb = np.asarray(inp["L1_ffn_ln_b"], np.float32)
    cpf.add("d_lg", np.concatenate([col_pp(lg[0]), col_pp(lg[1])], 1))
    cpf.add("d_lb", np.concatenate([col_pp(lb[0]), col_pp(lb[1])], 1))
    b1 = np.asarray(inp["L1_ffn_b1"], np.float32)
    b1a = np.zeros((2, 2816), np.float32)
    b1g = np.zeros((2, 2816), np.float32)
    b1a[:, :2730] = b1[:, :2730]
    b1g[:, :2730] = b1[:, 2730:]
    cpf.add("d_b1a", np.concatenate([b1a[0].reshape(22, 128).T, b1a[1].reshape(22, 128).T], 1))
    cpf.add("d_b1g", np.concatenate([b1g[0].reshape(22, 128).T, b1g[1].reshape(22, 128).T], 1))
    b2 = np.asarray(inp["L1_ffn_b2"], np.float32)
    cpf.add("d_b2", np.concatenate([col_pp(b2[0]), col_pp(b2[1])], 1))
    m0 = np.where(np.arange(128)[:, None] >= np.arange(128)[None, :], 0.0, -1e30)
    m4 = np.where(np.arange(128)[:, None] <= np.arange(128)[None, :], 0.0, -1e30)
    cpf.add("d_m0", m0)
    cpf.add("d_m4", m4)


def emit_ln_src(ctx, es_or_bufs, tag, banks, hsrc, dst, lgc, lbc, eps=1e-5):
    nc, b = ctx.nc, ctx.block
    CB = ctx.t_["CB"]
    bi = ctx.t_["cb_idx"]
    onesb = CB[:, bi["ones"]:bi["ones"] + 1]
    ones_row_b = CB[0:1, bi["onesrow"]:bi["onesrow"] + 128]
    SB, AB, ABL, REP, T5 = es_or_bufs
    psS = banks[0][0:1, 0:TCW]
    psQ = banks[1][0:1, 0:TCW]
    psR = banks[2]
    psR2 = banks[3]
    L = {k: ctx.base(tag + k) for k in ["sq", "mm", "rm", "rs", "rd", "bc", "ap", "pb", "rr"]}

    for tc in range(NTC):
        par2 = tc % 2

        def mk_se(tc):
            def f(se):
                for p8 in range(2):
                    for g in range(NG):
                        j = tc * 16 + p8 * 8 + g
                        par = j % 2
                        if j >= 2:
                            se.wait_ge(ctx.sem(tag + "mm"), L["mm"] + j - 1)
                        src = hsrc(g, tc)
                        if p8 == 0:
                            ins = se.copy(SB[:, par * TCW:(par + 1) * TCW], src)
                        else:
                            ins = se.activation(SB[:, par * TCW:(par + 1) * TCW], src, AF.Square)
                        ctx.inc(ins, tag + "sq")
            return f
        b.scalar(mk_se(tc))

        def mk_te(tc):
            def f(te):
                for p8 in range(2):
                    dps = psQ if p8 else psS
                    for g in range(NG):
                        j = tc * 16 + p8 * 8 + g
                        par = j % 2
                        te.wait_ge(ctx.sem(tag + "sq"), L["sq"] + j + 1)
                        if p8 == 0 and g == 0 and tc >= 1:
                            te.wait_ge(ctx.sem(tag + "rd"), L["rd"] + tc)
                        ctx.inc(te.matmul(dps, onesb, SB[:, par * TCW:(par + 1) * TCW],
                                          start=(g == 0), stop=(g == NG - 1)), tag + "mm")
            return f
        b.tensor(mk_te(tc))

        A = AB[0:1, 0:TCW]
        B = AB[0:1, TCW:2 * TCW]
        AH = ABL[0:1, 0:TCW]
        AL = ABL[0:1, TCW:2 * TCW]
        BH = ABL[0:1, 2 * TCW:3 * TCW]
        BL = ABL[0:1, 3 * TCW:4 * TCW]

        def mk_rm(tc, A, B):
            def f(ve):
                ve.wait_ge(ctx.sem(tag + "mm"), L["mm"] + tc * 16 + 16)
                if tc >= 1:
                    ve.wait_ge(ctx.sem(tag + "pb"), L["pb"] + tc)
                ve.tensor_scalar_mul(B, psS, 1.0 / D)
                ve.tensor_scalar_mul(A, psQ, 1.0 / D)
                ve.tensor_mul(psS, B, B)
                ve.tensor_sub(A, A, psS)
                ctx.inc(ve.tensor_scalar_add(A, A, eps), tag + "rm")
            return f
        b.vector(mk_rm(tc, A, B))

        def mk_rs(tc, A):
            def f(se):
                se.wait_ge(ctx.sem(tag + "rm"), L["rm"] + tc + 1)
                ctx.inc(se.activation(A, A, AF.Sqrt), tag + "rs")
            return f
        b.scalar(mk_rs(tc, A))

        def mk_rd(tc, A, B, AH, AL, BH, BL):
            def f(ve):
                ve.wait_ge(ctx.sem(tag + "rs"), L["rs"] + tc + 1)
                ctx.inc(ve.reciprocal(A, A), tag + "rr")
                ve.wait_ge(*ctx.tok(tag + "rr"))
                ve.scalar_tensor_tensor(B, B, -1.0, A, ALU.mult, ALU.mult)
                ve.tensor_copy(AH, A)
                ve.tensor_sub(AL, A, AH)
                ve.tensor_copy(BH, B)
                ctx.inc(ve.tensor_sub(BL, B, BH), tag + "rd")
            return f
        b.vector(mk_rd(tc, A, B, AH, AL, BH, BL))

        def mk_pb(tc, AH, AL, BH, BL):
            def f(te):
                te.wait_ge(ctx.sem(tag + "rd"), L["rd"] + tc + 1)
                if tc >= 1:
                    te.wait_ge(ctx.sem(tag + "bc"), L["bc"] + tc)
                te.matmul(psR[:, 0:TCW], ones_row_b, AH, start=True, stop=False)
                te.matmul(psR[:, 0:TCW], ones_row_b, AL, start=False, stop=True)
                te.matmul(psR2[:, 0:TCW], ones_row_b, BH, start=True, stop=False)
                ctx.inc(te.matmul(psR2[:, 0:TCW], ones_row_b, BL, start=False, stop=True),
                        tag + "pb")
            return f
        b.tensor(mk_pb(tc, AH, AL, BH, BL))

        def mk_bc(tc):
            def f(se):
                se.wait_ge(ctx.sem(tag + "pb"), L["pb"] + tc + 1)
                if tc >= 1:
                    se.wait_ge(ctx.sem(tag + "ap"), L["ap"] + tc * NG)
                se.copy(REP[:, 0:TCW], psR[:, 0:TCW])
                ctx.inc(se.copy(REP[:, TCW:2 * TCW], psR2[:, 0:TCW]), tag + "bc")
            return f
        b.scalar(mk_bc(tc))

        def mk_ap(tc):
            def f(ve):
                ve.wait_ge(ctx.sem(tag + "bc"), L["bc"] + tc + 1)
                for g in range(NG):
                    src = hsrc(g, tc)
                    ve.tensor_mul(T5[:, 0:TCW], src, REP[:, 0:TCW])
                    ctx.inc(ve.tensor_add(T5[:, 0:TCW], T5[:, 0:TCW], REP[:, TCW:2 * TCW]),
                            tag + "rr")
                    ve.wait_ge(*ctx.tok(tag + "rr"))
                    ctx.inc(ve.tensor_scalar(dst[:, g * N + tc * TCW: g * N + (tc + 1) * TCW],
                                             T5[:, 0:TCW], lgc(g), lbc(g),
                                             ALU.mult, ALU.add), tag + "ap")
            return f
        b.vector(mk_ap(tc))




def emit_phase_d(ctx, aps, upto=99):
    nc, b = ctx.nc, ctx.block
    es = ctx.es_phase
    XF, XB = ctx.t_["XF"], ctx.t_["XB"]
    CPF, CB = ctx.t_["CPF"], ctx.t_["CB"]
    ci, bi = ctx.t_["cpf_idx"], ctx.t_["cb_idx"]
    idn = CB[:, bi["idn"]:bi["idn"] + 128]
    onesb = CB[:, bi["ones"]:bi["ones"] + 1]
    ones_row_b = CB[0:1, bi["onesrow"]:bi["onesrow"] + 128]

    HBSm = es.enter_context(nc.sbuf_tensor("d_HBSm", [128, 4 * TCW], BF16))
    XOR_T = es.enter_context(nc.sbuf_tensor("d_XORT", [128, TCW], F32))
    ps = [es.enter_context(nc.psum_tensor(f"d_ps{i}", [128, TCW], F32)) for i in range(4)]
    psY = [es.enter_context(nc.psum_tensor(f"d_psy{i}", [128, TCW], F32)) for i in range(2)]
    xorig = nc.dram_tensor("d_xorig", [128, NG * N], F32).ap()

    @b.sync
    def _(sy):
        for g in range(NG):
            ctx.inc(sy.dma_start(out=xorig[:, g * N:(g + 1) * N],
                                 in_=XF[:, g * N:(g + 1) * N]), "dxo", 16)
    xotok = ctx.tok("dxo")
    barrier(ctx)

    def hsl(kg, t0, w):
        if kg < 4:
            return XB[:, kg * N + t0: kg * N + t0 + w]
        if t0 > 0:
            return XB[:, kg * N + t0 - 1: kg * N + t0 - 1 + w]
        return HBSm[:, (kg - 4) * TCW: (kg - 4) * TCW + w]

    for l in range(2):
        sfx = "LD"
        DD = {k: ctx.base(sfx + k) for k in
              ["qs", "q2", "sh", "qk", "qe", "sq", "sr", "nr", "rr", "tp", "tf",
               "sc", "pxm", "pxs", "pm", "px", "pe", "pt", "pv", "ov",
               "wo", "we", "xb", "s2", "aa", "ae", "gg", "ge", "tt",
               "yp", "ye", "x2"]}

        es_at = contextlib.ExitStack()
        es.callback(es_at.close)
        VTok = es_at.enter_context(nc.sbuf_tensor(f"d_VTok_{l}", [128, NT * TCW], BF16))
        QT2 = XB[:, 0:4 * N]
        KT2 = XB[:, 4 * N:8 * N]
        PB1 = es_at.enter_context(nc.sbuf_tensor(f"d_PB1_{l}", [128, 2 * 640], BF16))
        PT1 = es_at.enter_context(nc.sbuf_tensor(f"d_PT1_{l}", [128, 2 * 640], BF16))
        RMX = es_at.enter_context(nc.sbuf_tensor(f"d_RMX_{l}", [128, 16], F32))
        QSR = es_at.enter_context(nc.sbuf_tensor(f"d_QSR_{l}", [128, 256], BF16))
        psTb = [es_at.enter_context(nc.psum_tensor(f"d_pst{i}_{l}", [128, 1024], BF16))
                for i in range(2)]
        es_qk = contextlib.ExitStack()
        es_at.callback(es_qk.close)
        WQKV = es_qk.enter_context(nc.sbuf_tensor(f"d_WQKV_{l}", [128, NG * 1536], BF16))
        QTok = es_qk.enter_context(nc.sbuf_tensor(f"d_QTok_{l}", [128, NT * TCW], BF16))
        KTok = es_qk.enter_context(nc.sbuf_tensor(f"d_KTok_{l}", [128, NT * TCW], BF16))
        NSC = es_qk.enter_context(nc.sbuf_tensor(f"d_NSC_{l}", [128, 4 * TCW], BF16))
        NRM = es_qk.enter_context(nc.sbuf_tensor(f"d_NRM_{l}", [128, 32], F32))
        RQA = es_qk.enter_context(nc.sbuf_tensor(f"d_RQA_{l}", [128, NT * 64], BF16))
        RQB = es_qk.enter_context(nc.sbuf_tensor(f"d_RQB_{l}", [128, NT * 64], BF16))
        RKA = es_qk.enter_context(nc.sbuf_tensor(f"d_RKA_{l}", [128, NT * 64], BF16))
        RKB = es_qk.enter_context(nc.sbuf_tensor(f"d_RKB_{l}", [128, NT * 64], BF16))

        @b.gpsimd
        def _(po):
            for nm, t in [("d_rqa", RQA), ("d_rqb", RQB), ("d_rka", RKA), ("d_rkb", RKB)]:
                ctx.inc(po.dma_start(out=t[:, :], in_=aps[nm][:, :]), "dmw", 16)
            for kg in range(NG):
                ctx.inc(po.dma_start(out=WQKV[:, kg * 1536:(kg + 1) * 1536],
                                     in_=aps["d_wqkv"][l, kg * 128:(kg + 1) * 128, :]), "dmw", 16)
        qkvtok = ctx.tok("dmw")
        barrier(ctx)

        def mk_qs(l):
            def f(te):
                te.wait_ge(*qkvtok)
                lq = bi["d_qsr"] + l * 64
                lw = bi["d_qsw"] + l * 64
                lk = bi["d_ksr"] + l * 64
                lx = bi["d_ksw"] + l * 64
                te.matmul(ps[0][:, 0:64], ones_row_b, CB[0:1, lq:lq + 64],
                          start=True, stop=True)
                te.matmul(ps[0][:, 64:128], ones_row_b, CB[0:1, lw:lw + 64],
                          start=False, stop=True, skip_group_check=True)
                te.matmul(ps[0][:, 128:192], ones_row_b, CB[0:1, lk:lk + 64],
                          start=False, stop=True, skip_group_check=True)
                ctx.inc(te.matmul(ps[0][:, 192:256], ones_row_b, CB[0:1, lx:lx + 64],
                                  start=False, stop=True, skip_group_check=True), sfx + "qs")
            return f
        b.tensor(mk_qs(l))

        def mk_qs2(l):
            def f(se):
                se.wait_ge(ctx.sem(sfx + "qs"), DD["qs"] + 1)
                ctx.inc(se.copy(QSR[:, 0:256], ps[0][:, 0:256]), sfx + "q2")
            return f
        b.scalar(mk_qs2(l))

        def mk_qs3(l):
            def f(ve):
                ve.wait_ge(ctx.sem(sfx + "q2"), DD["q2"] + 1)
                ins = None
                for t, c0 in [(RQA, 0), (RQB, 64), (RKA, 128), (RKB, 192)]:
                    rv = t[:, :].rearrange("p (t d) -> p t d", d=64)
                    sv = QSR[:, c0:c0 + 64].unsqueeze(1).broadcast_to((128, NT, 64))
                    ins = ve.tensor_tensor(rv, rv, sv, ALU.mult)
                for g in range(4):
                    ve.memset(HBSm[:, g * TCW: g * TCW + 1], 0.0)
                    ins = ve.tensor_copy(HBSm[:, g * TCW + 1:(g + 1) * TCW],
                                         XB[:, (g + 4) * N:(g + 4) * N + TCW - 1])
                ctx.inc(ins, sfx + "sh")
            return f
        b.vector(mk_qs3(l))
        barrier(ctx)

        def mk_qkv(l):
            def f(te):
                for r in range(48):
                    m, t = divmod(r, 16)
                    if r >= 3:
                        te.wait_ge(ctx.sem(sfx + "qe"), DD["qe"] + r - 2)
                    for kg in range(NG):
                        ins = te.matmul(ps[r % 3][:, 0:TCW],
                                        hsl(kg, t * 128, 128),
                                        WQKV[:, kg * 1536 + m * 512: kg * 1536 + (m + 1) * 512],
                                        start=(kg == 0), stop=(kg == NG - 1))
                    ctx.inc(ins, sfx + "qk")
            return f
        b.tensor(mk_qkv(l))

        def mk_qkve(l):
            def f(ve):
                for r in range(48):
                    m, t = divmod(r, 16)
                    ve.wait_ge(ctx.sem(sfx + "qk"), DD["qk"] + r + 1)
                    dst = [QTok, KTok, VTok][m]
                    ctx.inc(ve.tensor_copy(dst[:, t * TCW:(t + 1) * TCW], ps[r % 3][:, 0:TCW]),
                            sfx + "qe")
            return f
        b.vector(mk_qkve(l))
        barrier(ctx)

        def mk_sq(l):
            def f(se):
                for j in range(32):
                    t, qi = divmod(j, 2)
                    tok = [QTok, KTok][qi]
                    if j >= 2:
                        se.wait_ge(ctx.sem(sfx + "nr"), DD["nr"] + j - 1)
                    ctx.inc(se.activation(NSC[:, (j % 2) * TCW:((j % 2) + 1) * TCW],
                                          tok[:, t * TCW:(t + 1) * TCW], AF.Square), sfx + "sq")
                    se.wait_ge(ctx.sem(sfx + "rr"), DD["rr"] + j * 5 + 2)
                    ctx.inc(se.activation(NRM[:, 16:24], NRM[:, 8:16], AF.Sqrt), sfx + "sr")
            return f
        b.scalar(mk_sq(l))

        def mk_nr(l):
            def f(ve):
                for j in range(32):
                    t, qi = divmod(j, 2)
                    tok = [QTok, KTok][qi]
                    RA = [RQA, RKA][qi]
                    RB = [RQB, RKB][qi]
                    tile = tok[:, t * TCW:(t + 1) * TCW]
                    tv = tile.rearrange("p (h d) -> p h d", d=64)
                    ve.wait_ge(ctx.sem(sfx + "sq"), DD["sq"] + j + 1)
                    sqv = NSC[:, (j % 2) * TCW:((j % 2) + 1) * TCW].rearrange(
                        "p (h d) -> p h d", d=64)
                    ctx.inc(ve.tensor_reduce(NRM[:, 0:8], sqv, AX.X, ALU.add), sfx + "rr")
                    ve.wait_ge(*ctx.tok(sfx + "rr"))
                    ctx.inc(ve.tensor_scalar_max(NRM[:, 8:16], NRM[:, 0:8], 1e-24), sfx + "rr")
                    ve.wait_ge(ctx.sem(sfx + "sr"), DD["sr"] + j + 1)
                    ctx.inc(ve.reciprocal(NRM[:, 24:32], NRM[:, 16:24]), sfx + "rr")
                    ve.wait_ge(*ctx.tok(sfx + "rr"))
                    TN = NSC[:, 2 * TCW:3 * TCW]
                    tnv = TN.rearrange("p (h d) -> p h d", d=64)
                    ctx.inc(ve.tensor_tensor(tnv, tv, bcast8(NRM[:, 24:32], 64), ALU.mult),
                            sfx + "rr")
                    ve.wait_ge(*ctx.tok(sfx + "rr"))
                    T2 = NSC[:, 3 * TCW:4 * TCW]
                    t2v = T2.rearrange("p (h d) -> p h d", d=64)
                    ra = RA[:, t * 64:(t + 1) * 64].unsqueeze(1).broadcast_to((128, 8, 64))
                    ve.tensor_tensor(t2v, tnv, ra, ALU.mult)
                    rblo = RB[:, t * 64:t * 64 + 32].unsqueeze(1).broadcast_to((128, 8, 32))
                    rbhi = RB[:, t * 64 + 32:(t + 1) * 64].unsqueeze(1).broadcast_to((128, 8, 32))
                    ve.tensor_tensor(tv[:, :, 0:32], tnv[:, :, 32:64], rblo, ALU.mult)
                    ctx.inc(ve.tensor_tensor(tv[:, :, 32:64], tnv[:, :, 0:32], rbhi, ALU.mult),
                            sfx + "rr")
                    ve.wait_ge(*ctx.tok(sfx + "rr"))
                    ctx.inc(ve.tensor_add(tile, tile, T2), sfx + "nr")
            return f
        b.vector(mk_nr(l))
        barrier(ctx)

        def mk_tp(l):
            def f(te):
                for j in range(128):
                    t, og = divmod(j, 8)
                    qi, og4 = divmod(og, 4)
                    tok = [QTok, KTok][qi]
                    if j >= 2:
                        te.wait_ge(ctx.sem(sfx + "tf"), DD["tf"] + j - 1)
                    ctx.inc(te.transpose(psTb[j % 2][:, 0:128],
                                         tok[:, t * TCW + og4 * 128: t * TCW + (og4 + 1) * 128],
                                         idn), sfx + "tp")
            return f
        b.tensor(mk_tp(l))

        def mk_tf(l):
            def f(se):
                for j in range(128):
                    t, og = divmod(j, 8)
                    qi, og4 = divmod(og, 4)
                    dst = [QT2, KT2][qi]
                    se.wait_ge(ctx.sem(sfx + "tp"), DD["tp"] + j + 1)
                    ctx.inc(se.copy(dst[:, og4 * N + t * 128: og4 * N + (t + 1) * 128],
                                    psTb[j % 2][:, 0:128]), sfx + "tf")
            return f
        b.scalar(mk_tf(l))
        barrier(ctx)
        es_qk.close()
        ODT = es_at.enter_context(nc.sbuf_tensor(f"d_ODT_{l}", [128, 4 * N], BF16))
        WO2 = es_at.enter_context(nc.sbuf_tensor(f"d_WO2_{l}", [128, 4 * D], BF16))

        @b.gpsimd
        def _(po):
            for kg in range(4):
                ctx.inc(po.dma_start(out=WO2[:, kg * D:(kg + 1) * D],
                                     in_=aps["d_wo"][l, kg * 128:(kg + 1) * 128, :]), "dmw", 16)
        wotok = ctx.tok("dmw")

        items5 = [(h, qc) for h in range(8) for qc in range(NT)]
        m0c = ci["d_m0"]
        m4c = ci["d_m4"]

        def mk_at_pe(l):
            def f(te):
                for i5, (h, qc) in enumerate(items5):
                    par = i5 % 2
                    r0 = (h % 2) * 64
                    g4 = h // 2
                    k0 = max(0, qc - 4)
                    nk = qc - k0 + 1
                    bX = ps[0] if h % 2 == 0 else ps[1]
                    bY = ps[2] if h % 2 == 0 else ps[3]
                    if i5 >= 2:
                        te.wait_ge(ctx.sem(sfx + "px"), DD["px"] + i5 - 1)
                    for ix, kc in enumerate(range(k0, qc + 1)):
                        dstp = bX[:, ix * 128:(ix + 1) * 128] if ix < 4 else bY[:, 0:128]
                        ins = te.matmul(dstp,
                                        QT2[r0:r0 + 64, g4 * N + qc * 128: g4 * N + (qc + 1) * 128],
                                        KT2[r0:r0 + 64, g4 * N + kc * 128: g4 * N + (kc + 1) * 128],
                                        start=True, stop=True, skip_group_check=(ix > 0))
                    ctx.inc(ins, sfx + "sc")
                    te.wait_ge(ctx.sem(sfx + "pm"), DD["pm"] + i5 + 1)
                    if i5 >= 2:
                        te.wait_ge(ctx.sem(sfx + "pt"), DD["pt"] + i5 - 1)
                    for ix in range(nk):
                        ins = te.transpose(psTb[par][:, ix * 128:(ix + 1) * 128],
                                           PB1[:, par * 640 + ix * 128: par * 640 + (ix + 1) * 128],
                                           idn)
                    ctx.inc(ins, sfx + "pe")
                    te.wait_ge(ctx.sem(sfx + "pt"), DD["pt"] + i5 + 1)
                    if i5 >= 2:
                        te.wait_ge(ctx.sem(sfx + "ov"), DD["ov"] + i5 - 1)
                    pO = psY[h % 2]
                    for ix, kc in enumerate(range(k0, qc + 1)):
                        ins = te.matmul(pO[0:64, 0:128],
                                        VTok[:, kc * TCW + h * 64: kc * TCW + (h + 1) * 64],
                                        PT1[:, par * 640 + ix * 128: par * 640 + (ix + 1) * 128],
                                        start=(ix == 0), stop=(ix == nk - 1))
                    ctx.inc(ins, sfx + "pv")
            return f
        b.tensor(mk_at_pe(l))

        def mk_at_ve(l):
            def f(ve):
                for i5, (h, qc) in enumerate(items5):
                    par = i5 % 2
                    k0 = max(0, qc - 4)
                    nk = qc - k0 + 1
                    nk1 = min(nk, 4)
                    bX = ps[0] if h % 2 == 0 else ps[1]
                    bY = ps[2] if h % 2 == 0 else ps[3]
                    ve.wait_ge(ctx.sem(sfx + "sc"), DD["sc"] + i5 + 1)
                    ixq = qc - k0
                    dq = bX[:, ixq * 128:(ixq + 1) * 128] if ixq < 4 else bY[:, 0:128]
                    ctx.inc(ve.tensor_add(dq, dq, CPF[:, m0c:m0c + 128]), sfx + "rr")
                    if qc >= 4:
                        ctx.inc(ve.tensor_add(bX[:, 0:128], bX[:, 0:128],
                                              CPF[:, m4c:m4c + 128]), sfx + "rr")
                    ve.wait_ge(*ctx.tok(sfx + "rr"))
                    ctx.inc(ve.reduce_max(RMX[:, 0:1], bX[:, 0:nk1 * 128], AX.X, negate=True),
                            sfx + "rr")
                    ve.wait_ge(*ctx.tok(sfx + "rr"))
                    if nk == 5:
                        ctx.inc(ve.reduce_max(RMX[:, 1:2], bY[:, 0:128], AX.X, negate=True),
                                sfx + "rr")
                        ve.wait_ge(*ctx.tok(sfx + "rr"))
                        ctx.inc(ve.tensor_tensor(RMX[:, 0:1], RMX[:, 0:1], RMX[:, 1:2],
                                                 ALU.min), sfx + "rr")
                        ve.wait_ge(*ctx.tok(sfx + "rr"))
                    ctx.inc(ve.tensor_scalar_mul(RMX[:, 2:3], RMX[:, 0:1], 8.0), sfx + "pxm")
                    ve.wait_ge(ctx.sem(sfx + "pxs"), DD["pxs"] + i5 + 1)
                    if nk == 5:
                        ctx.inc(ve.tensor_add(RMX[:, 4:5], RMX[:, 4:5], RMX[:, 5:6]), sfx + "rr")
                        ve.wait_ge(*ctx.tok(sfx + "rr"))
                    ctx.inc(ve.reciprocal(RMX[:, 6:7], RMX[:, 4:5]), sfx + "rr")
                    ve.wait_ge(*ctx.tok(sfx + "rr"))
                    ctx.inc(ve.tensor_scalar_mul(PB1[:, par * 640: par * 640 + nk * 128],
                                                 PB1[:, par * 640: par * 640 + nk * 128],
                                                 RMX[:, 6:7]), sfx + "pm")
                    ctx.inc(ve.tensor_copy(RMX[:, 7:8], RMX[:, 6:7]), sfx + "px")
            return f
        b.vector(mk_at_ve(l))

        def mk_at_se(l):
            def f(se):
                for i5, (h, qc) in enumerate(items5):
                    par = i5 % 2
                    k0 = max(0, qc - 4)
                    nk = qc - k0 + 1
                    nk1 = min(nk, 4)
                    bX = ps[0] if h % 2 == 0 else ps[1]
                    bY = ps[2] if h % 2 == 0 else ps[3]
                    se.wait_ge(ctx.sem(sfx + "pxm"), DD["pxm"] + i5 + 1)
                    se.activation(PB1[:, par * 640: par * 640 + nk1 * 128],
                                  bX[:, 0:nk1 * 128], AF.Exp, bias=RMX[:, 2:3],
                                  scale=8.0, accum_out=RMX[:, 4:5])
                    if nk == 5:
                        ins = se.activation(PB1[:, par * 640 + 512: par * 640 + 640],
                                            bY[:, 0:128], AF.Exp, bias=RMX[:, 2:3],
                                            scale=8.0, accum_out=RMX[:, 5:6])
                    else:
                        ins = se.copy(RMX[0:1, 5:6], RMX[0:1, 5:6])
                    ctx.inc(ins, sfx + "pxs")
                    se.wait_ge(ctx.sem(sfx + "pe"), DD["pe"] + i5 + 1)
                    ctx.inc(se.copy(PT1[:, par * 640: par * 640 + nk * 128],
                                    psTb[par][:, 0:nk * 128]), sfx + "pt")
                    se.wait_ge(ctx.sem(sfx + "pv"), DD["pv"] + i5 + 1)
                    r0 = (h % 2) * 64
                    g4 = h // 2
                    ctx.inc(se.copy(ODT[r0:r0 + 64, g4 * N + qc * 128: g4 * N + (qc + 1) * 128],
                                    psY[h % 2][0:64, 0:128]), sfx + "ov")
            return f
        b.scalar(mk_at_se(l))
        barrier(ctx)
        if TAP == "dod" and l == 0:
            emit_tap(ctx, ODT[:, :], 4 * N)

        def mk_wo(l):
            def f(te):
                te.wait_ge(*wotok)
                for r in range(32):
                    dog, tc = divmod(r, 4)
                    if r >= 3:
                        te.wait_ge(ctx.sem(sfx + "we"), DD["we"] + r - 2)
                    for kg in range(4):
                        ins = te.matmul(ps[r % 3][:, 0:TCW],
                                        WO2[:, kg * D + dog * 128: kg * D + (dog + 1) * 128],
                                        ODT[:, kg * N + tc * TCW: kg * N + (tc + 1) * TCW],
                                        start=(kg == 0), stop=(kg == 3))
                    ctx.inc(ins, sfx + "wo")
            return f
        b.tensor(mk_wo(l))

        def mk_we(l):
            def f(ve):
                for r in range(32):
                    dog, tc = divmod(r, 4)
                    ve.wait_ge(ctx.sem(sfx + "wo"), DD["wo"] + r + 1)
                    sl = np.s_[:, dog * N + tc * TCW: dog * N + (tc + 1) * TCW]
                    ctx.inc(ve.tensor_add(XF[sl], XF[sl], ps[r % 3][:, 0:TCW]), sfx + "we")
            return f
        b.vector(mk_we(l))

        def mk_xb(l):
            def f(se):
                for r in range(32):
                    dog, tc = divmod(r, 4)
                    se.wait_ge(ctx.sem(sfx + "we"), DD["we"] + r + 1)
                    sl = np.s_[:, dog * N + tc * TCW: dog * N + (tc + 1) * TCW]
                    ctx.inc(se.copy(XB[sl], XF[sl]), sfx + "xb")
            return f
        b.scalar(mk_xb(l))
        barrier(ctx)
        es_at.close()

        es_ff = contextlib.ExitStack()
        es.callback(es_ff.close)
        HLN = es_ff.enter_context(nc.sbuf_tensor(f"d_HLN_{l}", [128, NG * N], BF16))
        AAT = es_ff.enter_context(nc.sbuf_tensor(f"d_AAT_{l}", [128, 22 * TCW], BF16))
        GGT = es_ff.enter_context(nc.sbuf_tensor(f"d_GGT_{l}", [128, 22 * TCW], BF16))
        W1S = [es_ff.enter_context(nc.sbuf_tensor(f"d_W1S{i}_{l}", [128, 2 * 1024], BF16))
               for i in range(2)]
        W2S = [es_ff.enter_context(nc.sbuf_tensor(f"d_W2S{i}_{l}", [128, D], BF16))
               for i in range(2)]
        LSB = es_ff.enter_context(nc.sbuf_tensor(f"d_LSB_{l}", [128, 2 * TCW], BF16))
        LAB = es_ff.enter_context(nc.sbuf_tensor(f"d_LAB_{l}", [1, 2 * TCW], F32))
        LABL = es_ff.enter_context(nc.sbuf_tensor(f"d_LABL_{l}", [1, 4 * TCW], BF16))
        LREP = es_ff.enter_context(nc.sbuf_tensor(f"d_LREP_{l}", [128, 2 * TCW], F32))

        psW = [es_ff.enter_context(nc.psum_tensor(f"d_psw{i}_{l}", [128, TCW], F32))
               for i in range(2)]
        lnbufs = (LSB, LAB, LABL, LREP, XOR_T)

        def mk_sh2(l):
            def f(ve):
                ins = None
                for g in range(4):
                    ve.memset(HBSm[:, g * TCW: g * TCW + 1], 0.0)
                    ins = ve.tensor_copy(HBSm[:, g * TCW + 1:(g + 1) * TCW],
                                         XB[:, (g + 4) * N:(g + 4) * N + TCW - 1])
                ctx.inc(ins, sfx + "s2")
            return f
        b.vector(mk_sh2(l))
        barrier(ctx)

        lgc0 = ci["d_lg"] + l * 8
        lbc0 = ci["d_lb"] + l * 8
        emit_ln_src(ctx, lnbufs, sfx + "ln", ps,
                    lambda g, tc: hsl(g, tc * TCW, TCW),
                    HLN,
                    lambda g: CPF[:, lgc0 + g: lgc0 + g + 1],
                    lambda g: CPF[:, lbc0 + g: lbc0 + g + 1])
        barrier(ctx)

        b1a0 = ci["d_b1a"] + l * 22
        b1g0 = ci["d_b1g"] + l * 22
        b20 = ci["d_b2"] + l * 8
        wl_toks = {}
        bank8 = ps + psY + psW

        for tc in range(NTC):
            t0 = tc * TCW

            def mk_w1po(l, tc):
                def f(po):
                    for og in range(22):
                        j = tc * 22 + og
                        if j >= 2:
                            po.wait_ge(ctx.sem(sfx + "gg"), DD["gg"] + j - 1)
                        for kg in range(NG):
                            ctx.inc(po.dma_start(
                                out=W1S[og % 2][:, kg * 128:(kg + 1) * 128],
                                in_=aps["d_w1a"][l, kg * 128:(kg + 1) * 128,
                                                 og * 128:(og + 1) * 128]), "dmw", 16)
                        for kg in range(NG):
                            ctx.inc(po.dma_start(
                                out=W1S[og % 2][:, 1024 + kg * 128: 1024 + (kg + 1) * 128],
                                in_=aps["d_w1g"][l, kg * 128:(kg + 1) * 128,
                                                 og * 128:(og + 1) * 128]), "dmw", 16)
                        wl_toks[(tc, og)] = ctx.tok("dmw")
                return f
            b.gpsimd(mk_w1po(l, tc))

            def mk_ag(l, tc, t0):
                def f(te):
                    for og in range(22):
                        j = tc * 22 + og
                        te.wait_ge(*wl_toks[(tc, og)])
                        if og == 0 and tc >= 1:
                            te.wait_ge(ctx.sem(sfx + "ye"), DD["ye"] + tc * 8)
                        if j >= 2:
                            te.wait_ge(ctx.sem(sfx + "ae"), DD["ae"] + j - 1)
                            te.wait_ge(ctx.sem(sfx + "ge"), DD["ge"] + j - 1)
                        for kg in range(NG):
                            ins = te.matmul(ps[og % 2][:, 0:TCW],
                                            W1S[og % 2][:, kg * 128:(kg + 1) * 128],
                                            HLN[:, kg * N + t0: kg * N + t0 + TCW],
                                            start=(kg == 0), stop=(kg == NG - 1))
                        ctx.inc(ins, sfx + "aa")
                        for kg in range(NG):
                            ins = te.matmul(ps[2 + og % 2][:, 0:TCW],
                                            W1S[og % 2][:, 1024 + kg * 128: 1024 + (kg + 1) * 128],
                                            HLN[:, kg * N + t0: kg * N + t0 + TCW],
                                            start=(kg == 0), stop=(kg == NG - 1))
                        ctx.inc(ins, sfx + "gg")
                return f
            b.tensor(mk_ag(l, tc, t0))

            def mk_age(l, tc):
                def f(se):
                    for og in range(22):
                        j = tc * 22 + og
                        se.wait_ge(ctx.sem(sfx + "gg"), DD["gg"] + j + 1)
                        if og == 0 and tc >= 1:
                            se.wait_ge(ctx.sem(sfx + "yp"), DD["yp"] + tc * 22)
                        bcg = CPF[:, b1g0 + og: b1g0 + og + 1]
                        ctx.inc(se.activation(GGT[:, og * TCW:(og + 1) * TCW],
                                              ps[2 + og % 2][:, 0:TCW], AF.Gelu, bias=bcg),
                                sfx + "ge")
                return f
            b.scalar(mk_age(l, tc))

            def mk_aev(l, tc):
                def f(ve):
                    for og in range(22):
                        j = tc * 22 + og
                        ve.wait_ge(ctx.sem(sfx + "aa"), DD["aa"] + j + 1)
                        if og == 0 and tc >= 1:
                            ve.wait_ge(ctx.sem(sfx + "yp"), DD["yp"] + tc * 22)
                        bca = CPF[:, b1a0 + og: b1a0 + og + 1]
                        ctx.inc(ve.tensor_scalar_add(AAT[:, og * TCW:(og + 1) * TCW],
                                                     ps[og % 2][:, 0:TCW], bca), sfx + "ae")
                return f
            b.vector(mk_aev(l, tc))

            def mk_tt(l, tc):
                def f(po):
                    for og in range(22):
                        j = tc * 22 + og
                        po.wait_ge(ctx.sem(sfx + "ae"), DD["ae"] + j + 1)
                        po.wait_ge(ctx.sem(sfx + "ge"), DD["ge"] + j + 1)
                        ctx.inc(po.tensor_mul(AAT[:, og * TCW:(og + 1) * TCW],
                                              AAT[:, og * TCW:(og + 1) * TCW],
                                              GGT[:, og * TCW:(og + 1) * TCW]), sfx + "tt")
                return f
            b.gpsimd(mk_tt(l, tc))

            def mk_w2po(l, tc):
                def f(po):
                    for kg in range(22):
                        jk = tc * 22 + kg
                        if jk >= 2:
                            po.wait_ge(ctx.sem(sfx + "yp"), DD["yp"] + jk - 1)
                        ctx.inc(po.dma_start(out=W2S[kg % 2][:, 0:D],
                                             in_=aps["d_w2"][l, kg * 128:(kg + 1) * 128, :]),
                                "dmw", 16)
                        wl_toks[("w2", tc, kg)] = ctx.tok("dmw")
                return f
            b.gpsimd(mk_w2po(l, tc))

            def mk_w2(l, tc, t0):
                def f(te):
                    for kg in range(22):
                        te.wait_ge(*wl_toks[("w2", tc, kg)])
                        if kg == 0:
                            te.wait_ge(ctx.sem(sfx + "tt"), DD["tt"] + (tc + 1) * 22)
                            if tc >= 1:
                                te.wait_ge(ctx.sem(sfx + "ye"), DD["ye"] + tc * 8)
                        for dog in range(NG):
                            ins = te.matmul(bank8[dog][:, 0:TCW],
                                            W2S[kg % 2][:, dog * 128:(dog + 1) * 128],
                                            AAT[:, kg * TCW:(kg + 1) * TCW],
                                            start=(kg == 0), stop=(kg == 21))
                        ctx.inc(ins, sfx + "yp")
                return f
            b.tensor(mk_w2(l, tc, t0))

            def mk_ye(l, tc, t0):
                def f(ve):
                    ve.wait_ge(ctx.sem(sfx + "yp"), DD["yp"] + (tc + 1) * 22)
                    for dog in range(NG):
                        sl = np.s_[:, dog * N + t0: dog * N + t0 + TCW]
                        ctx.inc(ve.scalar_tensor_tensor(XF[sl], bank8[dog][:, 0:TCW],
                                                        CPF[:, b20 + dog: b20 + dog + 1],
                                                        XF[sl], ALU.add, ALU.add), sfx + "ye")
                return f
            b.vector(mk_ye(l, tc, t0))
        barrier(ctx)
        es_ff.close()

        def mk_xb2(l):
            def f(se):
                for r in range(32):
                    dog, tc = divmod(r, 4)
                    sl = np.s_[:, dog * N + tc * TCW: dog * N + (tc + 1) * TCW]
                    ctx.inc(se.copy(XB[sl], XF[sl]), sfx + "x2")
            return f
        b.scalar(mk_xb2(l))
        barrier(ctx)
        if TAP == "dl0" and l == 0:
            emit_tap(ctx, XB[:, :], NG * N)

    if upto < 9:
        return
    b_xr = ctx.base("dxr")
    b_xa = ctx.base("dxa")

    @b.gpsimd
    def _(po):
        po.wait_ge(*xotok)
        for i in range(32):
            g, tc = divmod(i, NTC)
            if i >= 1:
                po.wait_ge(ctx.sem("dxa"), b_xa + i)
            ctx.inc(po.dma_start(out=XOR_T[:, 0:TCW],
                                 in_=xorig[:, g * N + tc * TCW: g * N + (tc + 1) * TCW]),
                    "dxr", 16)

    @b.vector
    def _(ve):
        for i in range(32):
            g, tc = divmod(i, NTC)
            ve.wait_ge(ctx.sem("dxr"), b_xr + (i + 1) * 16)
            sl = np.s_[:, g * N + tc * TCW: g * N + (tc + 1) * TCW]
            ctx.inc(ve.tensor_add(XF[sl], XF[sl], XOR_T[:, 0:TCW]), "dxa")
    barrier(ctx)
    emit_ln(ctx, es, "lnd", ps)
    barrier(ctx)



UPTO = 99
TAP = None


def build_program(phases, inp, debug_taps=None):
    nc = bass.Bass("TRN2", target_bir_lowering=False)
    hbm = {}

    xT = nc.dram_tensor("xT", [D, N], F32, kind="ExternalInput").ap()
    xoT = nc.dram_tensor("xoT", [D, N], F32, kind="ExternalOutput").ap()
    dbg = nc.dram_tensor("dbg", [128, NG * N], F32, kind="ExternalOutput").ap()

    cpf, cb = base_packs()
    aps = {}
    for ph in phases:
        if ph == "c":
            aps.update(decl_c(nc))
            pack_c(cpf, cb, inp)
            hbm.update(host_prep_c(inp))
        elif ph == "b":
            aps.update(decl_b(nc))
            pack_b(cpf, cb, inp)
            hbm.update(host_prep_b(inp))
        elif ph == "a":
            aps.update(decl_a(nc))
            pack_a(cpf, cb, inp)
            hbm.update(host_prep_a(inp))
        elif ph == "d":
            aps.update(decl_d(nc))
            pack_d(cpf, cb, inp)
            hbm.update(host_prep_d(inp))
    CPF_arr = cpf.array()
    CB_arr = cb.array(bf16=True)
    cpf_t = nc.dram_tensor("cpf", list(CPF_arr.shape), F32, kind="ExternalInput").ap()
    cb_t = nc.dram_tensor("cb", list(CB_arr.shape), BF16, kind="ExternalInput").ap()
    hbm["cpf"] = CPF_arr
    hbm["cb"] = CB_arr

    es = contextlib.ExitStack()
    with es:
        XF = es.enter_context(nc.sbuf_tensor("XF", [128, NG * N], F32))
        XB = es.enter_context(nc.sbuf_tensor("XB", [128, NG * N], BF16))
        CPF = es.enter_context(nc.sbuf_tensor("CPF", [128, CPF_arr.shape[1]], F32))
        CB = es.enter_context(nc.sbuf_tensor("CB", [128, CB_arr.shape[1]], BF16))
        JUNK = es.enter_context(nc.sbuf_tensor("junk", [1, 8], F32))
        with nc.Block() as block:
            ctx = Ctx(nc, block, es)
            ctx.t_.update(XF=XF, XB=XB, CPF=CPF, CB=CB, junk=JUNK,
                          cpf_idx=cpf.idx, cb_idx=cb.idx, dbg=dbg)

            @block.sync
            def _(sy):
                for g in range(NG):
                    ctx.inc(sy.dma_start(out=XF[:, g * N:(g + 1) * N],
                                         in_=xT[g * 128:(g + 1) * 128, :]), "dmx", 16)
                ctx.inc(sy.dma_start(out=CPF[:, :], in_=cpf_t[:, :]), "dmx", 16)
                ctx.inc(sy.dma_start(out=CB[:, :], in_=cb_t[:, :]), "dmx", 16)

            @block.gpsimd
            def _(po):
                for g in range(NG):
                    ctx.inc(po.dma_start(out=XB[:, g * N:(g + 1) * N],
                                         in_=xT[g * 128:(g + 1) * 128, :]), "dmx", 16)
                po.memset(JUNK[:, :], 0.0)
            xtok = ctx.tok("dmx")
            for eng in ENG:
                def mk(e):
                    def f(x):
                        x.wait_ge(*xtok)
                    return f
                getattr(block, eng)(mk(eng))

            for ph in phases:
                with contextlib.ExitStack() as es_ph:
                    ctx.es_phase = es_ph
                    if ph == "c":
                        emit_phase_c(ctx, aps, upto=UPTO)
                    elif ph == "b":
                        emit_phase_b(ctx, aps, upto=UPTO)
                    elif ph == "a":
                        emit_phase_a(ctx, aps, upto=UPTO)
                    elif ph == "d":
                        emit_phase_d(ctx, aps, upto=UPTO)
                    elif ph == "ln":
                        ps4 = [es_ph.enter_context(nc.psum_tensor(f"l_ps{i}", [128, TCW], F32))
                               for i in range(4)]
                        barrier(ctx)
                        emit_ln(ctx, es_ph, "lnt", ps4)
                        barrier(ctx)

            @block.sync
            def _(sy):
                for g in range(NG):
                    ctx.inc(sy.dma_start(out=xoT[g * 128:(g + 1) * 128, :],
                                         in_=XF[:, g * N:(g + 1) * N]), "dmo", 16)
                sy.wait_ge(*ctx.tok("dmo"))
    return nc, hbm


def run_phase_test(phase, x_in, x_exp, inp):
    nc, hbm = build_program([phase], inp)
    io = dict(hbm)
    io["xT"] = np.ascontiguousarray(x_in.T, dtype=np.float32)
    res = run_bass_kernel_spmd(nc, [io], core_ids=[0])
    if TAP:
        np.save("/tmp/tap.npy", res.results[0]["dbg"])
    got = res.results[0]["xoT"].T
    np.save("/tmp/xo.npy", got)
    rv = ((got - x_exp) ** 2).mean() / ((x_exp ** 2).mean() + 1e-30)
    print(f"phase {phase}: resid-var ratio = {rv:.3e}  maxabs={np.abs(got - x_exp).max():.3e}")
    return got, rv




def kernel(**inputs):
    x = np.asarray(inputs["L0_attn_x"], np.float32)
    ncores = 8
    xT = [np.ascontiguousarray(x[c].T) for c in range(ncores)]
    for ph in ["a", "b", "c", "d"]:
        nc, hbm = build_program([ph], inputs)
        in_maps = []
        for c in range(ncores):
            io = dict(hbm)
            io["xT"] = xT[c]
            in_maps.append(io)
        res = run_bass_kernel_spmd(nc, in_maps, core_ids=list(range(ncores)))
        xT = [np.ascontiguousarray(res.results[c]["xoT"]) for c in range(ncores)]
    return np.stack([t.T for t in xT], axis=0).astype(np.float32)


# version 10
# speedup vs baseline: 1.3129x; 1.3129x over previous
import contextlib
import numpy as np
import ml_dtypes
import concourse.bass as bass
import concourse.mybir as mybir
from concourse.bass_utils import run_bass_kernel_spmd

F32 = mybir.dt.float32
BF16 = mybir.dt.bfloat16
AF = mybir.ActivationFunctionType
ALU = mybir.AluOpType
AX = mybir.AxisListType

N = 2048
D = 1024
NG = 8
NT = 16
NTC = 4
TCW = 512
TWO_PI = 6.283185307179586
PI = 3.141592653589793
HPI = 1.5707963267948966
ENG = ("tensor", "vector", "scalar", "gpsimd", "sync")


GLOBAL_SEMS = {"dmw", "dmx", "dmo", "dmt", "dxo", "dxr", "dxa",
               "bar_v", "bar_s", "bar_p", "bar_t"}


class Ctx:
    def __init__(self, nc, block, es):
        self.nc, self.block, self.es = nc, block, es
        self.sem_h = {}
        self.cnt = {}
        self.t_ = {}
        self.es_phase = None
        self.alias = {}
        self.alias_n = 0

    def phase_reset(self):
        self.alias = {}
        self.alias_n = 0

    def _n(self, name):
        if name in GLOBAL_SEMS:
            return name
        if name not in self.alias:
            self.alias[name] = f"s{self.alias_n}"
            self.alias_n += 1
        return self.alias[name]

    def sem(self, name):
        name = self._n(name)
        if name not in self.sem_h:
            self.sem_h[name] = self.es.enter_context(self.nc.semaphore(name))
            self.cnt[name] = 0
        return self.sem_h[name]

    def inc(self, ins, name, k=1):
        h = self.sem(name)
        self.cnt[self._n(name)] += k
        ins.then_inc(h, k)
        return (h, self.cnt[self._n(name)])

    def base(self, name):
        self.sem(name)
        return self.cnt[self._n(name)]

    def tok(self, name):
        return (self.sem(name), self.cnt[self._n(name)])


def barrier(ctx, engines=ENG):
    b = ctx.block
    toks = []
    j = ctx.t_["junk"]
    if "vector" in engines:
        @b.vector
        def _(ve):
            toks.append(ctx.inc(ve.tensor_copy(j[0:1, 0:1], j[0:1, 1:2]), "bar_v"))
    if "scalar" in engines:
        @b.scalar
        def _(se):
            toks.append(ctx.inc(se.copy(j[0:1, 2:3], j[0:1, 3:4]), "bar_s"))
    if "gpsimd" in engines:
        @b.gpsimd
        def _(po):
            toks.append(ctx.inc(po.tensor_copy(j[0:1, 4:5], j[0:1, 5:6]), "bar_p"))
    if "tensor" in engines:
        @b.tensor
        def _(te):
            toks.append(ctx.inc(te.nop(), "bar_t"))
    for eng in engines:
        def mk(eng_name):
            def f(e):
                for t in toks:
                    e.wait_ge(*t)
            return f
        getattr(b, eng)(mk(eng))


def stage(ctx, name, items, nbanks, banks, mm_fn, steps, pe_waits=None):
    b = ctx.block
    sp = name + "p"
    bp = ctx.base(sp)
    nst = len(steps)
    bs = [ctx.base(name + f"s{j}") for j in range(nst)]
    free_j = [j for j, st in enumerate(steps) if st[2]][0]

    @b.tensor
    def _(te):
        for i, item in enumerate(items):
            if pe_waits:
                for t in pe_waits(i, item):
                    te.wait_ge(*t)
            if i >= nbanks:
                te.wait_ge(ctx.sem(name + f"s{free_j}"), bs[free_j] + i - nbanks + 1)
            last = mm_fn(te, item, banks[i % nbanks], i)
            ctx.inc(last, sp)

    for j, (eng_name, fn, frees) in enumerate(steps):
        def mk(j, eng_name, fn):
            prev = sp if j == 0 else name + f"s{j-1}"
            prev_base = bp if j == 0 else bs[j - 1]

            def body(eng):
                for i, item in enumerate(items):
                    eng.wait_ge(ctx.sem(prev), prev_base + i + 1)
                    if j < nst - 1 and i >= 2:
                        eng.wait_ge(ctx.sem(name + f"s{nst-1}"), bs[nst - 1] + i - 1)
                    last = fn(eng, item, banks[i % nbanks], i)
                    ctx.inc(last, name + f"s{j}")
            return body
        getattr(b, eng_name)(mk(j, eng_name, fn))


class Pack:
    def __init__(self):
        self.cols = []
        self.idx = {}

    def add(self, name, arr):
        arr = np.asarray(arr, np.float32)
        assert arr.shape[0] == 128, (name, arr.shape)
        w = arr.shape[1]
        pad = (-w) % 16
        if pad:
            arr = np.concatenate([arr, np.zeros((128, pad), np.float32)], axis=1)
        self.idx[name] = sum(c.shape[1] for c in self.cols)
        self.cols.append(arr)
        return self.idx[name]

    def array(self, bf16=False):
        a = (np.concatenate(self.cols, axis=1) if self.cols
             else np.zeros((128, 1), np.float32))
        return a.astype(ml_dtypes.bfloat16) if bf16 else a.astype(np.float32)

    def width(self):
        return max(1, sum(c.shape[1] for c in self.cols))


def col_pp(vec):
    return np.asarray(vec, np.float32).reshape(NG, 128).T


def base_packs():
    cpf, cb = Pack(), Pack()
    cpf.add("ones128", np.ones((128, 128), np.float32))
    cb.add("ones", np.ones((128, 1), np.float32))
    cb.add("idn", np.eye(128, dtype=np.float32))
    cb.add("onesrow", np.ones((128, 128), np.float32))
    return cpf, cb


def bcast8(ap_2d, width):
    return ap_2d.unsqueeze(2).broadcast_to((ap_2d.shape[0], ap_2d.shape[1], width))


def heads_view(ap, blocks, inner0, inner1):
    r = ap.rearrange("p (b c) -> p b c", c=128)
    return r[:, :, inner0:inner1]



LNUP = 9


def emit_ln(ctx, es, tag, banks, gamma_beta=None, eps=1e-5):
    nc, b = ctx.nc, ctx.block
    XF, XB = ctx.t_["XF"], ctx.t_["XB"]
    CB = ctx.t_["CB"]
    bi = ctx.t_["cb_idx"]
    onesb = CB[:, bi["ones"]:bi["ones"] + 1]
    ones_row_b = CB[0:1, bi["onesrow"]:bi["onesrow"] + 128]

    SB = es.enter_context(nc.sbuf_tensor(f"{tag}_sb", [128, 2 * TCW], BF16))
    AB = es.enter_context(nc.sbuf_tensor(f"{tag}_abr", [1, 4 * TCW], F32))
    ABL = es.enter_context(nc.sbuf_tensor(f"{tag}_abl", [1, 8 * TCW], BF16))
    REP = es.enter_context(nc.sbuf_tensor(f"{tag}_rep", [128, 2 * TCW], F32))
    T5 = es.enter_context(nc.sbuf_tensor(f"{tag}_t5", [128, TCW], F32))
    psS = banks[0][0:1, 0:TCW]
    psQ = banks[1][0:1, 0:TCW]
    psR = banks[2]
    psR2 = banks[3]

    b_sq = ctx.base(tag + "sq")
    b_mm = ctx.base(tag + "mm")
    b_rm = ctx.base(tag + "rm")
    b_rs = ctx.base(tag + "rs")
    b_rd = ctx.base(tag + "rd")
    b_bc = ctx.base(tag + "bc")
    b_ap = ctx.base(tag + "ap")
    b_pb = ctx.base(tag + "pb")

    for tc in range(NTC):
        par2 = tc % 2
        A = AB[0:1, par2 * 2 * TCW: par2 * 2 * TCW + TCW]
        B = AB[0:1, par2 * 2 * TCW + TCW:(par2 + 1) * 2 * TCW]
        base_l = par2 * 4 * TCW
        AH = ABL[0:1, base_l + 0 * TCW: base_l + 1 * TCW]
        AL = ABL[0:1, base_l + 1 * TCW: base_l + 2 * TCW]
        BH = ABL[0:1, base_l + 2 * TCW: base_l + 3 * TCW]
        BL = ABL[0:1, base_l + 3 * TCW: base_l + 4 * TCW]

        def mk_se(tc, A):
            def f(se):
                for p8 in range(2):
                    for g in range(NG):
                        j = tc * 16 + p8 * 8 + g
                        par = j % 2
                        if j >= 2:
                            se.wait_ge(ctx.sem(tag + "mm"), b_mm + j - 1)
                        src = XF[:, g * N + tc * TCW: g * N + (tc + 1) * TCW]
                        if p8 == 0:
                            ins = se.copy(SB[:, par * TCW:(par + 1) * TCW], src)
                        else:
                            ins = se.activation(SB[:, par * TCW:(par + 1) * TCW], src, AF.Square)
                        ctx.inc(ins, tag + "sq")
                se.wait_ge(ctx.sem(tag + "rm"), b_rm + tc + 1)
                ctx.inc(se.activation(A, A, AF.Sqrt), tag + "rs")
                se.wait_ge(ctx.sem(tag + "pb"), b_pb + tc + 1)
                if tc >= 1:
                    se.wait_ge(ctx.sem(tag + "ap"), b_ap + tc * NG)
                se.copy(REP[:, 0:TCW], psR[:, 0:TCW])
                ctx.inc(se.copy(REP[:, TCW:2 * TCW], psR2[:, 0:TCW]), tag + "bc")
                for g in range(NG):
                    se.wait_ge(ctx.sem(tag + "ap"), b_ap + tc * NG + g + 1)
                    sl = np.s_[:, g * N + tc * TCW: g * N + (tc + 1) * TCW]
                    ctx.inc(se.copy(XB[sl], XF[sl]), tag + "xb")
            return f
        b.scalar(mk_se(tc, A))

        def mk_te(tc):
            def f(te):
                for p8 in range(2):
                    dst = psQ if p8 else psS
                    for g in range(NG):
                        j = tc * 16 + p8 * 8 + g
                        par = j % 2
                        te.wait_ge(ctx.sem(tag + "sq"), b_sq + j + 1)
                        if p8 == 0 and g == 0 and tc >= 1:
                            te.wait_ge(ctx.sem(tag + "rd"), b_rd + tc)
                        ctx.inc(te.matmul(dst, onesb, SB[:, par * TCW:(par + 1) * TCW],
                                          start=(g == 0), stop=(g == NG - 1)), tag + "mm")
                base_l = (tc % 2) * 4 * TCW
                te.wait_ge(ctx.sem(tag + "rd"), b_rd + tc + 1)
                if tc >= 1:
                    te.wait_ge(ctx.sem(tag + "bc"), b_bc + tc)
                te.matmul(psR[:, 0:TCW], ones_row_b,
                          ABL[0:1, base_l:base_l + TCW], start=True, stop=False)
                te.matmul(psR[:, 0:TCW], ones_row_b,
                          ABL[0:1, base_l + TCW: base_l + 2 * TCW], start=False, stop=True)
                te.matmul(psR2[:, 0:TCW], ones_row_b,
                          ABL[0:1, base_l + 2 * TCW: base_l + 3 * TCW], start=True, stop=False)
                ctx.inc(te.matmul(psR2[:, 0:TCW], ones_row_b,
                                  ABL[0:1, base_l + 3 * TCW: base_l + 4 * TCW],
                                  start=False, stop=True), tag + "pb")
            return f
        b.tensor(mk_te(tc))

        def mk_ve(tc, A, B, AH, AL, BH, BL):
            def f(ve):
                ve.wait_ge(ctx.sem(tag + "mm"), b_mm + tc * 16 + 16)
                if tc >= 2:
                    ve.wait_ge(ctx.sem(tag + "pb"), b_pb + tc - 1)
                ve.tensor_scalar_mul(B, psS, 1.0 / D)
                ve.tensor_scalar_mul(A, psQ, 1.0 / D)
                ve.tensor_mul(psS, B, B)
                ve.tensor_sub(A, A, psS)
                ctx.inc(ve.tensor_scalar_add(A, A, eps), tag + "rm")
                ve.wait_ge(ctx.sem(tag + "rs"), b_rs + tc + 1)
                ctx.inc(ve.reciprocal(A, A), tag + "rr")
                ve.wait_ge(*ctx.tok(tag + "rr"))
                ve.scalar_tensor_tensor(B, B, -1.0, A, ALU.mult, ALU.mult)
                ve.tensor_copy(AH, A)
                ve.tensor_sub(AL, A, AH)
                ve.tensor_copy(BH, B)
                ctx.inc(ve.tensor_sub(BL, B, BH), tag + "rd")
                ve.wait_ge(ctx.sem(tag + "bc"), b_bc + tc + 1)
                for g in range(NG):
                    sl = np.s_[:, g * N + tc * TCW: g * N + (tc + 1) * TCW]
                    ve.tensor_mul(T5[:, 0:TCW], XF[sl], REP[:, 0:TCW])
                    ins = ve.tensor_add(XF[sl], T5[:, 0:TCW], REP[:, TCW:2 * TCW])
                    if gamma_beta is not None:
                        gc, bc2 = gamma_beta
                        ins = ve.tensor_scalar(XF[sl], XF[sl], gc(g), bc2(g), ALU.mult, ALU.add)
                    ctx.inc(ins, tag + "ap")
            return f
        b.vector(mk_ve(tc, A, B, AH, AL, BH, BL))



def host_prep_c(inp):
    return {
        "c_wq": np.asarray(inp["L1_attn_Wq"], np.float32),
        "c_wd": np.asarray(inp["L1_attn_Wd"], np.float32),
        "c_wu": np.asarray(inp["L1_attn_Wu"], np.float32),
        "c_buv": np.asarray(inp["L1_attn_bu"], np.float32).reshape(1, N),
        "c_wo": np.asarray(inp["L1_attn_Wo"], np.float32),
    }


def decl_c(nc):
    return {
        "c_wq": nc.dram_tensor("c_wq", [D, D], F32, kind="ExternalInput").ap(),
        "c_wd": nc.dram_tensor("c_wd", [D, 256], F32, kind="ExternalInput").ap(),
        "c_wu": nc.dram_tensor("c_wu", [256, N], F32, kind="ExternalInput").ap(),
        "c_wo": nc.dram_tensor("c_wo", [D, D], F32, kind="ExternalInput").ap(),
        "c_buv": nc.dram_tensor("c_buv", [1, N], F32, kind="ExternalInput").ap(),
    }


def pack_c(cpf, cb, inp):
    cpf.add("c_bq", col_pp(inp["L1_attn_bq"]))
    cpf.add("c_bd", np.asarray(inp["L1_attn_bd"], np.float32).reshape(2, 128).T)
    cpf.add("c_bo", col_pp(inp["L1_attn_bo"]))



def emit_tap(ctx, src_ap, cols):
    b = ctx.block
    dbg = ctx.t_["dbg"]
    barrier(ctx)

    @b.gpsimd
    def _(po):
        CH = 2048
        for c0 in range(0, cols, CH):
            w = min(CH, cols - c0)
            ctx.inc(po.dma_start(out=dbg[:, c0:c0 + w], in_=src_ap[:, c0:c0 + w]), "dmt", 16)

    @b.sync
    def _(sy):
        sy.wait_ge(*ctx.tok("dmt"))
    barrier(ctx)


def emit_phase_c(ctx, aps, upto=99):
    nc, b = ctx.nc, ctx.block
    es = ctx.es_phase
    XF, XB = ctx.t_["XF"], ctx.t_["XB"]
    CPF, CB = ctx.t_["CPF"], ctx.t_["CB"]
    ci, bi = ctx.t_["cpf_idx"], ctx.t_["cb_idx"]

    QT = es.enter_context(nc.sbuf_tensor("c_QT", [128, NG * N], BF16))
    T1T = es.enter_context(nc.sbuf_tensor("c_T1T", [128, 2 * N], BF16))
    KVS = es.enter_context(nc.sbuf_tensor("c_KVS", [128, 2 * N], BF16))
    CTXB = es.enter_context(nc.sbuf_tensor("c_CTXB", [128, D], BF16))
    KSB = es.enter_context(nc.sbuf_tensor("c_KSB", [128, 16], BF16))
    ZT = es.enter_context(nc.sbuf_tensor("c_ZT", [128, NT * 16], F32))
    CS = es.enter_context(nc.sbuf_tensor("c_CS", [128, 6 * TCW], BF16))
    es_w = contextlib.ExitStack()
    es.callback(es_w.close)
    CBU = es_w.enter_context(nc.sbuf_tensor("c_CBU", [1, N], BF16))
    WQ = es_w.enter_context(nc.sbuf_tensor("c_WQ", [128, NG * D], BF16))
    WD = es_w.enter_context(nc.sbuf_tensor("c_WD", [128, NG * 256], BF16))
    WU = es_w.enter_context(nc.sbuf_tensor("c_WU", [128, 2 * N], BF16))
    ps4 = [es.enter_context(nc.psum_tensor(f"c_ps{i}", [128, TCW], F32)) for i in range(4)]
    ps = ps4[:3]
    psEO = es.enter_context(nc.psum_tensor("c_psEO", [128, 2 * TCW], F32))
    psE = psEO[:, 0:TCW]
    psO = psEO[:, TCW:2 * TCW]
    psKb = ps4[3]
    psK = psKb[:, 0:16]
    psB = [es.enter_context(nc.psum_tensor(f"c_psb{i}", [128, 1024], BF16)) for i in range(2)]

    @b.gpsimd
    def _(po):
        ctx.inc(po.dma_start(out=CBU[0:1, :], in_=aps["c_buv"][0:1, :]), "dmw", 16)
        for g in range(NG):
            ctx.inc(po.dma_start(out=WQ[:, g * D:(g + 1) * D],
                                 in_=aps["c_wq"][g * 128:(g + 1) * 128, :]), "dmw", 16)
        for g in range(NG):
            ctx.inc(po.dma_start(out=WD[:, g * 256:(g + 1) * 256],
                                 in_=aps["c_wd"][g * 128:(g + 1) * 128, :]), "dmw", 16)
        for g in range(2):
            ctx.inc(po.dma_start(out=WU[:, g * N:(g + 1) * N],
                                 in_=aps["c_wu"][g * 128:(g + 1) * 128, :]), "dmw", 16)
        po.memset(KSB[:, :], 0.0)
    wtok = ctx.tok("dmw")

    @b.vector
    def _(ve):
        ve.memset(psEO[:, 0:2 * TCW], 0.0)
        ve.memset(psKb[:, 0:16], 0.0)
    barrier(ctx)

    @b.tensor
    def _(te):
        te.wait_ge(*wtok)

    if upto < 1:
        return
    bqc = ci["c_bq"]

    def mm_q(te, item, p, i):
        dog, tc = item
        for kg in range(NG):
            ins = te.matmul(p[:, 0:TCW], WQ[:, kg * D + dog * 128: kg * D + (dog + 1) * 128],
                            XB[:, kg * N + tc * TCW: kg * N + (tc + 1) * TCW],
                            start=(kg == 0), stop=(kg == NG - 1))
        return ins

    def ev_q1(ve, item, p, i):
        dog, tc = item
        par = i % 2
        bq_ap = CPF[:, bqc + dog:bqc + dog + 1]
        ve.tensor_scalar(CS[:, par * TCW:(par + 1) * TCW], p[:, 0:TCW], bq_ap, 0.0,
                         ALU.add, ALU.min)
        return ve.tensor_scalar(CS[:, (2 + par) * TCW:(3 + par) * TCW], p[:, 0:TCW], bq_ap, 0.0,
                                ALU.add, ALU.max)

    def ev_q2(se, item, p, i):
        par = i % 2
        return se.activation(CS[:, (4 + par) * TCW:(5 + par) * TCW],
                             CS[:, par * TCW:(par + 1) * TCW], AF.Exp)

    def ev_q3(po, item, p, i):
        dog, tc = item
        par = i % 2
        return po.tensor_add(QT[:, dog * N + tc * TCW: dog * N + (tc + 1) * TCW],
                             CS[:, (2 + par) * TCW:(3 + par) * TCW],
                             CS[:, (4 + par) * TCW:(5 + par) * TCW])

    stage(ctx, "cq", [(dog, tc) for dog in range(NG) for tc in range(NTC)], 3, ps,
          mm_q, [("vector", ev_q1, True), ("scalar", ev_q2, False), ("gpsimd", ev_q3, False)])
    barrier(ctx)

    if upto < 2:
        return
    bdc = ci["c_bd"]

    def mm_t1(te, item, p, i):
        dog, tc = item
        for kg in range(NG):
            ins = te.matmul(p[:, 0:TCW], WD[:, kg * 256 + dog * 128: kg * 256 + (dog + 1) * 128],
                            XB[:, kg * N + tc * TCW: kg * N + (tc + 1) * TCW],
                            start=(kg == 0), stop=(kg == NG - 1))
        return ins

    def ev_t1(ve, item, p, i):
        dog, tc = item
        return ve.tensor_scalar_add(T1T[:, dog * N + tc * TCW: dog * N + (tc + 1) * TCW],
                                    p[:, 0:TCW], CPF[:, bdc + dog:bdc + dog + 1])

    stage(ctx, "ct1", [(dog, tc) for dog in range(2) for tc in range(NTC)], 3, ps,
          mm_t1, [("vector", ev_t1, True)])
    barrier(ctx)
    if TAP == "qt":
        emit_tap(ctx, QT[:, :], NG * N)
    if TAP == "t1t":
        emit_tap(ctx, T1T[:, :], 2 * N)

    if upto < 3:
        return
    ones_row_b = CB[0:1, bi["ones"]:bi["ones"] + 1].broadcast_to((1, 128))
    bu_row = CBU[0:1, 0:N]
    onesb = CB[:, bi["ones"]:bi["ones"] + 1]
    b_kvp = ctx.base("kvp")
    b_kv1 = ctx.base("kv1")
    b_kv2 = ctx.base("kv2")
    b_kv3 = ctx.base("kv3")
    b_cx = ctx.base("cxm")

    @b.tensor
    def _(te):
        for t in range(NT):
            slot = t % 2
            for cc in range(4):
                i = t * 4 + cc
                if i >= 3:
                    te.wait_ge(ctx.sem("kv1"), b_kv1 + i - 2)
                p = ps[i % 3]
                for kg in range(2):
                    te.matmul(p[:, 0:TCW], T1T[:, kg * N + t * 128: kg * N + (t + 1) * 128],
                              WU[:, kg * N + cc * TCW: kg * N + (cc + 1) * TCW],
                              start=(kg == 0), stop=False)
                ctx.inc(te.matmul(p[:, 0:TCW], ones_row_b,
                                  bu_row[0:1, cc * TCW:(cc + 1) * TCW],
                                  start=False, stop=True), "kvp")
            te.wait_ge(ctx.sem("kv3"), b_kv3 + (t + 1) * 4)
            te.wait_ge(ctx.sem("kv1"), b_kv1 + (t + 1) * 4)
            kv = KVS[:, slot * N:(slot + 1) * N]
            ins = None
            for h in range(16):
                kh = kv[:, h * 128: h * 128 + 64]
                vh = kv[:, h * 128 + 64:(h + 1) * 128]
                dst = psE if h % 2 == 0 else psO
                te.matmul(dst[0:64, (h // 2) * 64:(h // 2) * 64 + 64], kh, vh,
                          start=False, stop=True, skip_group_check=True)
                ins = te.matmul(psK[0:64, h:h + 1], kh, onesb,
                                start=False, stop=True, skip_group_check=True)
            ctx.inc(ins, "cxm")

    @b.vector
    def _(ve):
        for t in range(NT):
            slot = t % 2
            kv = KVS[:, slot * N:(slot + 1) * N]
            for cc in range(4):
                i = t * 4 + cc
                ve.wait_ge(ctx.sem("kvp"), b_kvp + i + 1)
                if t >= 2 and cc == 0:
                    ve.wait_ge(ctx.sem("cxm"), b_cx + t - 1)
                if i >= 2:
                    ve.wait_ge(ctx.sem("kv3"), b_kv3 + i - 1)
                p = ps[i % 3]
                par = i % 2
                pv = heads_view(p[:, 0:TCW], 4, 64, 128)
                kvv = heads_view(kv[:, cc * TCW:(cc + 1) * TCW], 4, 64, 128)
                pk = heads_view(p[:, 0:TCW], 4, 0, 64)
                ve.tensor_copy(kvv, pv)
                ve.tensor_single_scalar(CS[:, par * 256:par * 256 + 256], pk, 0.0, ALU.min)
                ins = ve.tensor_single_scalar(CS[:, 512 + par * 256:512 + par * 256 + 256],
                                              pk, 0.0, ALU.max)
                ctx.inc(ins, "kv1")

    @b.scalar
    def _(se):
        for i in range(NT * 4):
            se.wait_ge(ctx.sem("kv1"), b_kv1 + i + 1)
            if i >= 2:
                se.wait_ge(ctx.sem("kv3"), b_kv3 + i - 1)
            par = i % 2
            ctx.inc(se.activation(CS[:, 1024 + par * 256:1024 + par * 256 + 256],
                                  CS[:, par * 256:par * 256 + 256], AF.Exp), "kv2")

    @b.gpsimd
    def _(po):
        for t in range(NT):
            slot = t % 2
            kv = KVS[:, slot * N:(slot + 1) * N]
            for cc in range(4):
                i = t * 4 + cc
                po.wait_ge(ctx.sem("kv2"), b_kv2 + i + 1)
                par = i % 2
                kvk = heads_view(kv[:, cc * TCW:(cc + 1) * TCW], 4, 0, 64)
                ctx.inc(po.tensor_add(kvk, CS[:, 512 + par * 256:512 + par * 256 + 256],
                                      CS[:, 1024 + par * 256:1024 + par * 256 + 256]), "kv3")

    cx_done = ctx.tok("cxm")

    @b.scalar
    def _(se):
        se.wait_ge(*cx_done)
        for h in range(16):
            src = psE if h % 2 == 0 else psO
            r0 = (h % 2) * 64
            se.copy(CTXB[r0:r0 + 64, h * 64:(h + 1) * 64],
                    src[0:64, (h // 2) * 64:(h // 2) * 64 + 64])
            ctx.inc(se.copy(KSB[r0:r0 + 64, h:h + 1], psK[0:64, h:h + 1]), "cxe")
    barrier(ctx)

    if upto < 5:
        return
    if TAP == "ctx":
        emit_tap(ctx, CTXB[:, :], D)
    if TAP == "ksb":
        emit_tap(ctx, KSB[:, :], 16)
    es_w.close()
    WO = es.enter_context(nc.sbuf_tensor("c_WO", [128, NG * D], BF16))

    @b.gpsimd
    def _(po):
        for g in range(NG):
            ctx.inc(po.dma_start(out=WO[:, g * D:(g + 1) * D],
                                 in_=aps["c_wo"][g * 128:(g + 1) * 128, :]), "dmw", 16)
    wotokc = ctx.tok("dmw")

    b_zp = ctx.base("zp")
    b_zf = ctx.base("zf")

    @b.tensor
    def _(te):
        for t in range(NT):
            if t >= 1:
                te.wait_ge(ctx.sem("zf"), b_zf + t)
            ins = None
            for g in range(NG):
                ins = te.matmul(ps[0][:, 2 * g:2 * g + 2], QT[:, g * N + t * 128: g * N + (t + 1) * 128],
                                KSB[:, 2 * g:2 * g + 2], start=(g == 0), stop=True,
                                skip_group_check=True)
            ctx.inc(ins, "zp")

    @b.vector
    def _(ve):
        for t in range(NT):
            ve.wait_ge(ctx.sem("zp"), b_zp + t + 1)
            ins = ve.tensor_scalar_add(ZT[:, t * 16:(t + 1) * 16], ps[0][:, 0:16], 1e-6)
            if TAP != "zraw":
                ctx.inc(ins, "zadd")
                ve.wait_ge(*ctx.tok("zadd"))
                ins = ve.reciprocal(ZT[:, t * 16:(t + 1) * 16], ZT[:, t * 16:(t + 1) * 16])
            ctx.inc(ins, "zf")
    barrier(ctx)
    if TAP in ("zt", "zraw"):
        emit_tap(ctx, ZT[:, :], NT * 16)

    if upto < 6:
        return
    def mm_out(te, item, p, i):
        t, par = item
        r0 = par * 64
        ins = None
        for j in range(8):
            h = 2 * j + par
            ins = te.matmul(p[:, j * 64:(j + 1) * 64],
                            QT[r0:r0 + 64, (h // 2) * N + t * 128:(h // 2) * N + (t + 1) * 128],
                            CTXB[r0:r0 + 64, h * 64:(h + 1) * 64], start=(j == 0), stop=True,
                            skip_group_check=True)
        return ins

    def ev_out(ve, item, p, i):
        t, par = item
        zv = bcast8(ZT[:, t * 16 + par: t * 16 + 16: 2], 64)
        pv = p[:, 0:TCW].rearrange("p (h c) -> p h c", c=64)
        dst = XB[:, t * 1024:(t + 1) * 1024].rearrange("p (h c) -> p h c", c=64)[:, par::2, :]
        return ve.tensor_tensor(dst, pv, zv, ALU.mult)

    stage(ctx, "co", [(t, par) for t in range(NT) for par in range(2)], 3, ps,
          mm_out, [("vector", ev_out, True)])
    barrier(ctx)
    if TAP == "otok":
        emit_tap(ctx, XB[:, :], NG * N)

    if upto < 7:
        return
    idn = CB[:, bi["idn"]:bi["idn"] + 128]
    b_tp = ctx.base("tp")
    b_tf = ctx.base("tf")

    @b.tensor
    def _(te):
        for i in range(128):
            t, og = divmod(i, NG)
            if i >= 2:
                te.wait_ge(ctx.sem("tf"), b_tf + i - 1)
            ctx.inc(te.transpose(psB[i % 2][:, 0:128],
                                 XB[:, t * 1024 + og * 128: t * 1024 + (og + 1) * 128], idn), "tp")

    @b.scalar
    def _(se):
        for i in range(128):
            t, og = divmod(i, NG)
            se.wait_ge(ctx.sem("tp"), b_tp + i + 1)
            ins = se.copy(QT[:, og * N + t * 128: og * N + (t + 1) * 128], psB[i % 2][:, 0:128])
            ctx.inc(ins, "tf")
    barrier(ctx)
    if TAP == "outt":
        emit_tap(ctx, QT[:, :], NG * N)

    if upto < 8:
        return
    boc = ci["c_bo"]

    def mm_wo(te, item, p, i):
        dog, tc = item
        if i == 0:
            te.wait_ge(*wotokc)
        for kg in range(NG):
            ins = te.matmul(p[:, 0:TCW], WO[:, kg * D + dog * 128: kg * D + (dog + 1) * 128],
                            QT[:, kg * N + tc * TCW: kg * N + (tc + 1) * TCW],
                            start=(kg == 0), stop=(kg == NG - 1))
        return ins

    def ev_wo(ve, item, p, i):
        dog, tc = item
        sl = np.s_[:, dog * N + tc * TCW: dog * N + (tc + 1) * TCW]
        return ve.scalar_tensor_tensor(XF[sl], p[:, 0:TCW], CPF[:, boc + dog:boc + dog + 1],
                                       XF[sl], ALU.add, ALU.add)

    stage(ctx, "cw", [(dog, tc) for dog in range(NG) for tc in range(NTC)], 3, ps,
          mm_wo, [("vector", ev_wo, True)])
    barrier(ctx)

    if upto < 9:
        return
    emit_ln(ctx, es, "lnc", ps4)
    barrier(ctx)



def host_prep_a(inp):
    return {
        "a_wq": np.asarray(inp["L0_attn_Wq"], np.float32),
        "a_wk": np.asarray(inp["L0_attn_Wk"], np.float32),
        "a_wv": np.asarray(inp["L0_attn_Wv"], np.float32),
        "a_wo": np.asarray(inp["L0_attn_Wo"], np.float32),
        "a_g1": np.asarray(inp["L0_attn_G1"], np.float32),
        "a_g2": np.asarray(inp["L0_attn_G2"], np.float32),
        "a_gbr": np.concatenate([np.asarray(inp["L0_attn_gamma"], np.float32),
                                 np.asarray(inp["L0_attn_beta"], np.float32)]).reshape(1, 2 * D),
    }


def decl_a(nc):
    return {
        "a_wq": nc.dram_tensor("a_wq", [D, D], F32, kind="ExternalInput").ap(),
        "a_wk": nc.dram_tensor("a_wk", [D, D], F32, kind="ExternalInput").ap(),
        "a_wv": nc.dram_tensor("a_wv", [D, D], F32, kind="ExternalInput").ap(),
        "a_wo": nc.dram_tensor("a_wo", [D, D], F32, kind="ExternalInput").ap(),
        "a_g1": nc.dram_tensor("a_g1", [D, 64], F32, kind="ExternalInput").ap(),
        "a_g2": nc.dram_tensor("a_g2", [64, D], F32, kind="ExternalInput").ap(),
        "a_gbr": nc.dram_tensor("a_gbr", [1, 2 * D], F32, kind="ExternalInput").ap(),
    }


def pack_a(cpf, cb, inp):
    th = np.asarray(inp["L0_attn_theta"], np.float32).reshape(D)
    cpf.add("a_th", th.reshape(NG, 128).T)
    cpf.add("a_gam", col_pp(inp["L0_attn_gamma"]))



def emit_phase_a(ctx, aps, upto=99):
    nc, b = ctx.nc, ctx.block
    es = ctx.es_phase
    XF, XB = ctx.t_["XF"], ctx.t_["XB"]
    CPF, CB = ctx.t_["CPF"], ctx.t_["CB"]
    ci, bi = ctx.t_["cpf_idx"], ctx.t_["cb_idx"]
    idn = CB[:, bi["idn"]:bi["idn"] + 128]
    TH2 = 6.283185307179586

    OUT = es.enter_context(nc.sbuf_tensor("a_OUT", [128, NG * N], BF16))
    ctx.t_["a_OUT"] = OUT
    es_a = contextlib.ExitStack()
    es.callback(es_a.close)
    WQB = [es_a.enter_context(nc.sbuf_tensor(f"a_WQB{i}", [128, 3 * 8 * 128], BF16))
           for i in range(2)]
    QG = es_a.enter_context(nc.sbuf_tensor("a_QG", [128, N], BF16))
    KG = es_a.enter_context(nc.sbuf_tensor("a_KG", [128, N], BF16))
    VG = es_a.enter_context(nc.sbuf_tensor("a_VG", [128, N], BF16))
    ANG = es_a.enter_context(nc.sbuf_tensor("a_ANG", [128, 1024], F32))
    ANGU = es_a.enter_context(nc.sbuf_tensor("a_ANGU", [128, 1024], F32))
    ANGI = es_a.enter_context(nc.sbuf_tensor("a_ANGI", [128, 1024], mybir.dt.int32))
    SINT = es_a.enter_context(nc.sbuf_tensor("a_SINT", [128, N], BF16))
    COST = es_a.enter_context(nc.sbuf_tensor("a_COST", [128, N], BF16))
    QCOS = es_a.enter_context(nc.sbuf_tensor("a_QCOS", [128, N], BF16))
    QSIN = es_a.enter_context(nc.sbuf_tensor("a_QSIN", [128, N], BF16))
    KCOS = es_a.enter_context(nc.sbuf_tensor("a_KCOS", [128, N], BF16))
    KSIN = es_a.enter_context(nc.sbuf_tensor("a_KSIN", [128, N], BF16))
    KVT = es_a.enter_context(nc.sbuf_tensor("a_KVT", [128, 2 * 384], BF16))
    ES2 = es_a.enter_context(nc.sbuf_tensor("a_ES2", [128, 2 * 256], BF16))
    ESR = es_a.enter_context(nc.sbuf_tensor("a_ESR", [128, 2 * 256], BF16))
    SSB = es_a.enter_context(nc.sbuf_tensor("a_SSB", [128, 128], BF16))
    TBH = es_a.enter_context(nc.sbuf_tensor("a_TBH", [64, 2 * 128], F32))
    TB2 = es_a.enter_context(nc.sbuf_tensor("a_TB2", [64, 2 * 128], BF16))
    psE0 = es_a.enter_context(nc.psum_tensor("a_psE0", [128, TCW], F32))
    psE1 = es_a.enter_context(nc.psum_tensor("a_psE1", [128, TCW], F32))
    psT0 = es_a.enter_context(nc.psum_tensor("a_psT0", [128, 1024], BF16))
    psT1 = es_a.enter_context(nc.psum_tensor("a_psT1", [128, 1024], BF16))
    psST = es_a.enter_context(nc.psum_tensor("a_psST", [128, TCW], F32))
    psOA = es_a.enter_context(nc.psum_tensor("a_psOA", [128, TCW], F32))
    psOB = es_a.enter_context(nc.psum_tensor("a_psOB", [128, TCW], F32))
    psOC = es_a.enter_context(nc.psum_tensor("a_psOC", [128, TCW], F32))

    thc = ci["a_th"]
    wtoks = {}

    def load_ablk(po, gh, slot):
        for m, ap in enumerate([aps["a_wq"], aps["a_wk"], aps["a_wv"]]):
            ctx.inc(po.dma_start(
                out=WQB[slot][:, m * 1024:(m + 1) * 1024].rearrange(
                    "p (k c) -> p k c", c=128),
                in_=ap[:, gh * 128:(gh + 1) * 128].rearrange("(k p) c -> p k c", p=128)),
                "dmw", 16)
        return ctx.tok("dmw")

    @b.gpsimd
    def _(po):
        wtoks[0] = load_ablk(po, 0, 0)
        wtoks[1] = load_ablk(po, 1, 1)

    A = {k: ctx.base(k) for k in
         ["aio", "apj", "apv", "atg", "asn", "atr", "atp", "akv",
          "ae", "aes", "aem", "aop", "ass", "ath", "aoe", "ao2"]}

    for gh in range(NG):
        def mk_po1(gh):
            def f(po):
                if gh >= 2:
                    po.wait_ge(ctx.sem("apv"), A["apv"] + (gh - 1) * 12)
                    wtoks[gh] = load_ablk(po, gh, gh % 2)
            return f
        b.gpsimd(mk_po1(gh))

        def mk_pe1(gh):
            def f(te):
                te.wait_ge(*wtoks[gh])
                if gh > 0:
                    te.wait_ge(ctx.sem("aes"), A["aes"] + gh * 32)
                    te.wait_ge(ctx.sem("ass"), A["ass"] + gh * 16)
                rot = [psE0, psE1, psST]
                for r in range(12):
                    m, tc = divmod(r, 4)
                    j = gh * 12 + r
                    if r >= 3:
                        te.wait_ge(ctx.sem("apv"), A["apv"] + j - 2)
                    for kg in range(NG):
                        ins = te.matmul(rot[r % 3][:, 0:TCW],
                                        WQB[gh % 2][:, m * 1024 + kg * 128: m * 1024 + (kg + 1) * 128],
                                        XB[:, kg * N + tc * TCW: kg * N + (tc + 1) * TCW],
                                        start=(kg == 0), stop=(kg == NG - 1))
                    ctx.inc(ins, "apj")
            return f
        b.tensor(mk_pe1(gh))

        def mk_se1(gh):
            def f(se):
                rot = [psE0, psE1, psST]
                for r in range(12):
                    m, tc = divmod(r, 4)
                    j = gh * 12 + r
                    se.wait_ge(ctx.sem("apj"), A["apj"] + j + 1)
                    dstt = [QG, KG, VG][m]
                    if m < 2:
                        ins = se.activation(dstt[:, tc * TCW:(tc + 1) * TCW],
                                            rot[r % 3][:, 0:TCW], AF.Silu)
                    else:
                        ins = se.copy(dstt[:, tc * TCW:(tc + 1) * TCW], rot[r % 3][:, 0:TCW])
                    ctx.inc(ins, "apv")
            return f
        b.scalar(mk_se1(gh))

        def mk_ve1(gh, hf2):
            def f(ve):
                ve.wait_ge(ctx.sem("aio"), A["aio"] + gh * 2 + hf2 + 1)
                if gh >= 1 or hf2 >= 1:
                    ve.wait_ge(ctx.sem("asn"), A["asn"] + (gh * 2 + hf2 - 1) * 2 + 2)
                th_col = CPF[:, thc + gh: thc + gh + 1]
                R = ANG[:, 0:1024]
                U = ANGU[:, 0:1024]
                ve.tensor_scalar_mul(R, R, th_col)
                ve.tensor_scalar_mul(U, R, 1.0 / TH2)
                ve.tensor_copy(ANGI[:, 0:1024], U)
                ve.tensor_copy(U, ANGI[:, 0:1024])
                ve.scalar_tensor_tensor(R, U, -TH2, R, ALU.mult, ALU.add)
                ve.tensor_single_scalar(U, R, PI, ALU.is_gt)
                ctx.inc(ve.scalar_tensor_tensor(R, U, -TH2, R, ALU.mult, ALU.add), "atg")
            return f

        def mk_se2(gh, hf2):
            def f(se):
                se.wait_ge(ctx.sem("atg"), A["atg"] + (gh * 2 + hf2) * 2 + 1)
                ctx.inc(se.activation(SINT[:, hf2 * 1024:(hf2 + 1) * 1024],
                                      ANG[:, 0:1024], AF.Sin), "asn")
            return f

        def mk_ve2(gh, hf2):
            def f(ve):
                ve.wait_ge(ctx.sem("asn"), A["asn"] + (gh * 2 + hf2) * 2 + 1)
                R = ANG[:, 0:1024]
                U = ANGU[:, 0:1024]
                ve.tensor_scalar_add(R, R, HPI)
                ve.tensor_single_scalar(U, R, PI, ALU.is_gt)
                ctx.inc(ve.scalar_tensor_tensor(R, U, -TH2, R, ALU.mult, ALU.add), "atg")
            return f

        def mk_se3(gh, hf2):
            def f(se):
                se.wait_ge(ctx.sem("atg"), A["atg"] + (gh * 2 + hf2) * 2 + 2)
                ctx.inc(se.activation(COST[:, hf2 * 1024:(hf2 + 1) * 1024],
                                      ANG[:, 0:1024], AF.Sin), "asn")
            return f

        def mk_po1b(gh, hf2):
            def f(po):
                g2h = gh * 2 + hf2
                if g2h >= 1:
                    po.wait_ge(ctx.sem("asn"), A["asn"] + g2h * 2)
                ctx.inc(po.iota(ANG[:, 0:1024], [[1, 1024]], base=hf2 * 1024,
                                channel_multiplier=0,
                                allow_small_or_imprecise_dtypes=True), "aio")
            return f

        for hf2 in range(2):
            b.gpsimd(mk_po1b(gh, hf2))
            b.vector(mk_ve1(gh, hf2))
            b.scalar(mk_se2(gh, hf2))
            b.vector(mk_ve2(gh, hf2))
            b.scalar(mk_se3(gh, hf2))

        def mk_ve3(gh):
            def f(ve):
                ve.wait_ge(ctx.sem("asn"), A["asn"] + gh * 4 + 4)
                ve.wait_ge(ctx.sem("apv"), A["apv"] + gh * 12 + 12)
                if gh >= 1:
                    ve.wait_ge(ctx.sem("aop"), A["aop"] + gh * 16)
                    ve.wait_ge(ctx.sem("ae"), A["ae"] + gh * 32)
                ve.tensor_mul(QCOS[:, :], QG[:, :], COST[:, :])
                ve.tensor_mul(QSIN[:, :], QG[:, :], SINT[:, :])
                ve.tensor_mul(KCOS[:, :], KG[:, :], COST[:, :])
                ve.tensor_mul(KSIN[:, :], KG[:, :], SINT[:, :])
                ctx.inc(ve.memset(psST[:, 0:TCW], 0.0), "atr")
            return f
        b.vector(mk_ve3(gh))

        def mk_pe2(gh):
            def f(te):
                te.wait_ge(ctx.sem("atr"), A["atr"] + gh + 1)
                for c in range(NT):
                    cs = np.s_[c * 128:(c + 1) * 128]
                    par = c % 2
                    if c >= 1:
                        te.wait_ge(ctx.sem("akv"), A["akv"] + gh * 16 + c)
                    te.transpose(psT0[:, 0:64], KCOS[0:64, cs], idn[0:64, 0:64])
                    te.transpose(psT0[:, 64:128], KSIN[0:64, cs], idn[0:64, 0:64])
                    te.transpose(psT0[:, 128:192], VG[0:64, cs], idn[0:64, 0:64])
                    te.transpose(psT1[:, 0:64], KCOS[64:128, cs], idn[64:128, 64:128])
                    te.transpose(psT1[:, 64:128], KSIN[64:128, cs], idn[64:128, 64:128])
                    ins = te.transpose(psT1[:, 128:192], VG[64:128, cs], idn[64:128, 64:128])
                    ctx.inc(ins, "atp")
                    if c >= 2:
                        te.wait_ge(ctx.sem("aes"), A["aes"] + gh * 32 + (c - 1) * 2)
                    for hh, (r0, pse) in enumerate([(0, psE0), (64, psE1)]):
                        te.matmul(pse[:, 0:128], KCOS[r0:r0 + 64, cs], QCOS[r0:r0 + 64, cs],
                                  start=True, stop=False)
                        ins = te.matmul(pse[:, 0:128], KSIN[r0:r0 + 64, cs], QSIN[r0:r0 + 64, cs],
                                        start=False, stop=True)
                        ctx.inc(ins, "ae")
                    te.wait_ge(ctx.sem("aem"), A["aem"] + gh * 32 + c * 2 + 2)
                    te.wait_ge(ctx.sem("akv"), A["akv"] + gh * 16 + c + 1)
                    if c >= 1:
                        te.wait_ge(ctx.sem("ass"), A["ass"] + gh * 16 + c)
                        te.wait_ge(ctx.sem("aoe"), A["aoe"] + gh * 32 + 2 * c)
                        te.wait_ge(ctx.sem("ath"), A["ath"] + gh * 16 + c)
                    kv = KVT[:, par * 384:(par + 1) * 384]
                    if c == 0:
                        te.matmul(psOA[0:64, 0:128], kv[:, 128:192],
                                  ES2[:, par * 256: par * 256 + 128],
                                  start=True, stop=True)
                    else:
                        te.matmul(psOA[0:64, 0:128], kv[:, 128:192],
                                  ES2[:, par * 256: par * 256 + 128],
                                  start=True, stop=False)
                        te.matmul(psOA[0:64, 0:128], SSB[0:64, 0:64], QCOS[0:64, cs],
                                  start=False, stop=False, skip_group_check=True)
                        te.matmul(psOA[0:64, 0:128], SSB[0:64, 64:128], QSIN[0:64, cs],
                                  start=False, stop=True, skip_group_check=True)
                    te.matmul(psOB[0:64, 0:128], kv[:, 320:384],
                              ES2[:, par * 256 + 128: par * 256 + 256],
                              start=True, stop=True)
                    if c >= 1:
                        te.matmul(psOC[0:64, 0:128], SSB[64:128, 0:64], QCOS[64:128, cs],
                                  start=True, stop=False)
                        te.matmul(psOC[0:64, 0:128], SSB[64:128, 64:128], QSIN[64:128, cs],
                                  start=False, stop=True)
                    te.matmul(psST[0:64, 0:64], kv[:, 0:64], kv[:, 128:192],
                              start=False, stop=True, skip_group_check=True)
                    te.matmul(psST[0:64, 64:128], kv[:, 64:128], kv[:, 128:192],
                              start=False, stop=True, skip_group_check=True)
                    te.matmul(psST[0:64, 128:192], kv[:, 192:256], kv[:, 320:384],
                              start=False, stop=True, skip_group_check=True)
                    ins = te.matmul(psST[0:64, 192:256], kv[:, 256:320], kv[:, 320:384],
                                    start=False, stop=True, skip_group_check=True)
                    ctx.inc(ins, "aop")
            return f
        b.tensor(mk_pe2(gh))

        def mk_se4(gh):
            def f(se):
                for c in range(NT):
                    par = c % 2
                    se.wait_ge(ctx.sem("atp"), A["atp"] + gh * 16 + c + 1)
                    if c >= 2:
                        se.wait_ge(ctx.sem("aop"), A["aop"] + gh * 16 + c - 1)
                    kv = KVT[:, par * 384:(par + 1) * 384]
                    se.copy(kv[:, 0:192], psT0[:, 0:192])
                    ctx.inc(se.copy(kv[:, 192:384], psT1[:, 0:192]), "akv")
                    se.wait_ge(ctx.sem("ae"), A["ae"] + gh * 32 + c * 2 + 2)
                    ctx.inc(se.copy(ESR[:, par * 256: par * 256 + 128], psE0[:, 0:128]), "aes")
                    ctx.inc(se.copy(ESR[:, par * 256 + 128: par * 256 + 256], psE1[:, 0:128]),
                            "aes")
                    se.wait_ge(ctx.sem("aop"), A["aop"] + gh * 16 + c + 1)
                    se.copy(SSB[0:64, 0:128], psST[0:64, 0:128])
                    ctx.inc(se.copy(SSB[64:128, 0:128], psST[0:64, 128:256]), "ass")
                    if c >= 1:
                        ins = se.copy(TBH[0:64, par * 128:(par + 1) * 128], psOC[0:64, 0:128])
                    else:
                        ins = se.copy(TBH[0:64, 0:1], psOC[0:64, 0:1])
                    ctx.inc(ins, "ath")
                    se.wait_ge(ctx.sem("aoe"), A["aoe"] + gh * 32 + 2 * c + 2)
                    ocs = np.s_[gh * N + c * 128: gh * N + (c + 1) * 128]
                    ctx.inc(se.copy(OUT[64:128, ocs], TB2[0:64, par * 128:(par + 1) * 128]),
                            "ao2")
            return f
        b.scalar(mk_se4(gh))

        def mk_po2(gh):
            def f(po):
                for c in range(NT):
                    par = c % 2
                    po.wait_ge(ctx.sem("aes"), A["aes"] + gh * 32 + c * 2 + 2)
                    if c >= 2:
                        po.wait_ge(ctx.sem("aop"), A["aop"] + gh * 16 + c - 1)
                    po.affine_select(ES2[:, par * 256: par * 256 + 128],
                                     ESR[:, par * 256: par * 256 + 128],
                                     [[1, 128]], ALU.is_ge, 0.0, base=0, channel_multiplier=-1)
                    ctx.inc(po.affine_select(ES2[:, par * 256 + 128: par * 256 + 256],
                                             ESR[:, par * 256 + 128: par * 256 + 256],
                                             [[1, 128]], ALU.is_ge, 0.0,
                                             base=0, channel_multiplier=-1), "aem")
                    ctx.inc(po.tensor_copy(ES2[0:1, par * 256: par * 256 + 1],
                                           ES2[0:1, par * 256: par * 256 + 1]), "aem")
            return f
        b.gpsimd(mk_po2(gh))

        def mk_ve4(gh):
            def f(ve):
                for c in range(NT):
                    par = c % 2
                    ocs = np.s_[gh * N + c * 128: gh * N + (c + 1) * 128]
                    ve.wait_ge(ctx.sem("aop"), A["aop"] + gh * 16 + c + 1)
                    if c >= 2:
                        ve.wait_ge(ctx.sem("ao2"), A["ao2"] + gh * 16 + c - 1)
                    ctx.inc(ve.tensor_copy(OUT[0:64, ocs], psOA[0:64, 0:128]), "aoe")
                    ve.wait_ge(ctx.sem("ath"), A["ath"] + gh * 16 + c + 1)
                    if c >= 1:
                        ins = ve.tensor_add(TB2[0:64, par * 128:(par + 1) * 128],
                                            psOB[0:64, 0:128],
                                            TBH[0:64, par * 128:(par + 1) * 128])
                    else:
                        ins = ve.tensor_copy(TB2[0:64, par * 128:(par + 1) * 128],
                                             psOB[0:64, 0:128])
                    ctx.inc(ins, "aoe")
            return f
        b.vector(mk_ve4(gh))

    barrier(ctx)
    if TAP == "aatt":
        emit_tap(ctx, OUT[:, :], NG * N)
    if TAP == "aqc":
        emit_tap(ctx, QCOS[:, :], N)
    if TAP == "akc":
        emit_tap(ctx, KCOS[:, :], N)
    if TAP == "aqg":
        emit_tap(ctx, QG[:, :], N)
    es_a.close()
    if upto < 2:
        return
    es_f = contextlib.ExitStack()
    es.callback(es_f.close)
    emit_phase_a_finale(ctx, aps, es_f)
    if upto < 3:
        return
    barrier(ctx)
    es_f.close()
    psL = [es.enter_context(nc.psum_tensor(f"a_psL{i}", [128, TCW], F32)) for i in range(4)]
    emit_ln(ctx, es, "lna", psL)
    barrier(ctx)


def emit_phase_a_finale(ctx, aps, es):
    nc, b = ctx.nc, ctx.block
    XF, XB = ctx.t_["XF"], ctx.t_["XB"]
    CPF, CB = ctx.t_["CPF"], ctx.t_["CB"]
    ci, bi = ctx.t_["cpf_idx"], ctx.t_["cb_idx"]
    OUT = ctx.t_["a_OUT"]
    onesb = CB[:, bi["ones"]:bi["ones"] + 1]
    ones_row_b = CB[0:1, bi["onesrow"]:bi["onesrow"] + 128]

    GBR = es.enter_context(nc.sbuf_tensor("af_GBR", [1, 2 * D], BF16))
    G1B = es.enter_context(nc.sbuf_tensor("af_G1B", [128, NG * 64], BF16))
    G2B = es.enter_context(nc.sbuf_tensor("af_G2B", [64, D], BF16))
    WOB = es.enter_context(nc.sbuf_tensor("af_WOB", [128, NG * D], BF16))
    G1T = es.enter_context(nc.sbuf_tensor("af_G1T", [64, N], BF16))
    GTC = es.enter_context(nc.sbuf_tensor("af_GTC", [128, 2 * TCW], BF16))
    GOC = es.enter_context(nc.sbuf_tensor("af_GOC", [128, NG * TCW], BF16))
    SQ5 = es.enter_context(nc.sbuf_tensor("af_SQ5", [128, NG * TCW], BF16))
    AB5 = es.enter_context(nc.sbuf_tensor("af_AB5", [1, 4 * TCW], F32))
    AL5 = es.enter_context(nc.sbuf_tensor("af_AL5", [1, 8 * TCW], BF16))
    RP5 = es.enter_context(nc.sbuf_tensor("af_RP5", [128, 2 * TCW], F32))
    T55 = es.enter_context(nc.sbuf_tensor("af_T55", [128, TCW], F32))
    ps = [es.enter_context(nc.psum_tensor(f"af_ps{i}", [128, TCW], F32)) for i in range(4)]

    @b.gpsimd
    def _(po):
        ctx.inc(po.dma_start(out=GBR[0:1, :], in_=aps["a_gbr"][0:1, :]), "dmw", 16)
        for g in range(NG):
            ctx.inc(po.dma_start(out=G1B[:, g * 64:(g + 1) * 64],
                                 in_=aps["a_g1"][g * 128:(g + 1) * 128, :]), "dmw", 16)
        ctx.inc(po.dma_start(out=G2B[0:64, :], in_=aps["a_g2"][:, :]), "dmw", 16)
        for g in range(NG):
            ctx.inc(po.dma_start(out=WOB[:, g * D:(g + 1) * D],
                                 in_=aps["a_wo"][g * 128:(g + 1) * 128, :]), "dmw", 16)
    wtok = ctx.tok("dmw")
    barrier(ctx)

    F = {k: ctx.base(k) for k in ["fg1", "fge", "fgp", "fgt", "fwo", "fgo", "fsq",
                                  "fmm", "frm", "frs", "frd", "fpb", "fbc", "fap",
                                  "fgb", "fge2", "frr"]}
    gamc = ci["a_gam"]

    @b.tensor
    def _(te):
        te.wait_ge(*wtok)
        for tc in range(NTC):
            if tc >= 2:
                te.wait_ge(ctx.sem("fge"), F["fge"] + tc - 1)
            for kg in range(NG):
                ins = te.matmul(ps[tc % 2][0:64, 0:TCW], G1B[:, kg * 64:(kg + 1) * 64],
                                XB[:, kg * N + tc * TCW: kg * N + (tc + 1) * TCW],
                                start=(kg == 0), stop=(kg == NG - 1))
            ctx.inc(ins, "fg1")

    @b.vector
    def _(ve):
        for tc in range(NTC):
            ve.wait_ge(ctx.sem("fg1"), F["fg1"] + tc + 1)
            ctx.inc(ve.tensor_copy(G1T[0:64, tc * TCW:(tc + 1) * TCW],
                                   ps[tc % 2][0:64, 0:TCW]), "fge")

    for tc in range(NTC):
        par = tc % 2

        def mk_pe(tc, par):
            def f(te):
                te.wait_ge(ctx.sem("fge"), F["fge"] + tc + 1)
                for dog in range(NG):
                    jg = tc * NG + dog
                    if jg >= 1:
                        te.wait_ge(ctx.sem("fgt"), F["fgt"] + jg)
                    ctx.inc(te.matmul(ps[2][:, 0:TCW], G2B[0:64, dog * 128:(dog + 1) * 128],
                                      G1T[0:64, tc * TCW:(tc + 1) * TCW],
                                      start=True, stop=True), "fgp")
                    if jg >= 1:
                        te.wait_ge(ctx.sem("fgo"), F["fgo"] + jg)
                    for kg in range(NG):
                        ins = te.matmul(ps[3][:, 0:TCW],
                                        WOB[:, kg * D + dog * 128: kg * D + (dog + 1) * 128],
                                        OUT[:, kg * N + tc * TCW: kg * N + (tc + 1) * TCW],
                                        start=(kg == 0), stop=(kg == NG - 1))
                    ctx.inc(ins, "fwo")
            return f
        b.tensor(mk_pe(tc, par))

        def mk_se(tc, par):
            def f(se):
                for dog in range(NG):
                    jg = tc * NG + dog
                    se.wait_ge(ctx.sem("fgp"), F["fgp"] + jg + 1)
                    if jg >= 2:
                        se.wait_ge(ctx.sem("fgo"), F["fgo"] + jg - 1)
                    ctx.inc(se.activation(GTC[:, (jg % 2) * TCW:((jg % 2) + 1) * TCW],
                                          ps[2][:, 0:TCW], AF.Sigmoid), "fgt")
                    se.wait_ge(ctx.sem("fgo"), F["fgo"] + jg + 1)
                    if dog == 0 and tc >= 1:
                        se.wait_ge(ctx.sem("fmm"), F["fmm"] + tc * 16)
                    ctx.inc(se.activation(
                        SQ5[:, dog * TCW:(dog + 1) * TCW],
                        GOC[:, dog * TCW:(dog + 1) * TCW],
                        AF.Square), "fsq")
            return f
        b.scalar(mk_se(tc, par))

        def mk_ve(tc, par):
            def f(ve):
                for dog in range(NG):
                    jg = tc * NG + dog
                    ve.wait_ge(ctx.sem("fgt"), F["fgt"] + jg + 1)
                    ve.wait_ge(ctx.sem("fwo"), F["fwo"] + jg + 1)
                    if dog == 0 and tc >= 1:
                        ve.wait_ge(ctx.sem("fap"), F["fap"] + tc * NG)
                    ctx.inc(ve.tensor_mul(
                        GOC[:, dog * TCW:(dog + 1) * TCW],
                        ps[3][:, 0:TCW], GTC[:, (jg % 2) * TCW:((jg % 2) + 1) * TCW]), "fgo")
            return f
        b.vector(mk_ve(tc, par))

        def mk_pe2(tc, par):
            def f(te):
                for p8 in range(2):
                    for dog in range(NG):
                        jg = tc * NG + dog
                        te.wait_ge(ctx.sem("fsq"), F["fsq"] + jg + 1)
                        if p8 == 0 and dog == 0 and tc == 0:
                            te.wait_ge(ctx.sem("fge"), F["fge"] + NTC)
                        if p8 == 0 and dog == 0 and tc >= 1:
                            te.wait_ge(ctx.sem("frd"), F["frd"] + tc)
                            te.wait_ge(ctx.sem("fbc"), F["fbc"] + tc)
                            te.wait_ge(ctx.sem("fge2"), F["fge2"] + tc * NG)
                        if p8 == 0:
                            ins = te.matmul(
                                ps[0][0:1, 0:TCW], onesb,
                                GOC[:, dog * TCW:(dog + 1) * TCW],
                                start=(dog == 0), stop=(dog == NG - 1))
                        else:
                            ins = te.matmul(ps[1][0:1, 0:TCW], onesb,
                                            SQ5[:, dog * TCW:(dog + 1) * TCW],
                                            start=(dog == 0), stop=(dog == NG - 1))
                        ctx.inc(ins, "fmm")
            return f
        b.tensor(mk_pe2(tc, par))

        A5 = AB5[0:1, par * 2 * TCW: par * 2 * TCW + TCW]
        B5 = AB5[0:1, par * 2 * TCW + TCW:(par + 1) * 2 * TCW]
        base_l = par * 4 * TCW
        AH5 = AL5[0:1, base_l: base_l + TCW]
        ALo5 = AL5[0:1, base_l + TCW: base_l + 2 * TCW]
        BH5 = AL5[0:1, base_l + 2 * TCW: base_l + 3 * TCW]
        BL5 = AL5[0:1, base_l + 3 * TCW: base_l + 4 * TCW]

        def mk_rm(tc, A5, B5):
            def f(ve):
                ve.wait_ge(ctx.sem("fmm"), F["fmm"] + (tc + 1) * 16)
                if tc >= 2:
                    ve.wait_ge(ctx.sem("fgb"), F["fgb"] + (tc - 1) * NG)
                ctx.inc(ve.tensor_scalar_mul(B5, ps[0][0:1, 0:TCW], 1.0 / D), "frr")
                ve.wait_ge(*ctx.tok("frr"))
                ctx.inc(ve.tensor_scalar_mul(A5, ps[1][0:1, 0:TCW], 1.0 / D), "frr")
                ve.wait_ge(*ctx.tok("frr"))
                ctx.inc(ve.tensor_mul(ps[0][0:1, 0:TCW], B5, B5), "frr")
                ve.wait_ge(*ctx.tok("frr"))
                ctx.inc(ve.tensor_sub(A5, A5, ps[0][0:1, 0:TCW]), "frr")
                ve.wait_ge(*ctx.tok("frr"))
                ctx.inc(ve.tensor_scalar_add(A5, A5, 1e-5), "frm")
            return f
        b.vector(mk_rm(tc, A5, B5))

        def mk_rs(tc, A5):
            def f(se):
                se.wait_ge(ctx.sem("frm"), F["frm"] + tc + 1)
                ctx.inc(se.activation(A5, A5, AF.Sqrt), "frs")
            return f
        b.scalar(mk_rs(tc, A5))

        def mk_rd(tc, A5, B5, AH5, ALo5, BH5, BL5):
            def f(ve):
                ve.wait_ge(ctx.sem("frs"), F["frs"] + tc + 1)
                ctx.inc(ve.reciprocal(A5, A5), "frr")
                ve.wait_ge(*ctx.tok("frr"))
                ctx.inc(ve.scalar_tensor_tensor(B5, B5, -1.0, A5, ALU.mult, ALU.mult), "frr")
                ve.wait_ge(*ctx.tok("frr"))
                ve.tensor_copy(AH5, A5)
                ctx.inc(ve.tensor_sub(ALo5, A5, AH5), "frr")
                ve.wait_ge(*ctx.tok("frr"))
                ve.tensor_copy(BH5, B5)
                ctx.inc(ve.tensor_sub(BL5, B5, BH5), "frd")
            return f
        b.vector(mk_rd(tc, A5, B5, AH5, ALo5, BH5, BL5))

        def mk_pb(tc, AH5, ALo5):
            def f(te):
                te.wait_ge(ctx.sem("frd"), F["frd"] + tc + 1)
                if tc >= 1:
                    te.wait_ge(ctx.sem("fbc"), F["fbc"] + tc)
                te.matmul(ps[0][:, 0:TCW], ones_row_b, AH5, start=True, stop=False)
                ctx.inc(te.matmul(ps[0][:, 0:TCW], ones_row_b, ALo5, start=False, stop=True),
                        "fpb")
            return f
        b.tensor(mk_pb(tc, AH5, ALo5))

        def mk_bc(tc):
            def f(se):
                se.wait_ge(ctx.sem("fpb"), F["fpb"] + tc + 1)
                if tc >= 1:
                    se.wait_ge(ctx.sem("fap"), F["fap"] + tc * NG)
                ctx.inc(se.copy(RP5[:, 0:TCW], ps[0][:, 0:TCW]), "fbc")
            return f
        b.scalar(mk_bc(tc))

        def mk_pgb(tc, BH5, BL5):
            def f(te):
                for dog in range(NG):
                    jg = tc * NG + dog
                    te.wait_ge(ctx.sem("fbc"), F["fbc"] + tc + 1)
                    if jg >= 1:
                        te.wait_ge(ctx.sem("fge2"), F["fge2"] + jg)
                    gr = GBR[0:1, dog * 128:(dog + 1) * 128]
                    br = GBR[0:1, D + dog * 128: D + (dog + 1) * 128]
                    te.matmul(ps[1][:, 0:TCW], gr, BH5, start=True, stop=False)
                    te.matmul(ps[1][:, 0:TCW], gr, BL5, start=False, stop=False)
                    ctx.inc(te.matmul(ps[1][:, 0:TCW], br,
                                      ones_row_b[0:1, 0:1].broadcast_to((1, TCW)),
                                      start=False, stop=True), "fgb")
            return f
        b.tensor(mk_pgb(tc, BH5, BL5))

        def mk_ge2(tc):
            def f(se):
                for dog in range(NG):
                    jg = tc * NG + dog
                    se.wait_ge(ctx.sem("fgb"), F["fgb"] + jg + 1)
                    se.wait_ge(ctx.sem("fap"), F["fap"] + jg)
                    ctx.inc(se.copy(RP5[:, TCW:2 * TCW], ps[1][:, 0:TCW]), "fge2")
            return f
        b.scalar(mk_ge2(tc))

        def mk_ap(tc, par):
            def f(ve):
                for dog in range(NG):
                    jg = tc * NG + dog
                    ve.wait_ge(ctx.sem("fge2"), F["fge2"] + jg + 1)
                    ve.wait_ge(ctx.sem("fbc"), F["fbc"] + tc + 1)
                    gcol = CPF[:, gamc + dog: gamc + dog + 1]
                    src = GOC[:, dog * TCW:(dog + 1) * TCW]
                    sl = np.s_[:, dog * N + tc * TCW: dog * N + (tc + 1) * TCW]
                    ctx.inc(ve.scalar_tensor_tensor(T55[:, 0:TCW], src, gcol,
                                                    RP5[:, 0:TCW], ALU.mult, ALU.mult), "frr")
                    ve.wait_ge(*ctx.tok("frr"))
                    ctx.inc(ve.tensor_add(T55[:, 0:TCW], T55[:, 0:TCW], RP5[:, TCW:2 * TCW]),
                            "frr")
                    ve.wait_ge(*ctx.tok("frr"))
                    ctx.inc(ve.tensor_add(XF[sl], XF[sl], T55[:, 0:TCW]), "fap")
            return f
        b.vector(mk_ap(tc, par))



def host_prep_b(inp):
    d = {
        "b_gwt": np.ascontiguousarray(
            np.asarray(inp["L0_ffn_gate_w"], np.float32).T.reshape(8, 128, 16)
            .transpose(1, 0, 2).reshape(128, 128)),
        "b_be2": np.asarray(inp["L0_ffn_be2"], np.float32),
        "b_we1": np.asarray(inp["L0_ffn_We1"], np.float32),
        "b_we3": np.asarray(inp["L0_ffn_We3"], np.float32),
        "b_we2": np.asarray(inp["L0_ffn_We2"], np.float32),
        "b_ws1": np.asarray(inp["L0_ffn_ws1"], np.float32),
        "b_ws3": np.asarray(inp["L0_ffn_ws3"], np.float32),
        "b_ws2": np.asarray(inp["L0_ffn_ws2"], np.float32),
    }
    return d


def decl_b(nc):
    return {
        "b_gwt": nc.dram_tensor("b_gwt", [128, 128], F32, kind="ExternalInput").ap(),
        "b_be2": nc.dram_tensor("b_be2", [16, D], F32, kind="ExternalInput").ap(),
        "b_we1": nc.dram_tensor("b_we1", [16, D, 512], F32, kind="ExternalInput").ap(),
        "b_we3": nc.dram_tensor("b_we3", [16, D, 512], F32, kind="ExternalInput").ap(),
        "b_we2": nc.dram_tensor("b_we2", [16, 512, D], F32, kind="ExternalInput").ap(),
        "b_ws1": nc.dram_tensor("b_ws1", [D, D], F32, kind="ExternalInput").ap(),
        "b_ws3": nc.dram_tensor("b_ws3", [D, D], F32, kind="ExternalInput").ap(),
        "b_ws2": nc.dram_tensor("b_ws2", [D, D], F32, kind="ExternalInput").ap(),
    }


def pack_b(cpf, cb, inp):
    cpf.add("b_gb", np.broadcast_to(np.asarray(inp["L0_ffn_gate_b"], np.float32), (128, 16)))
    cpf.add("b_be1", np.asarray(inp["L0_ffn_be1"], np.float32).reshape(16 * 4, 128).T)
    cpf.add("b_be3", np.asarray(inp["L0_ffn_be3"], np.float32).reshape(16 * 4, 128).T)
    cpf.add("b_bs1", np.asarray(inp["L0_ffn_bs1"], np.float32).reshape(8, 128).T)
    cpf.add("b_bs3", np.asarray(inp["L0_ffn_bs3"], np.float32).reshape(8, 128).T)
    cpf.add("b_bs2", col_pp(inp["L0_ffn_bs2"]))


def emit_phase_b(ctx, aps, upto=99):
    nc, b = ctx.nc, ctx.block
    es = ctx.es_phase
    XF, XB = ctx.t_["XF"], ctx.t_["XB"]
    CPF, CB = ctx.t_["CPF"], ctx.t_["CB"]
    ci, bi = ctx.t_["cpf_idx"], ctx.t_["cb_idx"]
    BIG = 1e30

    GWT = es.enter_context(nc.sbuf_tensor("b_GWT", [128, 128], F32))
    GWB = es.enter_context(nc.sbuf_tensor("b_GWB", [128, 128], BF16))
    GWL = es.enter_context(nc.sbuf_tensor("b_GWL", [128, 128], BF16))
    XLS = es.enter_context(nc.sbuf_tensor("b_XLS", [128, 2 * 1024], BF16))
    BE2 = es.enter_context(nc.sbuf_tensor("b_BE2", [16, D], BF16))
    SCR = es.enter_context(nc.sbuf_tensor("b_SCR", [128, 256], F32))
    CMB = es.enter_context(nc.sbuf_tensor("b_CMB", [128, NT * 16], BF16))
    CTT = es.enter_context(nc.sbuf_tensor("b_CTT", [16, N], BF16))
    CRP = es.enter_context(nc.sbuf_tensor("b_CRP", [128, 2 * 1024], BF16))
    HS = [es.enter_context(nc.sbuf_tensor(f"b_HS{i}", [128, 4 * 1024], BF16)) for i in range(3)]
    es_w = contextlib.ExitStack()
    es.callback(es_w.close)
    WE1 = [es_w.enter_context(nc.sbuf_tensor(f"b_WE1{i}", [128, NG * 512], BF16)) for i in range(2)]
    WE3 = [es_w.enter_context(nc.sbuf_tensor(f"b_WE3{i}", [128, NG * 512], BF16)) for i in range(2)]
    WE2 = [es_w.enter_context(nc.sbuf_tensor(f"b_WE2{i}", [128, 4 * 1024], BF16)) for i in range(2)]
    ps4 = [es.enter_context(nc.psum_tensor(f"b_ps{i}", [128, TCW], F32)) for i in range(4)]
    psY = [es.enter_context(nc.psum_tensor(f"b_psy{i}", [128, TCW], F32)) for i in range(2)]
    psC = es.enter_context(nc.psum_tensor("b_psc", [128, TCW], F32))
    psT = es.enter_context(nc.psum_tensor("b_pst", [128, 1024], BF16))

    onesrow = CB[0:1, bi["onesrow"]:bi["onesrow"] + 128]
    idn = CB[:, bi["idn"]:bi["idn"] + 128]

    @b.gpsimd
    def _(po):
        ctx.inc(po.dma_start(out=GWT[:, :], in_=aps["b_gwt"][:, :]), "dmw", 16)
        ctx.inc(po.dma_start(out=BE2[0:16, :], in_=aps["b_be2"][:, :]), "dmw", 16)
    b0tok = ctx.tok("dmw")

    BIGc = 1e30
    b_xl = ctx.base("bxl")
    b_rp = ctx.base("brp")
    b_r0 = ctx.base("br0")
    b_r1 = ctx.base("br1")
    b_r2 = ctx.base("br2")

    def rt_pre(t):
        def f(ve):
            if t == 0:
                ve.wait_ge(*b0tok)
                ve.tensor_copy(GWB[:, :], GWT[:, :])
                ve.tensor_sub(GWL[:, :], GWT[:, :], GWB[:, :])
            if t >= 2:
                ve.wait_ge(ctx.sem("brp"), b_rp + t - 1)
            par = t % 2
            for g in range(NG):
                ins = ve.tensor_sub(XLS[:, par * 1024 + g * 128: par * 1024 + (g + 1) * 128],
                                    XF[:, g * N + t * 128: g * N + (t + 1) * 128],
                                    XB[:, g * N + t * 128: g * N + (t + 1) * 128])
            ctx.inc(ins, "bxl")
        return f

    def rt_mm(t):
        def f(te):
            te.wait_ge(ctx.sem("bxl"), b_xl + t + 1)
            if t >= 2:
                te.wait_ge(ctx.sem("br1"), b_r1 + t - 1)
            p = ps4[t % 3]
            for g in range(NG):
                te.matmul(p[:, 0:16], XB[:, g * N + t * 128: g * N + (t + 1) * 128],
                          GWB[:, g * 16:(g + 1) * 16], start=(g == 0), stop=False)
                te.matmul(p[:, 0:16], XB[:, g * N + t * 128: g * N + (t + 1) * 128],
                          GWL[:, g * 16:(g + 1) * 16], start=False, stop=False)
                ins = te.matmul(p[:, 0:16],
                                XLS[:, (t % 2) * 1024 + g * 128: (t % 2) * 1024 + (g + 1) * 128],
                                GWB[:, g * 16:(g + 1) * 16],
                                start=False, stop=(g == NG - 1))
            ctx.inc(ins, "brp")
        return f

    def rt_s0(t):
        def f(ve):
            ve.wait_ge(ctx.sem("brp"), b_rp + t + 1)
            if t >= 2:
                ve.wait_ge(ctx.sem("br2"), b_r2 + t - 1)
            p = ps4[t % 3]
            if TAP == "praw":
                ve.tensor_copy(CMB[:, t * 16:(t + 1) * 16], p[:, 0:16])
            ctx.inc(ve.reduce_max(SCR[:, (t % 2) * 128: (t % 2) * 128 + 1], p[:, 0:16],
                                  AX.X, negate=True), "br0")
        return f

    def rt_s1(t):
        def f(se):
            se.wait_ge(ctx.sem("br0"), b_r0 + t + 1)
            p = ps4[t % 3]
            pb = (t % 2) * 128
            ctx.inc(se.activation(SCR[:, pb + 16: pb + 32], p[:, 0:16], AF.Exp,
                                  bias=SCR[:, pb:pb + 1]), "br1")
        return f

    def rt_s2(t):
        def f(ve):
            ve.wait_ge(ctx.sem("br1"), b_r1 + t + 1)

            def ev(ins):
                ctx.inc(ins, "rtr")
                ve.wait_ge(*ctx.tok("rtr"))

            pb = (t % 2) * 128
            E = SCR[:, pb + 16: pb + 32]
            SM = SCR[:, pb + 1: pb + 2]
            OR = SCR[:, pb + 32: pb + 48]
            S2 = SCR[:, pb + 48: pb + 64]
            PR = SCR[:, pb + 64: pb + 88]
            GS = SCR[:, pb + 88: pb + 92]
            T1 = SCR[:, pb + 92: pb + 96]
            M = SCR[:, pb + 96: pb + 112]
            ev(ve.reduce_sum(SM, E, AX.X))
            ev(ve.reciprocal(SCR[:, pb + 2: pb + 3], SM))
            ev(ve.tensor_scalar_mul(OR, E, SCR[:, pb + 2: pb + 3]))
            ev(ve.tensor_add(S2, OR, CPF[:, ci["b_gb"]:ci["b_gb"] + 16]))
            k = 0
            for a in range(4):
                for c in range(a + 1, 4):
                    ev(ve.tensor_add(PR[:, k * 4:(k + 1) * 4], S2[:, a:16:4], S2[:, c:16:4]))
                    k += 1
            ev(ve.tensor_max(GS, PR[:, 0:4], PR[:, 4:8]))
            ev(ve.tensor_max(GS, GS, PR[:, 8:12]))
            ev(ve.tensor_max(GS, GS, PR[:, 12:16]))
            ev(ve.tensor_max(GS, GS, PR[:, 16:20]))
            ev(ve.tensor_max(GS, GS, PR[:, 20:24]))
            ev(ve.reduce_max(SCR[:, pb + 3: pb + 4], GS, AX.X))
            ev(ve.tensor_scalar(T1, GS, SCR[:, pb + 3: pb + 4], None, ALU.is_equal))
            ev(ve.scalar_tensor_tensor(T1, T1, -BIGc, GS, ALU.mult, ALU.add))
            ev(ve.reduce_max(SCR[:, pb + 4: pb + 5], T1, AX.X))
            ev(ve.tensor_scalar(T1, GS, SCR[:, pb + 4: pb + 5], None, ALU.is_ge))
            ev(ve.tensor_scalar(T1, T1, 1.0, BIGc, ALU.subtract, ALU.mult))
            mv = M.rearrange("p (g j) -> p g j", j=4)
            s2v = S2.rearrange("p (g j) -> p g j", j=4)
            ev(ve.tensor_tensor(mv, s2v, bcast8(T1, 4), ALU.add))
            ev(ve.reduce_max(SCR[:, pb + 5: pb + 6], M, AX.X))
            ev(ve.tensor_scalar(E, M, SCR[:, pb + 5: pb + 6], None, ALU.is_equal))
            ev(ve.scalar_tensor_tensor(E, E, -BIGc, M, ALU.mult, ALU.add))
            ev(ve.reduce_max(SCR[:, pb + 6: pb + 7], E, AX.X))
            ev(ve.tensor_scalar(E, M, SCR[:, pb + 6: pb + 7], None, ALU.is_ge))
            if TAP == "orig":
                ctx.inc(ve.tensor_copy(CMB[:, t * 16:(t + 1) * 16], OR), "br2")
            elif TAP == "selm":
                ctx.inc(ve.tensor_copy(CMB[:, t * 16:(t + 1) * 16], E), "br2")
            elif TAP == "s2m":
                ctx.inc(ve.tensor_copy(CMB[:, t * 16:(t + 1) * 16], S2), "br2")
            else:
                ev(ve.tensor_mul(E, OR, E))
                ctx.inc(ve.tensor_copy(CMB[:, t * 16:(t + 1) * 16], E), "br2")
        return f

    for t in range(NT):
        b.vector(rt_pre(t))
        b.tensor(rt_mm(t))
        b.vector(rt_s0(t))
        b.scalar(rt_s1(t))
        b.vector(rt_s2(t))
    barrier(ctx)
    if TAP in ("cmb", "orig", "selm", "s2m", "praw"):
        emit_tap(ctx, CMB[:, :], NT * 16)
    if TAP == "gwt":
        emit_tap(ctx, GWT[:, :], 128)
    if TAP == "xbt":
        emit_tap(ctx, XB[:, :], NG * N)
    if TAP == "scr":
        emit_tap(ctx, SCR[:, :], 256)
    if TAP == "gwb":
        emit_tap(ctx, GWB[:, :], 128)

    if upto < 2:
        return
    b_tp = ctx.base("btp")
    b_tf = ctx.base("btf")

    @b.tensor
    def _(te):
        for t in range(NT):
            if t >= 2:
                te.wait_ge(ctx.sem("btf"), b_tf + t - 1)
            ctx.inc(te.transpose(psT[0:16, (t % 2) * 128:(t % 2) * 128 + 128],
                                 CMB[:, t * 16:(t + 1) * 16], idn), "btp")

    @b.scalar
    def _(se):
        for t in range(NT):
            se.wait_ge(ctx.sem("btp"), b_tp + t + 1)
            ctx.inc(se.copy(CTT[0:16, t * 128:(t + 1) * 128],
                            psT[0:16, (t % 2) * 128:(t % 2) * 128 + 128]), "btf")
    barrier(ctx)

    if upto < 3:
        return
    items = [(e, hf) for e in range(18) for hf in range(2)]
    if TAP in ("ht0", "y0"):
        items = items[:1]
    nit = len(items)
    HR, YR = 8, 16

    def load_expert(po, e, slot):
        if e < 16:
            ctx.inc(po.dma_start(
                out=WE1[slot][:, 0:NG * 512].rearrange("p (k c) -> p k c", c=512),
                in_=aps["b_we1"][e].rearrange("(k p) c -> p k c", p=128)), "dmw", 16)
            ctx.inc(po.dma_start(
                out=WE3[slot][:, 0:NG * 512].rearrange("p (k c) -> p k c", c=512),
                in_=aps["b_we3"][e].rearrange("(k p) c -> p k c", p=128)), "dmw", 16)
            ctx.inc(po.dma_start(
                out=WE2[slot][:, 0:4 * 1024].rearrange("p (k c) -> p k c", c=1024),
                in_=aps["b_we2"][e].rearrange("(k p) c -> p k c", p=128)), "dmw", 16)
        else:
            hh = e - 16
            ctx.inc(po.dma_start(
                out=WE1[slot][:, 0:NG * 512].rearrange("p (k c) -> p k c", c=512),
                in_=aps["b_ws1"][:, hh * 512:(hh + 1) * 512].rearrange(
                    "(k p) c -> p k c", p=128)), "dmw", 16)
            ctx.inc(po.dma_start(
                out=WE3[slot][:, 0:NG * 512].rearrange("p (k c) -> p k c", c=512),
                in_=aps["b_ws3"][:, hh * 512:(hh + 1) * 512].rearrange(
                    "(k p) c -> p k c", p=128)), "dmw", 16)
            ctx.inc(po.dma_start(
                out=WE2[slot][:, 0:4 * 1024].rearrange("p (k c) -> p k c", c=1024),
                in_=aps["b_ws2"][hh * 512:(hh + 1) * 512, :].rearrange(
                    "(k p) c -> p k c", p=128)), "dmw", 16)
        return ctx.tok("dmw")

    wtoks = {}

    @b.gpsimd
    def _(po):
        wtoks[0] = load_expert(po, 0, 0)
        wtoks[1] = load_expert(po, 1, 1)

    B3 = {k: ctx.base(k) for k in ["bcp", "bcf", "bh1", "bh3", "bsa", "bsd",
                                   "bht", "byp", "byf"]}
    crp_ix = {}
    nci = 0
    for i, (e, hf) in enumerate(items):
        if e < 16:
            crp_ix[i] = nci
            nci += 1

    def emit_y_rounds(te, i):
        e, hf = items[i]
        t0 = hf * 1024
        te.wait_ge(ctx.sem("bht"), B3["bht"] + (i + 1) * HR)
        for r in range(YR):
            jy = i * YR + r
            dog, tcc = divmod(r, 2)
            if jy >= 2:
                te.wait_ge(ctx.sem("byf"), B3["byf"] + jy - 1)
            pb = psY[jy % 2]
            for kg in range(4):
                ins = te.matmul(pb[:, 0:TCW],
                                WE2[e % 2][:, kg * 1024 + dog * 128: kg * 1024 + (dog + 1) * 128],
                                HS[i % 3][:, kg * 1024 + tcc * TCW: kg * 1024 + (tcc + 1) * TCW],
                                start=(kg == 0), stop=(kg == 3 and e != 0))
            if e == 0:
                ins = te.matmul(pb[:, 0:TCW], BE2[0:16, dog * 128:(dog + 1) * 128],
                                CTT[0:16, t0 + tcc * TCW: t0 + (tcc + 1) * TCW],
                                start=False, stop=True)
            ctx.inc(ins, "byp")

    def emit_y_evacs(ve, i):
        ep, hp = items[i]
        t0p = hp * 1024
        for r in range(YR):
            jy = i * YR + r
            dog, tcc = divmod(r, 2)
            ve.wait_ge(ctx.sem("byp"), B3["byp"] + jy + 1)
            slx = np.s_[:, dog * N + t0p + tcc * TCW: dog * N + t0p + (tcc + 1) * TCW]
            if ep == 16:
                ins = ve.scalar_tensor_tensor(
                    XF[slx], psY[jy % 2][:, 0:TCW],
                    CPF[:, ci["b_bs2"] + dog: ci["b_bs2"] + dog + 1],
                    XF[slx], ALU.add, ALU.add)
            else:
                ins = ve.tensor_add(XF[slx], XF[slx], psY[jy % 2][:, 0:TCW])
            ctx.inc(ins, "byf")

    for i, (e, hf) in enumerate(items):
        t0 = hf * 1024

        def mk_pe(i, e, hf, t0):
            def f(te):
                if e < 16:
                    for cc in range(2):
                        jc = 2 * crp_ix[i] + cc
                        if jc >= 1:
                            te.wait_ge(ctx.sem("bcf"), B3["bcf"] + jc)
                        sel = idn[0:16, e:e + 1].broadcast_to((16, 128))
                        ctx.inc(te.matmul(psC[:, 0:TCW], sel,
                                          CTT[0:16, t0 + cc * TCW: t0 + (cc + 1) * TCW],
                                          start=True, stop=True), "bcp")
                if hf == 0:
                    te.wait_ge(*wtoks[e])
                for r in range(HR):
                    j = i * HR + r
                    dog4, tcc = divmod(r, 2)
                    if j >= 2:
                        te.wait_ge(ctx.sem("bsa"), B3["bsa"] + j - 1)
                    for kg in range(NG):
                        ins = te.matmul(ps4[r % 2][:, 0:TCW],
                                        WE1[e % 2][:, kg * 512 + dog4 * 128: kg * 512 + (dog4 + 1) * 128],
                                        XB[:, kg * N + t0 + tcc * TCW: kg * N + t0 + (tcc + 1) * TCW],
                                        start=(kg == 0), stop=(kg == NG - 1))
                    ctx.inc(ins, "bh1")
                    if j >= 2:
                        te.wait_ge(ctx.sem("bsd"), B3["bsd"] + j - 1)
                    for kg in range(NG):
                        ins = te.matmul(ps4[2 + r % 2][:, 0:TCW],
                                        WE3[e % 2][:, kg * 512 + dog4 * 128: kg * 512 + (dog4 + 1) * 128],
                                        XB[:, kg * N + t0 + tcc * TCW: kg * N + t0 + (tcc + 1) * TCW],
                                        start=(kg == 0), stop=(kg == NG - 1))
                    ctx.inc(ins, "bh3")
                if i >= 1:
                    emit_y_rounds(te, i - 1)
                if i == nit - 1:
                    emit_y_rounds(te, i)
            return f
        b.tensor(mk_pe(i, e, hf, t0))

        def mk_se(i, e, hf, t0):
            def f(se):
                if e < 16:
                    for cc in range(2):
                        jc = 2 * crp_ix[i] + cc
                        se.wait_ge(ctx.sem("bcp"), B3["bcp"] + jc + 1)
                        if i >= 2:
                            se.wait_ge(ctx.sem("bsd"), B3["bsd"] + (i - 1) * HR)
                        ctx.inc(se.copy(CRP[:, (i % 2) * 1024 + cc * TCW:
                                             (i % 2) * 1024 + (cc + 1) * TCW],
                                        psC[:, 0:TCW]), "bcf")
                for r in range(HR):
                    j = i * HR + r
                    dog4, tcc = divmod(r, 2)
                    se.wait_ge(ctx.sem("bh1"), B3["bh1"] + j + 1)
                    if r == 0 and i >= 1:
                        se.wait_ge(ctx.sem("bht"), B3["bht"] + i * HR)
                    if r == 0 and i >= 3:
                        se.wait_ge(ctx.sem("byp"), B3["byp"] + (i - 2) * YR)
                    if e < 16:
                        bc_col = CPF[:, ci["b_be1"] + e * 4 + dog4: ci["b_be1"] + e * 4 + dog4 + 1]
                    else:
                        bc_col = CPF[:, ci["b_bs1"] + (e - 16) * 4 + dog4:
                                     ci["b_bs1"] + (e - 16) * 4 + dog4 + 1]
                    sl = np.s_[:, dog4 * 1024 + tcc * TCW: dog4 * 1024 + (tcc + 1) * TCW]
                    ctx.inc(se.activation(HS[i % 3][sl], ps4[r % 2][:, 0:TCW], AF.Silu,
                                          bias=bc_col), "bsa")
            return f
        b.scalar(mk_se(i, e, hf, t0))

        def mk_ve(i, e, hf, t0):
            def f(ve):
                for r in range(HR):
                    j = i * HR + r
                    dog4, tcc = divmod(r, 2)
                    ve.wait_ge(ctx.sem("bh3"), B3["bh3"] + j + 1)
                    if r == 0 and i >= 2:
                        ve.wait_ge(ctx.sem("byp"), B3["byp"] + (i - 1) * YR)
                    sl = np.s_[:, dog4 * 1024 + tcc * TCW: dog4 * 1024 + (tcc + 1) * TCW]
                    if e < 16:
                        if r == 0:
                            ve.wait_ge(ctx.sem("bcf"), B3["bcf"] + 2 * (crp_ix[i] + 1))
                        b3c = CPF[:, ci["b_be3"] + e * 4 + dog4: ci["b_be3"] + e * 4 + dog4 + 1]
                        ins = ve.scalar_tensor_tensor(
                            HS[(i + 1) % 3][sl], ps4[2 + r % 2][:, 0:TCW], b3c,
                            CRP[:, (i % 2) * 1024 + tcc * TCW: (i % 2) * 1024 + (tcc + 1) * TCW],
                            ALU.add, ALU.mult)
                    else:
                        b3c = CPF[:, ci["b_bs3"] + (e - 16) * 4 + dog4:
                                  ci["b_bs3"] + (e - 16) * 4 + dog4 + 1]
                        ins = ve.tensor_scalar_add(HS[(i + 1) % 3][sl],
                                                   ps4[2 + r % 2][:, 0:TCW], b3c)
                    ctx.inc(ins, "bsd")
                if i >= 1:
                    emit_y_evacs(ve, i - 1)
                if i == nit - 1:
                    emit_y_evacs(ve, i)
            return f
        b.vector(mk_ve(i, e, hf, t0))

        def mk_po(i, e, hf):
            def f(po):
                for r in range(HR):
                    j = i * HR + r
                    dog4, tcc = divmod(r, 2)
                    po.wait_ge(ctx.sem("bsa"), B3["bsa"] + j + 1)
                    po.wait_ge(ctx.sem("bsd"), B3["bsd"] + j + 1)
                    sl = np.s_[:, dog4 * 1024 + tcc * TCW: dog4 * 1024 + (tcc + 1) * TCW]
                    sA = HS[i % 3]
                    sB = HS[(i + 1) % 3]
                    ctx.inc(po.tensor_mul(sA[sl], sA[sl], sB[sl]), "bht")
                if hf == 1 and e + 2 < 18:
                    po.wait_ge(ctx.sem("byp"), B3["byp"] + (i + 1) * YR)
                    wtoks[e + 2] = load_expert(po, e + 2, e % 2)
            return f
        b.gpsimd(mk_po(i, e, hf))

    barrier(ctx)
    if TAP == "ht0":
        emit_tap(ctx, HS[0][:, :], 4 * 1024)
    es_w.close()

    if upto < 4:
        return
    emit_ln(ctx, es, "lnb", ps4)
    barrier(ctx)




def host_prep_d(inp):
    w1 = np.asarray(inp["L1_ffn_w1"], np.float32)
    w2 = np.asarray(inp["L1_ffn_w2"], np.float32)
    w1a = np.zeros((2, D, 2816), np.float32)
    w1g = np.zeros((2, D, 2816), np.float32)
    w2p = np.zeros((2, 2816, D), np.float32)
    w1a[:, :, :2730] = w1[:, :, :2730]
    w1g[:, :, :2730] = w1[:, :, 2730:]
    w2p[:, :2730, :] = w2
    DH, WIN = 64, 512
    half = np.arange(0, DH, 2, dtype=np.float32)
    inv_freq = 1.0 / 10000.0 ** (half / DH)
    t_idx = np.arange(N, dtype=np.float32)
    freqs = np.concatenate([t_idx[:, None] * inv_freq[None, :]] * 2, -1)
    base = (half + 0.4 * DH) / (1.4 * DH)
    power = (t_idx - N // 2) / (WIN // 2)
    scale = np.concatenate([base[None, :] ** power[:, None]] * 2, -1)
    cosf, sinf = np.cos(freqs), np.sin(freqs)
    sgn = np.concatenate([-np.ones(32, np.float32), np.ones(32, np.float32)])
    def tm(a):
        return np.ascontiguousarray(
            a.reshape(NT, 128, 64).transpose(1, 0, 2).reshape(128, NT * 64), dtype=np.float32)
    return {
        "d_wqkv": np.asarray(inp["L1_ffn_wqkv"], np.float32),
        "d_wo": np.asarray(inp["L1_ffn_wo"], np.float32),
        "d_w1a": w1a, "d_w1g": w1g, "d_w2": w2p,
        "d_rqa": tm(cosf * scale), "d_rqb": tm(sinf * scale * sgn),
        "d_rka": tm(cosf / scale), "d_rkb": tm(sinf / scale * sgn),
    }


def decl_d(nc):
    return {
        "d_wqkv": nc.dram_tensor("d_wqkv", [2, D, 1536], F32, kind="ExternalInput").ap(),
        "d_wo": nc.dram_tensor("d_wo", [2, 512, D], F32, kind="ExternalInput").ap(),
        "d_w1a": nc.dram_tensor("d_w1a", [2, D, 2816], F32, kind="ExternalInput").ap(),
        "d_w1g": nc.dram_tensor("d_w1g", [2, D, 2816], F32, kind="ExternalInput").ap(),
        "d_w2": nc.dram_tensor("d_w2", [2, 2816, D], F32, kind="ExternalInput").ap(),
        "d_rqa": nc.dram_tensor("d_rqa", [128, NT * 64], F32, kind="ExternalInput").ap(),
        "d_rqb": nc.dram_tensor("d_rqb", [128, NT * 64], F32, kind="ExternalInput").ap(),
        "d_rka": nc.dram_tensor("d_rka", [128, NT * 64], F32, kind="ExternalInput").ap(),
        "d_rkb": nc.dram_tensor("d_rkb", [128, NT * 64], F32, kind="ExternalInput").ap(),
    }


def pack_d(cpf, cb, inp):
    qs = np.asarray(inp["L1_ffn_q_scale"], np.float32)
    ks = np.asarray(inp["L1_ffn_k_scale"], np.float32)
    def rowpack(v):
        r = np.zeros((128, v.size), np.float32)
        r[0, :] = v.reshape(-1)
        return r
    sw = np.concatenate([np.arange(32, 64), np.arange(0, 32)])
    cb.add("d_qsr", rowpack(np.concatenate([qs[0], qs[1]])))
    cb.add("d_qsw", rowpack(np.concatenate([qs[0][sw], qs[1][sw]])))
    cb.add("d_ksr", rowpack(np.concatenate([ks[0], ks[1]])))
    cb.add("d_ksw", rowpack(np.concatenate([ks[0][sw], ks[1][sw]])))
    lg = np.asarray(inp["L1_ffn_ln_g"], np.float32)
    lb = np.asarray(inp["L1_ffn_ln_b"], np.float32)
    cpf.add("d_lg", np.concatenate([col_pp(lg[0]), col_pp(lg[1])], 1))
    cpf.add("d_lb", np.concatenate([col_pp(lb[0]), col_pp(lb[1])], 1))
    b1 = np.asarray(inp["L1_ffn_b1"], np.float32)
    b1a = np.zeros((2, 2816), np.float32)
    b1g = np.zeros((2, 2816), np.float32)
    b1a[:, :2730] = b1[:, :2730]
    b1g[:, :2730] = b1[:, 2730:]
    cpf.add("d_b1a", np.concatenate([b1a[0].reshape(22, 128).T, b1a[1].reshape(22, 128).T], 1))
    cpf.add("d_b1g", np.concatenate([b1g[0].reshape(22, 128).T, b1g[1].reshape(22, 128).T], 1))
    b2 = np.asarray(inp["L1_ffn_b2"], np.float32)
    cpf.add("d_b2", np.concatenate([col_pp(b2[0]), col_pp(b2[1])], 1))
    m0 = np.where(np.arange(128)[:, None] >= np.arange(128)[None, :], 0.0, -1e30)
    m4 = np.where(np.arange(128)[:, None] <= np.arange(128)[None, :], 0.0, -1e30)
    cpf.add("d_m0", m0)
    cpf.add("d_m4", m4)


def emit_ln_src(ctx, es_or_bufs, tag, banks, hsrc, dst, lgc, lbc, eps=1e-5):
    nc, b = ctx.nc, ctx.block
    CB = ctx.t_["CB"]
    bi = ctx.t_["cb_idx"]
    onesb = CB[:, bi["ones"]:bi["ones"] + 1]
    ones_row_b = CB[0:1, bi["onesrow"]:bi["onesrow"] + 128]
    SB, AB, ABL, REP, T5 = es_or_bufs
    psS = banks[0][0:1, 0:TCW]
    psQ = banks[1][0:1, 0:TCW]
    psR = banks[2]
    psR2 = banks[3]
    L = {k: ctx.base(tag + k) for k in ["sq", "mm", "rm", "rs", "rd", "bc", "ap", "pb", "rr"]}

    for tc in range(NTC):
        par2 = tc % 2

        def mk_se(tc):
            def f(se):
                for p8 in range(2):
                    for g in range(NG):
                        j = tc * 16 + p8 * 8 + g
                        if j >= 1:
                            se.wait_ge(ctx.sem(tag + "mm"), L["mm"] + j)
                        src = hsrc(g, tc)
                        if p8 == 0:
                            ins = se.copy(SB[:, 0:TCW], src)
                        else:
                            ins = se.activation(SB[:, 0:TCW], src, AF.Square)
                        ctx.inc(ins, tag + "sq")
            return f
        b.scalar(mk_se(tc))

        def mk_te(tc):
            def f(te):
                for p8 in range(2):
                    dps = psQ if p8 else psS
                    for g in range(NG):
                        j = tc * 16 + p8 * 8 + g
                        te.wait_ge(ctx.sem(tag + "sq"), L["sq"] + j + 1)
                        if p8 == 0 and g == 0 and tc >= 1:
                            te.wait_ge(ctx.sem(tag + "rd"), L["rd"] + tc)
                        ctx.inc(te.matmul(dps, onesb, SB[:, 0:TCW],
                                          start=(g == 0), stop=(g == NG - 1)), tag + "mm")
            return f
        b.tensor(mk_te(tc))

        A = AB[0:1, 0:TCW]
        B = AB[0:1, TCW:2 * TCW]
        AH = ABL[0:1, 0:TCW]
        AL = ABL[0:1, TCW:2 * TCW]
        BH = ABL[0:1, 2 * TCW:3 * TCW]
        BL = ABL[0:1, 3 * TCW:4 * TCW]

        def mk_rm(tc, A, B):
            def f(ve):
                ve.wait_ge(ctx.sem(tag + "mm"), L["mm"] + tc * 16 + 16)
                if tc >= 1:
                    ve.wait_ge(ctx.sem(tag + "pb"), L["pb"] + tc)
                ve.tensor_scalar_mul(B, psS, 1.0 / D)
                ve.tensor_scalar_mul(A, psQ, 1.0 / D)
                ve.tensor_mul(psS, B, B)
                ve.tensor_sub(A, A, psS)
                ctx.inc(ve.tensor_scalar_add(A, A, eps), tag + "rm")
            return f
        b.vector(mk_rm(tc, A, B))

        def mk_rs(tc, A):
            def f(se):
                se.wait_ge(ctx.sem(tag + "rm"), L["rm"] + tc + 1)
                ctx.inc(se.activation(A, A, AF.Sqrt), tag + "rs")
            return f
        b.scalar(mk_rs(tc, A))

        def mk_rd(tc, A, B, AH, AL, BH, BL):
            def f(ve):
                ve.wait_ge(ctx.sem(tag + "rs"), L["rs"] + tc + 1)
                ctx.inc(ve.reciprocal(A, A), tag + "rr")
                ve.wait_ge(*ctx.tok(tag + "rr"))
                ve.scalar_tensor_tensor(B, B, -1.0, A, ALU.mult, ALU.mult)
                ve.tensor_copy(AH, A)
                ve.tensor_sub(AL, A, AH)
                ve.tensor_copy(BH, B)
                ctx.inc(ve.tensor_sub(BL, B, BH), tag + "rd")
            return f
        b.vector(mk_rd(tc, A, B, AH, AL, BH, BL))

        def mk_pb(tc, AH, AL, BH, BL):
            def f(te):
                te.wait_ge(ctx.sem(tag + "rd"), L["rd"] + tc + 1)
                if tc >= 1:
                    te.wait_ge(ctx.sem(tag + "bc"), L["bc"] + tc)
                te.matmul(psR[:, 0:TCW], ones_row_b, AH, start=True, stop=False)
                te.matmul(psR[:, 0:TCW], ones_row_b, AL, start=False, stop=True)
                te.matmul(psR2[:, 0:TCW], ones_row_b, BH, start=True, stop=False)
                ctx.inc(te.matmul(psR2[:, 0:TCW], ones_row_b, BL, start=False, stop=True),
                        tag + "pb")
            return f
        b.tensor(mk_pb(tc, AH, AL, BH, BL))

        def mk_bc(tc):
            def f(se):
                se.wait_ge(ctx.sem(tag + "pb"), L["pb"] + tc + 1)
                if tc >= 1:
                    se.wait_ge(ctx.sem(tag + "ap"), L["ap"] + tc * NG)
                se.copy(REP[:, 0:TCW], psR[:, 0:TCW])
                ctx.inc(se.copy(REP[:, TCW:2 * TCW], psR2[:, 0:TCW]), tag + "bc")
            return f
        b.scalar(mk_bc(tc))

        def mk_ap(tc):
            def f(ve):
                ve.wait_ge(ctx.sem(tag + "bc"), L["bc"] + tc + 1)
                for g in range(NG):
                    src = hsrc(g, tc)
                    ve.tensor_mul(T5[:, 0:TCW], src, REP[:, 0:TCW])
                    ctx.inc(ve.tensor_add(T5[:, 0:TCW], T5[:, 0:TCW], REP[:, TCW:2 * TCW]),
                            tag + "rr")
                    ve.wait_ge(*ctx.tok(tag + "rr"))
                    ctx.inc(ve.tensor_scalar(dst[:, g * N + tc * TCW: g * N + (tc + 1) * TCW],
                                             T5[:, 0:TCW], lgc(g), lbc(g),
                                             ALU.mult, ALU.add), tag + "ap")
            return f
        b.vector(mk_ap(tc))




def emit_phase_d(ctx, aps, upto=99):
    nc, b = ctx.nc, ctx.block
    es = ctx.es_phase
    XF, XB = ctx.t_["XF"], ctx.t_["XB"]
    CPF, CB = ctx.t_["CPF"], ctx.t_["CB"]
    ci, bi = ctx.t_["cpf_idx"], ctx.t_["cb_idx"]
    idn = CB[:, bi["idn"]:bi["idn"] + 128]
    onesb = CB[:, bi["ones"]:bi["ones"] + 1]
    ones_row_b = CB[0:1, bi["onesrow"]:bi["onesrow"] + 128]

    HBSm = es.enter_context(nc.sbuf_tensor("d_HBSm", [128, 4 * TCW], BF16))
    XOR_T = es.enter_context(nc.sbuf_tensor("d_XORT", [128, TCW], F32))
    ps = [es.enter_context(nc.psum_tensor(f"d_ps{i}", [128, TCW], F32)) for i in range(4)]
    psY = [es.enter_context(nc.psum_tensor(f"d_psy{i}", [128, TCW], F32)) for i in range(2)]
    xorig = nc.dram_tensor("d_xorig", [128, NG * N], F32).ap()

    @b.sync
    def _(sy):
        for g in range(NG):
            ctx.inc(sy.dma_start(out=xorig[:, g * N:(g + 1) * N],
                                 in_=XF[:, g * N:(g + 1) * N]), "dxo", 16)
    xotok = ctx.tok("dxo")
    barrier(ctx)

    def hsl(kg, t0, w):
        if kg < 4:
            return XB[:, kg * N + t0: kg * N + t0 + w]
        if t0 > 0:
            return XB[:, kg * N + t0 - 1: kg * N + t0 - 1 + w]
        return HBSm[:, (kg - 4) * TCW: (kg - 4) * TCW + w]

    for l in range(2):
        sfx = "LD"
        DD = {k: ctx.base(sfx + k) for k in
              ["qs", "q2", "sh", "qk", "qe", "sq", "sr", "nr", "rr", "tp", "tf",
               "sc", "pxm", "pxs", "pm", "px", "pe", "pt", "pv", "ov",
               "wo", "we", "xb", "s2", "aa", "ae", "gg", "ge", "tt",
               "yp", "ye", "x2"]}

        es_at = contextlib.ExitStack()
        es.callback(es_at.close)
        VTok = es_at.enter_context(nc.sbuf_tensor(f"d_VTok_{l}", [128, NT * TCW], BF16))
        QT2 = XB[:, 0:4 * N]
        KT2 = XB[:, 4 * N:8 * N]
        PB1 = es_at.enter_context(nc.sbuf_tensor(f"d_PB1_{l}", [128, 2 * 640], BF16))
        PT1 = es_at.enter_context(nc.sbuf_tensor(f"d_PT1_{l}", [128, 2 * 640], BF16))
        RMX = es_at.enter_context(nc.sbuf_tensor(f"d_RMX_{l}", [128, 16], F32))
        QSR = es_at.enter_context(nc.sbuf_tensor(f"d_QSR_{l}", [128, 256], BF16))
        psTb = [es_at.enter_context(nc.psum_tensor(f"d_pst{i}_{l}", [128, 1024], BF16))
                for i in range(2)]
        es_qk = contextlib.ExitStack()
        es_at.callback(es_qk.close)
        WQKV = es_qk.enter_context(nc.sbuf_tensor(f"d_WQKV_{l}", [128, NG * 1536], BF16))
        QTok = es_qk.enter_context(nc.sbuf_tensor(f"d_QTok_{l}", [128, NT * TCW], BF16))
        KTok = es_qk.enter_context(nc.sbuf_tensor(f"d_KTok_{l}", [128, NT * TCW], BF16))
        NSC = es_qk.enter_context(nc.sbuf_tensor(f"d_NSC_{l}", [128, 4 * TCW], BF16))
        NRM = es_qk.enter_context(nc.sbuf_tensor(f"d_NRM_{l}", [128, 32], F32))
        RQA = es_qk.enter_context(nc.sbuf_tensor(f"d_RQA_{l}", [128, NT * 64], BF16))
        RQB = es_qk.enter_context(nc.sbuf_tensor(f"d_RQB_{l}", [128, NT * 64], BF16))
        RKA = es_qk.enter_context(nc.sbuf_tensor(f"d_RKA_{l}", [128, NT * 64], BF16))
        RKB = es_qk.enter_context(nc.sbuf_tensor(f"d_RKB_{l}", [128, NT * 64], BF16))

        @b.gpsimd
        def _(po):
            for nm, t in [("d_rqa", RQA), ("d_rqb", RQB), ("d_rka", RKA), ("d_rkb", RKB)]:
                ctx.inc(po.dma_start(out=t[:, :], in_=aps[nm][:, :]), "dmw", 16)
            ctx.inc(po.dma_start(
                out=WQKV[:, 0:NG * 1536].rearrange("p (k c) -> p k c", c=1536),
                in_=aps["d_wqkv"][l].rearrange("(k p) c -> p k c", p=128)), "dmw", 16)
        qkvtok = ctx.tok("dmw")
        barrier(ctx)

        def mk_qs(l):
            def f(te):
                te.wait_ge(*qkvtok)
                lq = bi["d_qsr"] + l * 64
                lw = bi["d_qsw"] + l * 64
                lk = bi["d_ksr"] + l * 64
                lx = bi["d_ksw"] + l * 64
                te.matmul(ps[0][:, 0:64], ones_row_b, CB[0:1, lq:lq + 64],
                          start=True, stop=True)
                te.matmul(ps[0][:, 64:128], ones_row_b, CB[0:1, lw:lw + 64],
                          start=False, stop=True, skip_group_check=True)
                te.matmul(ps[0][:, 128:192], ones_row_b, CB[0:1, lk:lk + 64],
                          start=False, stop=True, skip_group_check=True)
                ctx.inc(te.matmul(ps[0][:, 192:256], ones_row_b, CB[0:1, lx:lx + 64],
                                  start=False, stop=True, skip_group_check=True), sfx + "qs")
            return f
        b.tensor(mk_qs(l))

        def mk_qs2(l):
            def f(se):
                se.wait_ge(ctx.sem(sfx + "qs"), DD["qs"] + 1)
                ctx.inc(se.copy(QSR[:, 0:256], ps[0][:, 0:256]), sfx + "q2")
            return f
        b.scalar(mk_qs2(l))

        def mk_qs3(l):
            def f(ve):
                ve.wait_ge(ctx.sem(sfx + "q2"), DD["q2"] + 1)
                ins = None
                for t, c0 in [(RQA, 0), (RQB, 64), (RKA, 128), (RKB, 192)]:
                    rv = t[:, :].rearrange("p (t d) -> p t d", d=64)
                    sv = QSR[:, c0:c0 + 64].unsqueeze(1).broadcast_to((128, NT, 64))
                    ins = ve.tensor_tensor(rv, rv, sv, ALU.mult)
                for g in range(4):
                    ve.memset(HBSm[:, g * TCW: g * TCW + 1], 0.0)
                    ins = ve.tensor_copy(HBSm[:, g * TCW + 1:(g + 1) * TCW],
                                         XB[:, (g + 4) * N:(g + 4) * N + TCW - 1])
                ctx.inc(ins, sfx + "sh")
            return f
        b.vector(mk_qs3(l))
        barrier(ctx)

        def mk_qkv(l):
            def f(te):
                for r in range(48):
                    m, t = divmod(r, 16)
                    if r >= 3:
                        te.wait_ge(ctx.sem(sfx + "qe"), DD["qe"] + r - 2)
                    for kg in range(NG):
                        ins = te.matmul(ps[r % 3][:, 0:TCW],
                                        hsl(kg, t * 128, 128),
                                        WQKV[:, kg * 1536 + m * 512: kg * 1536 + (m + 1) * 512],
                                        start=(kg == 0), stop=(kg == NG - 1))
                    ctx.inc(ins, sfx + "qk")
            return f
        b.tensor(mk_qkv(l))

        def mk_qkve(l):
            def f(ve):
                for r in range(48):
                    m, t = divmod(r, 16)
                    ve.wait_ge(ctx.sem(sfx + "qk"), DD["qk"] + r + 1)
                    dst = [QTok, KTok, VTok][m]
                    ctx.inc(ve.tensor_copy(dst[:, t * TCW:(t + 1) * TCW], ps[r % 3][:, 0:TCW]),
                            sfx + "qe")
            return f
        b.vector(mk_qkve(l))
        barrier(ctx)

        def mk_sq(l):
            def f(se):
                for j in range(32):
                    t, qi = divmod(j, 2)
                    tok = [QTok, KTok][qi]
                    if j >= 2:
                        se.wait_ge(ctx.sem(sfx + "nr"), DD["nr"] + j - 1)
                    ctx.inc(se.activation(NSC[:, (j % 2) * TCW:((j % 2) + 1) * TCW],
                                          tok[:, t * TCW:(t + 1) * TCW], AF.Square), sfx + "sq")
                    se.wait_ge(ctx.sem(sfx + "rr"), DD["rr"] + j * 5 + 2)
                    ctx.inc(se.activation(NRM[:, 16:24], NRM[:, 8:16], AF.Sqrt), sfx + "sr")
            return f
        b.scalar(mk_sq(l))

        def mk_nr(l):
            def f(ve):
                for j in range(32):
                    t, qi = divmod(j, 2)
                    tok = [QTok, KTok][qi]
                    RA = [RQA, RKA][qi]
                    RB = [RQB, RKB][qi]
                    tile = tok[:, t * TCW:(t + 1) * TCW]
                    tv = tile.rearrange("p (h d) -> p h d", d=64)
                    ve.wait_ge(ctx.sem(sfx + "sq"), DD["sq"] + j + 1)
                    sqv = NSC[:, (j % 2) * TCW:((j % 2) + 1) * TCW].rearrange(
                        "p (h d) -> p h d", d=64)
                    ctx.inc(ve.tensor_reduce(NRM[:, 0:8], sqv, AX.X, ALU.add), sfx + "rr")
                    ve.wait_ge(*ctx.tok(sfx + "rr"))
                    ctx.inc(ve.tensor_scalar_max(NRM[:, 8:16], NRM[:, 0:8], 1e-24), sfx + "rr")
                    ve.wait_ge(ctx.sem(sfx + "sr"), DD["sr"] + j + 1)
                    ctx.inc(ve.reciprocal(NRM[:, 24:32], NRM[:, 16:24]), sfx + "rr")
                    ve.wait_ge(*ctx.tok(sfx + "rr"))
                    TN = NSC[:, 2 * TCW:3 * TCW]
                    tnv = TN.rearrange("p (h d) -> p h d", d=64)
                    ctx.inc(ve.tensor_tensor(tnv, tv, bcast8(NRM[:, 24:32], 64), ALU.mult),
                            sfx + "rr")
                    ve.wait_ge(*ctx.tok(sfx + "rr"))
                    T2 = NSC[:, 3 * TCW:4 * TCW]
                    t2v = T2.rearrange("p (h d) -> p h d", d=64)
                    ra = RA[:, t * 64:(t + 1) * 64].unsqueeze(1).broadcast_to((128, 8, 64))
                    ve.tensor_tensor(t2v, tnv, ra, ALU.mult)
                    rblo = RB[:, t * 64:t * 64 + 32].unsqueeze(1).broadcast_to((128, 8, 32))
                    rbhi = RB[:, t * 64 + 32:(t + 1) * 64].unsqueeze(1).broadcast_to((128, 8, 32))
                    ve.tensor_tensor(tv[:, :, 0:32], tnv[:, :, 32:64], rblo, ALU.mult)
                    ctx.inc(ve.tensor_tensor(tv[:, :, 32:64], tnv[:, :, 0:32], rbhi, ALU.mult),
                            sfx + "rr")
                    ve.wait_ge(*ctx.tok(sfx + "rr"))
                    ctx.inc(ve.tensor_add(tile, tile, T2), sfx + "nr")
            return f
        b.vector(mk_nr(l))
        barrier(ctx)

        def mk_tp(l):
            def f(te):
                for j in range(128):
                    t, og = divmod(j, 8)
                    qi, og4 = divmod(og, 4)
                    tok = [QTok, KTok][qi]
                    if j >= 2:
                        te.wait_ge(ctx.sem(sfx + "tf"), DD["tf"] + j - 1)
                    ctx.inc(te.transpose(psTb[j % 2][:, 0:128],
                                         tok[:, t * TCW + og4 * 128: t * TCW + (og4 + 1) * 128],
                                         idn), sfx + "tp")
            return f
        b.tensor(mk_tp(l))

        def mk_tf(l):
            def f(se):
                for j in range(128):
                    t, og = divmod(j, 8)
                    qi, og4 = divmod(og, 4)
                    dst = [QT2, KT2][qi]
                    se.wait_ge(ctx.sem(sfx + "tp"), DD["tp"] + j + 1)
                    ctx.inc(se.copy(dst[:, og4 * N + t * 128: og4 * N + (t + 1) * 128],
                                    psTb[j % 2][:, 0:128]), sfx + "tf")
            return f
        b.scalar(mk_tf(l))
        barrier(ctx)
        es_qk.close()
        ODT = es_at.enter_context(nc.sbuf_tensor(f"d_ODT_{l}", [128, 4 * N], BF16))
        WO2 = es_at.enter_context(nc.sbuf_tensor(f"d_WO2_{l}", [128, 4 * D], BF16))

        @b.gpsimd
        def _(po):
            for kg in range(4):
                ctx.inc(po.dma_start(out=WO2[:, kg * D:(kg + 1) * D],
                                     in_=aps["d_wo"][l, kg * 128:(kg + 1) * 128, :]), "dmw", 16)
        wotok = ctx.tok("dmw")

        items5 = [(h, qc) for h in range(8) for qc in range(NT)]
        m0c = ci["d_m0"]
        m4c = ci["d_m4"]

        def mk_at_pe(l):
            def f(te):
                for i5, (h, qc) in enumerate(items5):
                    par = i5 % 2
                    r0 = (h % 2) * 64
                    g4 = h // 2
                    k0 = max(0, qc - 4)
                    nk = qc - k0 + 1
                    bX = ps[0] if h % 2 == 0 else ps[1]
                    bY = ps[2] if h % 2 == 0 else ps[3]
                    if i5 >= 2:
                        te.wait_ge(ctx.sem(sfx + "px"), DD["px"] + i5 - 1)
                    for ix, kc in enumerate(range(k0, qc + 1)):
                        dstp = bX[:, ix * 128:(ix + 1) * 128] if ix < 4 else bY[:, 0:128]
                        ins = te.matmul(dstp,
                                        QT2[r0:r0 + 64, g4 * N + qc * 128: g4 * N + (qc + 1) * 128],
                                        KT2[r0:r0 + 64, g4 * N + kc * 128: g4 * N + (kc + 1) * 128],
                                        start=True, stop=True, skip_group_check=(ix > 0))
                    ctx.inc(ins, sfx + "sc")
                    te.wait_ge(ctx.sem(sfx + "pm"), DD["pm"] + i5 + 1)
                    if i5 >= 2:
                        te.wait_ge(ctx.sem(sfx + "pt"), DD["pt"] + i5 - 1)
                    for ix in range(nk):
                        ins = te.transpose(psTb[par][:, ix * 128:(ix + 1) * 128],
                                           PB1[:, par * 640 + ix * 128: par * 640 + (ix + 1) * 128],
                                           idn)
                    ctx.inc(ins, sfx + "pe")
                    te.wait_ge(ctx.sem(sfx + "pt"), DD["pt"] + i5 + 1)
                    if i5 >= 2:
                        te.wait_ge(ctx.sem(sfx + "ov"), DD["ov"] + i5 - 1)
                    pO = psY[h % 2]
                    for ix, kc in enumerate(range(k0, qc + 1)):
                        ins = te.matmul(pO[0:64, 0:128],
                                        VTok[:, kc * TCW + h * 64: kc * TCW + (h + 1) * 64],
                                        PT1[:, par * 640 + ix * 128: par * 640 + (ix + 1) * 128],
                                        start=(ix == 0), stop=(ix == nk - 1))
                    ctx.inc(ins, sfx + "pv")
            return f
        b.tensor(mk_at_pe(l))

        def mk_at_ve(l):
            def f(ve):
                for i5, (h, qc) in enumerate(items5):
                    par = i5 % 2
                    k0 = max(0, qc - 4)
                    nk = qc - k0 + 1
                    nk1 = min(nk, 4)
                    bX = ps[0] if h % 2 == 0 else ps[1]
                    bY = ps[2] if h % 2 == 0 else ps[3]
                    ve.wait_ge(ctx.sem(sfx + "sc"), DD["sc"] + i5 + 1)
                    ixq = qc - k0
                    dq = bX[:, ixq * 128:(ixq + 1) * 128] if ixq < 4 else bY[:, 0:128]
                    ctx.inc(ve.tensor_add(dq, dq, CPF[:, m0c:m0c + 128]), sfx + "rr")
                    if qc >= 4:
                        ctx.inc(ve.tensor_add(bX[:, 0:128], bX[:, 0:128],
                                              CPF[:, m4c:m4c + 128]), sfx + "rr")
                    ve.wait_ge(*ctx.tok(sfx + "rr"))
                    ctx.inc(ve.reduce_max(RMX[:, 0:1], bX[:, 0:nk1 * 128], AX.X, negate=True),
                            sfx + "rr")
                    ve.wait_ge(*ctx.tok(sfx + "rr"))
                    if nk == 5:
                        ctx.inc(ve.reduce_max(RMX[:, 1:2], bY[:, 0:128], AX.X, negate=True),
                                sfx + "rr")
                        ve.wait_ge(*ctx.tok(sfx + "rr"))
                        ctx.inc(ve.tensor_tensor(RMX[:, 0:1], RMX[:, 0:1], RMX[:, 1:2],
                                                 ALU.min), sfx + "rr")
                        ve.wait_ge(*ctx.tok(sfx + "rr"))
                    ctx.inc(ve.tensor_scalar_mul(RMX[:, 2:3], RMX[:, 0:1], 8.0), sfx + "pxm")
                    ve.wait_ge(ctx.sem(sfx + "pxs"), DD["pxs"] + i5 + 1)
                    if nk == 5:
                        ctx.inc(ve.tensor_add(RMX[:, 4:5], RMX[:, 4:5], RMX[:, 5:6]), sfx + "rr")
                        ve.wait_ge(*ctx.tok(sfx + "rr"))
                    ctx.inc(ve.reciprocal(RMX[:, 6:7], RMX[:, 4:5]), sfx + "rr")
                    ve.wait_ge(*ctx.tok(sfx + "rr"))
                    ctx.inc(ve.tensor_scalar_mul(PB1[:, par * 640: par * 640 + nk * 128],
                                                 PB1[:, par * 640: par * 640 + nk * 128],
                                                 RMX[:, 6:7]), sfx + "pm")
                    ctx.inc(ve.tensor_copy(RMX[:, 7:8], RMX[:, 6:7]), sfx + "px")
            return f
        b.vector(mk_at_ve(l))

        def mk_at_se(l):
            def f(se):
                for i5, (h, qc) in enumerate(items5):
                    par = i5 % 2
                    k0 = max(0, qc - 4)
                    nk = qc - k0 + 1
                    nk1 = min(nk, 4)
                    bX = ps[0] if h % 2 == 0 else ps[1]
                    bY = ps[2] if h % 2 == 0 else ps[3]
                    se.wait_ge(ctx.sem(sfx + "pxm"), DD["pxm"] + i5 + 1)
                    se.activation(PB1[:, par * 640: par * 640 + nk1 * 128],
                                  bX[:, 0:nk1 * 128], AF.Exp, bias=RMX[:, 2:3],
                                  scale=8.0, accum_out=RMX[:, 4:5])
                    if nk == 5:
                        ins = se.activation(PB1[:, par * 640 + 512: par * 640 + 640],
                                            bY[:, 0:128], AF.Exp, bias=RMX[:, 2:3],
                                            scale=8.0, accum_out=RMX[:, 5:6])
                    else:
                        ins = se.copy(RMX[0:1, 5:6], RMX[0:1, 5:6])
                    ctx.inc(ins, sfx + "pxs")
                    se.wait_ge(ctx.sem(sfx + "pe"), DD["pe"] + i5 + 1)
                    ctx.inc(se.copy(PT1[:, par * 640: par * 640 + nk * 128],
                                    psTb[par][:, 0:nk * 128]), sfx + "pt")
                    se.wait_ge(ctx.sem(sfx + "pv"), DD["pv"] + i5 + 1)
                    r0 = (h % 2) * 64
                    g4 = h // 2
                    ctx.inc(se.copy(ODT[r0:r0 + 64, g4 * N + qc * 128: g4 * N + (qc + 1) * 128],
                                    psY[h % 2][0:64, 0:128]), sfx + "ov")
            return f
        b.scalar(mk_at_se(l))
        barrier(ctx)
        if TAP == "dod" and l == 0:
            emit_tap(ctx, ODT[:, :], 4 * N)

        def mk_wo(l):
            def f(te):
                te.wait_ge(*wotok)
                for r in range(32):
                    dog, tc = divmod(r, 4)
                    if r >= 3:
                        te.wait_ge(ctx.sem(sfx + "we"), DD["we"] + r - 2)
                    for kg in range(4):
                        ins = te.matmul(ps[r % 3][:, 0:TCW],
                                        WO2[:, kg * D + dog * 128: kg * D + (dog + 1) * 128],
                                        ODT[:, kg * N + tc * TCW: kg * N + (tc + 1) * TCW],
                                        start=(kg == 0), stop=(kg == 3))
                    ctx.inc(ins, sfx + "wo")
            return f
        b.tensor(mk_wo(l))

        def mk_we(l):
            def f(ve):
                for r in range(32):
                    dog, tc = divmod(r, 4)
                    ve.wait_ge(ctx.sem(sfx + "wo"), DD["wo"] + r + 1)
                    sl = np.s_[:, dog * N + tc * TCW: dog * N + (tc + 1) * TCW]
                    ctx.inc(ve.tensor_add(XF[sl], XF[sl], ps[r % 3][:, 0:TCW]), sfx + "we")
            return f
        b.vector(mk_we(l))

        def mk_xb(l):
            def f(se):
                for r in range(32):
                    dog, tc = divmod(r, 4)
                    se.wait_ge(ctx.sem(sfx + "we"), DD["we"] + r + 1)
                    sl = np.s_[:, dog * N + tc * TCW: dog * N + (tc + 1) * TCW]
                    ctx.inc(se.copy(XB[sl], XF[sl]), sfx + "xb")
            return f
        b.scalar(mk_xb(l))
        barrier(ctx)
        es_at.close()

        es_ff = contextlib.ExitStack()
        es.callback(es_ff.close)
        HLN = es_ff.enter_context(nc.sbuf_tensor(f"d_HLN_{l}", [128, NG * N], BF16))
        AAT = es_ff.enter_context(nc.sbuf_tensor(f"d_AAT_{l}", [128, 22 * TCW], BF16))
        GGT = es_ff.enter_context(nc.sbuf_tensor(f"d_GGT_{l}", [128, 22 * TCW], BF16))
        W1S = [es_ff.enter_context(nc.sbuf_tensor(f"d_W1S{i}_{l}", [128, 2 * 1024], BF16))
               for i in range(2)]
        W2S = [es_ff.enter_context(nc.sbuf_tensor(f"d_W2S{i}_{l}", [128, D], BF16))
               for i in range(2)]
        LSB = es_ff.enter_context(nc.sbuf_tensor(f"d_LSB_{l}", [128, TCW], BF16))
        LAB = es_ff.enter_context(nc.sbuf_tensor(f"d_LAB_{l}", [1, 2 * TCW], F32))
        LABL = es_ff.enter_context(nc.sbuf_tensor(f"d_LABL_{l}", [1, 4 * TCW], BF16))
        LREP = es_ff.enter_context(nc.sbuf_tensor(f"d_LREP_{l}", [128, 2 * TCW], F32))

        psW = [es_ff.enter_context(nc.psum_tensor(f"d_psw{i}_{l}", [128, TCW], F32))
               for i in range(2)]
        lnbufs = (LSB, LAB, LABL, LREP, XOR_T)

        def mk_sh2(l):
            def f(ve):
                ins = None
                for g in range(4):
                    ve.memset(HBSm[:, g * TCW: g * TCW + 1], 0.0)
                    ins = ve.tensor_copy(HBSm[:, g * TCW + 1:(g + 1) * TCW],
                                         XB[:, (g + 4) * N:(g + 4) * N + TCW - 1])
                ctx.inc(ins, sfx + "s2")
            return f
        b.vector(mk_sh2(l))
        barrier(ctx)

        lgc0 = ci["d_lg"] + l * 8
        lbc0 = ci["d_lb"] + l * 8
        emit_ln_src(ctx, lnbufs, sfx + "ln", ps,
                    lambda g, tc: hsl(g, tc * TCW, TCW),
                    HLN,
                    lambda g: CPF[:, lgc0 + g: lgc0 + g + 1],
                    lambda g: CPF[:, lbc0 + g: lbc0 + g + 1])
        barrier(ctx)

        b1a0 = ci["d_b1a"] + l * 22
        b1g0 = ci["d_b1g"] + l * 22
        b20 = ci["d_b2"] + l * 8
        wl_toks = {}
        bank8 = ps + psY + psW

        for tc in range(NTC):
            t0 = tc * TCW

            def mk_w1po(l, tc):
                def f(po):
                    for og in range(22):
                        j = tc * 22 + og
                        if j >= 2:
                            po.wait_ge(ctx.sem(sfx + "gg"), DD["gg"] + j - 1)
                        src_a = aps["d_w1a"][l, :, og * 128:(og + 1) * 128].rearrange(
                            "(k p) c -> p k c", p=128)
                        ctx.inc(po.dma_start(
                            out=W1S[og % 2][:, 0:1024].rearrange("p (k c) -> p k c", c=128),
                            in_=src_a), "dmw", 16)
                        src_g = aps["d_w1g"][l, :, og * 128:(og + 1) * 128].rearrange(
                            "(k p) c -> p k c", p=128)
                        ctx.inc(po.dma_start(
                            out=W1S[og % 2][:, 1024:2048].rearrange("p (k c) -> p k c", c=128),
                            in_=src_g), "dmw", 16)
                        wl_toks[(tc, og)] = ctx.tok("dmw")
                return f
            b.gpsimd(mk_w1po(l, tc))

            def mk_ag(l, tc, t0):
                def f(te):
                    for og in range(22):
                        j = tc * 22 + og
                        te.wait_ge(*wl_toks[(tc, og)])
                        if og == 0 and tc >= 1:
                            te.wait_ge(ctx.sem(sfx + "ye"), DD["ye"] + tc * 8)
                        if j >= 2:
                            te.wait_ge(ctx.sem(sfx + "ae"), DD["ae"] + j - 1)
                            te.wait_ge(ctx.sem(sfx + "ge"), DD["ge"] + j - 1)
                        for kg in range(NG):
                            ins = te.matmul(ps[og % 2][:, 0:TCW],
                                            W1S[og % 2][:, kg * 128:(kg + 1) * 128],
                                            HLN[:, kg * N + t0: kg * N + t0 + TCW],
                                            start=(kg == 0), stop=(kg == NG - 1))
                        ctx.inc(ins, sfx + "aa")
                        for kg in range(NG):
                            ins = te.matmul(ps[2 + og % 2][:, 0:TCW],
                                            W1S[og % 2][:, 1024 + kg * 128: 1024 + (kg + 1) * 128],
                                            HLN[:, kg * N + t0: kg * N + t0 + TCW],
                                            start=(kg == 0), stop=(kg == NG - 1))
                        ctx.inc(ins, sfx + "gg")
                return f
            b.tensor(mk_ag(l, tc, t0))

            def mk_age(l, tc):
                def f(se):
                    for og in range(22):
                        j = tc * 22 + og
                        se.wait_ge(ctx.sem(sfx + "gg"), DD["gg"] + j + 1)
                        if og == 0 and tc >= 1:
                            se.wait_ge(ctx.sem(sfx + "yp"), DD["yp"] + tc * 22)
                        bcg = CPF[:, b1g0 + og: b1g0 + og + 1]
                        ctx.inc(se.activation(GGT[:, og * TCW:(og + 1) * TCW],
                                              ps[2 + og % 2][:, 0:TCW], AF.Gelu, bias=bcg),
                                sfx + "ge")
                return f
            b.scalar(mk_age(l, tc))

            def mk_aev(l, tc):
                def f(ve):
                    for og in range(22):
                        j = tc * 22 + og
                        ve.wait_ge(ctx.sem(sfx + "aa"), DD["aa"] + j + 1)
                        if og == 0 and tc >= 1:
                            ve.wait_ge(ctx.sem(sfx + "yp"), DD["yp"] + tc * 22)
                        bca = CPF[:, b1a0 + og: b1a0 + og + 1]
                        ctx.inc(ve.tensor_scalar_add(AAT[:, og * TCW:(og + 1) * TCW],
                                                     ps[og % 2][:, 0:TCW], bca), sfx + "ae")
                return f
            b.vector(mk_aev(l, tc))

            def mk_tt(l, tc):
                def f(po):
                    for og in range(22):
                        j = tc * 22 + og
                        po.wait_ge(ctx.sem(sfx + "ae"), DD["ae"] + j + 1)
                        po.wait_ge(ctx.sem(sfx + "ge"), DD["ge"] + j + 1)
                        ctx.inc(po.tensor_mul(AAT[:, og * TCW:(og + 1) * TCW],
                                              AAT[:, og * TCW:(og + 1) * TCW],
                                              GGT[:, og * TCW:(og + 1) * TCW]), sfx + "tt")
                return f
            b.gpsimd(mk_tt(l, tc))

            def mk_w2po(l, tc):
                def f(po):
                    for kg in range(22):
                        jk = tc * 22 + kg
                        if jk >= 2:
                            po.wait_ge(ctx.sem(sfx + "yp"), DD["yp"] + jk - 1)
                        ctx.inc(po.dma_start(out=W2S[kg % 2][:, 0:D],
                                             in_=aps["d_w2"][l, kg * 128:(kg + 1) * 128, :]),
                                "dmw", 16)
                        wl_toks[("w2", tc, kg)] = ctx.tok("dmw")
                return f
            b.gpsimd(mk_w2po(l, tc))

            def mk_w2(l, tc, t0):
                def f(te):
                    for kg in range(22):
                        te.wait_ge(*wl_toks[("w2", tc, kg)])
                        if kg == 0:
                            te.wait_ge(ctx.sem(sfx + "tt"), DD["tt"] + (tc + 1) * 22)
                            if tc >= 1:
                                te.wait_ge(ctx.sem(sfx + "ye"), DD["ye"] + tc * 8)
                        for dog in range(NG):
                            ins = te.matmul(bank8[dog][:, 0:TCW],
                                            W2S[kg % 2][:, dog * 128:(dog + 1) * 128],
                                            AAT[:, kg * TCW:(kg + 1) * TCW],
                                            start=(kg == 0), stop=(kg == 21))
                        ctx.inc(ins, sfx + "yp")
                return f
            b.tensor(mk_w2(l, tc, t0))

            def mk_ye(l, tc, t0):
                def f(ve):
                    ve.wait_ge(ctx.sem(sfx + "yp"), DD["yp"] + (tc + 1) * 22)
                    for dog in range(NG):
                        sl = np.s_[:, dog * N + t0: dog * N + t0 + TCW]
                        ctx.inc(ve.scalar_tensor_tensor(XF[sl], bank8[dog][:, 0:TCW],
                                                        CPF[:, b20 + dog: b20 + dog + 1],
                                                        XF[sl], ALU.add, ALU.add), sfx + "ye")
                return f
            b.vector(mk_ye(l, tc, t0))
        barrier(ctx)
        es_ff.close()

        def mk_xb2(l):
            def f(se):
                for r in range(32):
                    dog, tc = divmod(r, 4)
                    sl = np.s_[:, dog * N + tc * TCW: dog * N + (tc + 1) * TCW]
                    ctx.inc(se.copy(XB[sl], XF[sl]), sfx + "x2")
            return f
        b.scalar(mk_xb2(l))
        barrier(ctx)
        if TAP == "dl0" and l == 0:
            emit_tap(ctx, XB[:, :], NG * N)

    if upto < 9:
        return
    b_xr = ctx.base("dxr")
    b_xa = ctx.base("dxa")

    @b.gpsimd
    def _(po):
        po.wait_ge(*xotok)
        for i in range(32):
            g, tc = divmod(i, NTC)
            if i >= 1:
                po.wait_ge(ctx.sem("dxa"), b_xa + i)
            ctx.inc(po.dma_start(out=XOR_T[:, 0:TCW],
                                 in_=xorig[:, g * N + tc * TCW: g * N + (tc + 1) * TCW]),
                    "dxr", 16)

    @b.vector
    def _(ve):
        for i in range(32):
            g, tc = divmod(i, NTC)
            ve.wait_ge(ctx.sem("dxr"), b_xr + (i + 1) * 16)
            sl = np.s_[:, g * N + tc * TCW: g * N + (tc + 1) * TCW]
            ctx.inc(ve.tensor_add(XF[sl], XF[sl], XOR_T[:, 0:TCW]), "dxa")
    barrier(ctx)
    emit_ln(ctx, es, "lnd", ps)
    barrier(ctx)



UPTO = 99
TAP = None


def build_program(phases, inp, debug_taps=None):
    nc = bass.Bass("TRN2", target_bir_lowering=False)
    hbm = {}

    xT = nc.dram_tensor("xT", [D, N], F32, kind="ExternalInput").ap()
    xoT = nc.dram_tensor("xoT", [D, N], F32, kind="ExternalOutput").ap()
    dbg = nc.dram_tensor("dbg", [128, NG * N], F32, kind="ExternalOutput").ap()

    cpf, cb = base_packs()
    aps = {}
    for ph in phases:
        if ph == "c":
            aps.update(decl_c(nc))
            pack_c(cpf, cb, inp)
            hbm.update(host_prep_c(inp))
        elif ph == "b":
            aps.update(decl_b(nc))
            pack_b(cpf, cb, inp)
            hbm.update(host_prep_b(inp))
        elif ph == "a":
            aps.update(decl_a(nc))
            pack_a(cpf, cb, inp)
            hbm.update(host_prep_a(inp))
        elif ph == "d":
            aps.update(decl_d(nc))
            pack_d(cpf, cb, inp)
            hbm.update(host_prep_d(inp))
    CPF_arr = cpf.array()
    CB_arr = cb.array(bf16=True)
    cpf_t = nc.dram_tensor("cpf", list(CPF_arr.shape), F32, kind="ExternalInput").ap()
    cb_t = nc.dram_tensor("cb", list(CB_arr.shape), BF16, kind="ExternalInput").ap()
    hbm["cpf"] = CPF_arr
    hbm["cb"] = CB_arr

    es = contextlib.ExitStack()
    with es:
        XF = es.enter_context(nc.sbuf_tensor("XF", [128, NG * N], F32))
        XB = es.enter_context(nc.sbuf_tensor("XB", [128, NG * N], BF16))
        CPF = es.enter_context(nc.sbuf_tensor("CPF", [128, CPF_arr.shape[1]], F32))
        CB = es.enter_context(nc.sbuf_tensor("CB", [128, CB_arr.shape[1]], BF16))
        JUNK = es.enter_context(nc.sbuf_tensor("junk", [1, 8], F32))
        with nc.Block() as block:
            ctx = Ctx(nc, block, es)
            ctx.t_.update(XF=XF, XB=XB, CPF=CPF, CB=CB, junk=JUNK,
                          cpf_idx=cpf.idx, cb_idx=cb.idx, dbg=dbg)

            @block.sync
            def _(sy):
                for g in range(NG):
                    ctx.inc(sy.dma_start(out=XF[:, g * N:(g + 1) * N],
                                         in_=xT[g * 128:(g + 1) * 128, :]), "dmx", 16)
                ctx.inc(sy.dma_start(out=CPF[:, :], in_=cpf_t[:, :]), "dmx", 16)
                ctx.inc(sy.dma_start(out=CB[:, :], in_=cb_t[:, :]), "dmx", 16)

            @block.gpsimd
            def _(po):
                for g in range(NG):
                    ctx.inc(po.dma_start(out=XB[:, g * N:(g + 1) * N],
                                         in_=xT[g * 128:(g + 1) * 128, :]), "dmx", 16)
                po.memset(JUNK[:, :], 0.0)
            xtok = ctx.tok("dmx")
            for eng in ENG:
                def mk(e):
                    def f(x):
                        x.wait_ge(*xtok)
                    return f
                getattr(block, eng)(mk(eng))

            for ph in phases:
                ctx.phase_reset()
                with contextlib.ExitStack() as es_ph:
                    ctx.es_phase = es_ph
                    if ph == "c":
                        emit_phase_c(ctx, aps, upto=UPTO)
                    elif ph == "b":
                        emit_phase_b(ctx, aps, upto=UPTO)
                    elif ph == "a":
                        emit_phase_a(ctx, aps, upto=UPTO)
                    elif ph == "d":
                        emit_phase_d(ctx, aps, upto=UPTO)
                    elif ph == "ln":
                        ps4 = [es_ph.enter_context(nc.psum_tensor(f"l_ps{i}", [128, TCW], F32))
                               for i in range(4)]
                        barrier(ctx)
                        emit_ln(ctx, es_ph, "lnt", ps4)
                        barrier(ctx)

            @block.sync
            def _(sy):
                for g in range(NG):
                    ctx.inc(sy.dma_start(out=xoT[g * 128:(g + 1) * 128, :],
                                         in_=XF[:, g * N:(g + 1) * N]), "dmo", 16)
                sy.wait_ge(*ctx.tok("dmo"))
    return nc, hbm


def run_phase_test(phase, x_in, x_exp, inp):
    nc, hbm = build_program([phase], inp)
    io = dict(hbm)
    io["xT"] = np.ascontiguousarray(x_in.T, dtype=np.float32)
    res = run_bass_kernel_spmd(nc, [io], core_ids=[0])
    if TAP:
        np.save("/tmp/tap.npy", res.results[0]["dbg"])
    got = res.results[0]["xoT"].T
    np.save("/tmp/xo.npy", got)
    rv = ((got - x_exp) ** 2).mean() / ((x_exp ** 2).mean() + 1e-30)
    print(f"phase {phase}: resid-var ratio = {rv:.3e}  maxabs={np.abs(got - x_exp).max():.3e}")
    return got, rv




def kernel(**inputs):
    x = np.asarray(inputs["L0_attn_x"], np.float32)
    ncores = 8
    nc, hbm = build_program(["a", "b", "c", "d"], inputs)
    in_maps = []
    for c in range(ncores):
        io = dict(hbm)
        io["xT"] = np.ascontiguousarray(x[c].T)
        in_maps.append(io)
    res = run_bass_kernel_spmd(nc, in_maps, core_ids=list(range(ncores)))
    return np.stack([np.ascontiguousarray(res.results[c]["xoT"]).T
                     for c in range(ncores)], axis=0).astype(np.float32)
